# version 16
# baseline (speedup 1.0000x reference)
import numpy as np
import contextlib
import concourse.bass as bass
import concourse.mybir as mybir
from concourse.bass_utils import run_bass_kernel_spmd
from concourse import library_config

F32 = mybir.dt.float32
BF16 = mybir.dt.bfloat16
U16 = mybir.dt.uint16
I16 = mybir.dt.int16
AL = mybir.AluOpType
AF = mybir.ActivationFunctionType
AX = mybir.AxisListType

BIGC = float(2 ** 23)
NEG_BIG = -1e30
ENGINES = ("sync", "scalar", "vector", "tensor", "gpsimd")


class Prog:

    def __init__(self, nc):
        self.nc = nc
        self.steps = {e: [] for e in ENGINES}
        self.sems = {}
        self.counts = {}
        self.last_dma = {}
        self._es = contextlib.ExitStack()

    def sem(self, name):
        if name not in self.sems:
            self.sems[name] = self._es.enter_context(self.nc.semaphore(name))
            self.counts[name] = 0
        return self.sems[name]

    NSUB = 6

    def step(self, engine, emit, waits=(), inc=None, inc_by=1):
        tok = None
        waits = [w for w in waits if w is not None]
        if inc is not None:
            if inc_by == 16:
                jname = inc + "#j"
                j = self.counts.get(jname, 0)
                self.counts[jname] = j + 1
                sub = "%s%d" % (inc, j % self.NSUB)
                self.sem(sub)
                self.counts[sub] += 16
                if self.counts[sub] > 16:
                    waits.append((sub, self.counts[sub] - 16))
                tok = (sub, self.counts[sub], 16)
                self.last_dma.setdefault(inc, {})[sub] = tok
            else:
                self.sem(inc)
                self.counts[inc] += inc_by
                tok = (inc, self.counts[inc], inc_by)
        self.steps[engine].append((tuple(waits), emit, tok))
        return tok

    def tails(self, *queues):
        out = []
        for q in queues:
            out.extend(self.last_dma.get(q, {}).values())
        return out

    def emit(self, block):
        for engine in ENGINES:
            steps = self.steps[engine]
            if not steps:
                continue

            def body(eng, engine_steps=steps):
                waited = {}
                for waits, emit_fn, tok in engine_steps:
                    for w in waits:
                        sname, val = w[0], w[1]
                        if waited.get(sname, -1) < val:
                            eng.wait_ge(self.sems[sname], val)
                            waited[sname] = val
                    last = emit_fn(eng)
                    if tok is not None:
                        if last is None:
                            last = eng.nop()
                        last.then_inc(self.sems[tok[0]], tok[2])

            getattr(block, engine)(body)


def build_program(S=2048, F=5632):
    D, H = 2048, 128
    NT = S // 128
    DT = D // 128
    FT = F // 128
    NCH = S // 512
    G = H // 2
    GB = 8
    NBLK = G // GB
    NST = 4 if FT % 4 == 0 else (2 if FT % 2 == 0 else 1)
    SFT = FT // NST
    n_w_elems = float(F * D)

    nc = bass.Bass("TRN2", target_bir_lowering=False)
    p = Prog(nc)

    NVE_ = D // 512
    NST_ = 4 if FT % 4 == 0 else (2 if FT % 2 == 0 else 1)
    SFT_ = FT // NST_
    x_tok = nc.dram_tensor("x_tok", [S, D], F32, kind="ExternalInput").ap()
    xT_d = nc.dram_tensor("xT", [D, S], F32, kind="ExternalInput").ap()
    wqs_d = nc.dram_tensor("wqs", [128, DT, H], F32, kind="ExternalInput").ap()
    wks_d = nc.dram_tensor("wks", [128, DT, H], F32, kind="ExternalInput").ap()
    wvs_d = nc.dram_tensor("wvs", [NVE_, 128, DT, 512], F32,
                           kind="ExternalInput").ap()
    wos_d = nc.dram_tensor("wos", [DT, 128, DT, 128], F32,
                           kind="ExternalInput").ap()
    w1s_d = nc.dram_tensor("w1s", [FT, 128, DT, 128], F32,
                           kind="ExternalInput").ap()
    w2s_d = nc.dram_tensor("w2s", [FT, 128, DT, 128], F32,
                           kind="ExternalInput").ap()
    w3s_d = nc.dram_tensor("w3s", [NST_, DT, 128, SFT_, 128], F32,
                           kind="ExternalInput").ap()
    out_d = nc.dram_tensor("out", [D, S], F32, kind="ExternalOutput").ap()
    at_scr = nc.dram_tensor("at_scr", [D, S], BF16, kind="Internal").ap()
    o3_scr = nc.dram_tensor("o3_scr", [D, S], BF16, kind="Internal").ap()
    v_scr = nc.dram_tensor("v_scr", [S, D], BF16, kind="Internal").ap()

    es = contextlib.ExitStack()

    def sbuf(name, shape, dtype, stack=None, side=None):
        kw = {} if side is None else {"side": side}
        return (stack or es).enter_context(
            nc.sbuf_tensor(name, shape, dtype, **kw))

    def psum(name, shape, dtype=F32, stack=None):
        return (stack or es).enter_context(
            nc.psum_tensor(name, shape, dtype))

    xnT_sc = contextlib.ExitStack()
    xnT = sbuf("xnT", [128, DT, S], BF16, xnT_sc, side="right")
    ident = sbuf("ident", [128, 128], BF16)
    iota_r = sbuf("iota_r", [128, 128], F32)
    iota_g = sbuf("iota_g", [128, G], F32)
    iota_rp = sbuf("iota_rp", [128, 128], F32)
    hcol = sbuf("hcol", [128, 1], F32)
    hcol_i = sbuf("hcol_i", [128, 1], mybir.dt.int32)
    hbit_i = sbuf("hbit_i", [128, 1], mybir.dt.int32)
    hm2 = sbuf("hm2", [128, 1], F32)
    h2col = sbuf("h2col", [128, 1], F32)
    rlt64 = sbuf("rlt64", [128, 1], F32)
    nt0 = sbuf("nt0", [128, 1], F32)
    biascol = sbuf("biascol", [128, 1], F32)
    Gm = sbuf("Gm", [128, G], BF16)
    Pm = sbuf("Pm", [128, 128], BF16)
    csr = sbuf("csr", [128, 128], F32)
    KV2 = sbuf("KV2", [128, 128], BF16)
    CP2 = sbuf("CP2", [128, 128], BF16)
    gam_cols = sbuf("gam_cols", [128, 192], F32)
    gam_acc = sbuf("gam_acc", [128, 3], F32)
    gam_sc = sbuf("gam_sc", [1, 3], F32)
    thr_sc = sbuf("thr_sc", [1, 3], F32)
    ones_row = sbuf("ones_row", [1, 128], F32)
    thr = sbuf("thr", [128, 3], F32)
    thrn = sbuf("thrn", [128, 3], F32)
    qb = sbuf("qb", [128, S], BF16)
    gq = sbuf("gq", [128, S], BF16)
    kvals = sbuf("kvals", [128, 64], F32)
    Vsel = sbuf("Vsel", [128, G, 34], BF16)
    idxb0 = sbuf("idxb0", [64, G], mybir.dt.uint32)
    idxb1 = sbuf("idxb1", [64, G], mybir.dt.uint32)

    gsc = contextlib.ExitStack()
    GCW = 4096
    gslab = sbuf("gslab", [128, 2, GCW], F32, gsc)
    gdum = sbuf("gdum", [128, GCW], BF16, gsc)

    gchunks = []
    nper = []
    for wd in (w1s_d, w2s_d, w3s_d):
        flat = wd.rearrange("a b c d -> (a b c d)") if len(wd.shape) == 4 \
            else wd.rearrange("a b c d e -> (a b c d e)")
        nch_ = flat.shape[0] // (128 * GCW)
        assert nch_ * 128 * GCW == flat.shape[0]
        chv = flat.rearrange("(n p q) -> n p q", p=128, q=GCW)
        for i_ in range(nch_):
            gchunks.append((chv[i_], GCW))
        nper.append(nch_)
    NGC = len(gchunks)
    assert NGC <= 192
    r1 = nper[0]
    r2 = nper[0] + nper[1]

    gam_toks = [None] * NGC
    gslab_toks = [None] * NGC

    def reg_gamma(i):
        ap, nel = gchunks[i]

        def em_dma(sy, ap=ap, nel=nel, i=i):
            return sy.dma_start(out=gslab[:, i % 2, :nel], in_=ap)
        w = [gam_toks[i - 2]] if i >= 2 else []
        gslab_toks[i] = p.step("sync", em_dma, waits=w, inc="dY", inc_by=16)

        def em_abs(sc, i=i, nel=nel):
            sc.activation(gdum[:, :nel], gslab[:, i % 2, :nel],
                          AF.Abs, accum_out=gam_cols[:, i:i + 1])
            sc.drain()
            return sc.nop()
        gam_toks[i] = p.step("scalar", em_abs, waits=[gslab_toks[i]],
                             inc="sGa")

    def em_iotas(gp):
        gp.iota(iota_r[:], pattern=[[1, 128]], base=0, channel_multiplier=0,
                allow_small_or_imprecise_dtypes=True)
        gp.iota(iota_g[:], pattern=[[1, G]], base=0, channel_multiplier=0,
                allow_small_or_imprecise_dtypes=True)
        gp.iota(iota_rp[:], pattern=[[1, 2], [0, 64]], base=0,
                channel_multiplier=0, allow_small_or_imprecise_dtypes=True)
        gp.iota(hcol[:], pattern=[[0, 1]], base=0, channel_multiplier=1,
                allow_small_or_imprecise_dtypes=True)
        gp.iota(hcol_i[:], pattern=[[0, 1]], base=0, channel_multiplier=1)
        gp.iota(csr[:], pattern=[[0, 2], [1, 2], [0, 32]], base=0,
                channel_multiplier=0, allow_small_or_imprecise_dtypes=True)
        gp.memset(gam_cols[:], 0.0)
        gp.memset(ones_row[:], 1.0)
        gp.drain()
        return gp.nop()
    tok_iota = p.step("gpsimd", em_iotas, inc="gp")

    def em_ident(v):
        v.tensor_scalar(ident[:], iota_r[:], hcol[:], None, op0=AL.is_equal)
        v.drain()
        return v.nop()
    tok_ident = p.step("vector", em_ident, waits=[tok_iota], inc="v")

    def em_masks(v):
        v.tensor_scalar(hbit_i[:], hcol_i[:], 1, None, op0=AL.bitwise_and)
        v.drain()
        v.tensor_copy(hm2[:], hbit_i[:])
        v.drain()
        v.tensor_tensor(h2col[:], hcol[:], hm2[:], op=AL.subtract)
        v.drain()
        v.tensor_scalar(h2col[:], h2col[:], 0.5, None, op0=AL.mult)
        v.drain()
        v.tensor_scalar(Gm[:], iota_g[:], h2col[:], None, op0=AL.is_equal)
        v.drain()
        v.tensor_scalar(Pm[:], iota_rp[:], hm2[:], None, op0=AL.is_equal)
        v.drain()
        v.tensor_scalar(csr[:], csr[:], -2.0 * BIGC, BIGC,
                        op0=AL.mult, op1=AL.add)
        v.drain()
        v.tensor_tensor(CP2[:], csr[:], Pm[:], op=AL.mult)
        v.drain()
        v.tensor_scalar(rlt64[:], hcol[:], 64.0, None, op0=AL.is_lt)
        v.drain()
        v.tensor_scalar(hbit_i[:], hcol_i[:], 32, None, op0=AL.bitwise_and)
        v.drain()
        v.tensor_copy(nt0[:], hbit_i[:])
        v.drain()
        v.tensor_scalar(nt0[:], nt0[:], 0.0, None, op0=AL.is_equal)
        v.drain()
        v.tensor_scalar(biascol[:], nt0[:], -BIGC, None, op0=AL.mult)
        v.drain()
        v.memset(Vsel[:], 0.0)
        v.drain()
        v.tensor_scalar(Vsel[:, :, 32:33],
                        rlt64[:, None, :].broadcast_to((128, G, 1)),
                        1.0, None, op0=AL.mult)
        v.drain()
        v.tensor_scalar(Vsel[:, :, 33:34],
                        rlt64[:, None, :].broadcast_to((128, G, 1)),
                        -1.0, 1.0, op0=AL.mult, op1=AL.add)
        v.drain()
        return v.nop()
    tok_masks = p.step("vector", em_masks, inc="v")

    p1 = contextlib.ExitStack()
    XR = 3
    x_tile = sbuf("x_tile", [128, XR, D], F32, p1)
    xn_stage = sbuf("xn_stage", [128, XR, D], BF16, p1)
    stats = sbuf("stats", [128, XR, 24], F32, p1)
    mv = sbuf("mv", [128, XR, 2], F32, p1)
    veps = sbuf("veps", [128, XR, 1], F32, p1)
    rs0 = sbuf("rs0", [128, XR, 1], F32, p1)
    rs1 = sbuf("rs1", [128, XR, 1], F32, p1)
    nt1 = sbuf("nt1", [128, XR, 1], F32, p1)
    pxn = [psum("pxn%d" % i, [128, DT, 128], BF16, p1) for i in range(2)]

    gamma_ptr = [0]
    xn_toks = [None] * NT
    xload_toks = [None] * NT
    xpose_toks = [[None] * DT for _ in range(NT)]
    xev_toks = [[None] * DT for _ in range(NT)]
    sqrt_toks = [None] * NT

    for st in range(NT):
        b = st % XR

        def em_xload(gp, st=st, b=b):
            return gp.dma_start(out=x_tile[:, b, :],
                                in_=x_tok[128 * st:128 * (st + 1), :])
        w = [xn_toks[st - XR]] if st >= XR else []
        xload_toks[st] = p.step("gpsimd", em_xload, waits=w, inc="dG",
                                inc_by=16)

        def em_stats(v, b=b):
            for c in range(4):
                v.bn_stats(stats[:, b, 6 * c:6 * c + 6],
                           x_tile[:, b, 512 * c:512 * (c + 1)])
            v.drain()
            v.bn_aggr(mv[:, b, :], stats[:, b, :])
            v.drain()
            v.tensor_scalar(veps[:, b, :], mv[:, b, 1:2], 1e-5, None,
                            op0=AL.add)
            v.drain()
            return v.nop()
        t_stats = p.step("vector", em_stats, waits=[xload_toks[st]], inc="v")

        def em_sqrt(sc, b=b):
            return sc.activation(rs0[:, b, :], veps[:, b, :], AF.Sqrt)
        sqrt_toks[st] = p.step("scalar", em_sqrt, waits=[t_stats], inc="sc")

        def em_xn(v, b=b):
            v.reciprocal(rs1[:, b, :], rs0[:, b, :])
            v.drain()
            v.tensor_tensor(nt1[:, b, :], rs1[:, b, :], rs1[:, b, :],
                            op=AL.mult)
            v.drain()
            v.tensor_tensor(nt1[:, b, :], nt1[:, b, :], veps[:, b, :],
                            op=AL.mult)
            v.drain()
            v.tensor_scalar(nt1[:, b, :], nt1[:, b, :], -0.5, 1.5,
                            op0=AL.mult, op1=AL.add)
            v.drain()
            v.tensor_tensor(rs1[:, b, :], rs1[:, b, :], nt1[:, b, :],
                            op=AL.mult)
            v.drain()
            v.tensor_scalar(xn_stage[:, b, :], x_tile[:, b, :],
                            mv[:, b, 0:1], rs1[:, b, :],
                            op0=AL.subtract, op1=AL.mult)
            v.drain()
            return v.nop()
        xn_toks[st] = p.step("vector", em_xn, waits=[sqrt_toks[st]], inc="v")

        def em_tp(t, st=st, b=b):
            last = None
            for dt in range(DT):
                last = t.transpose(pxn[st % 2][:, dt, :],
                                   xn_stage[:, b, 128 * dt:128 * (dt + 1)],
                                   ident[:])
            return last
        w = [xn_toks[st], tok_ident]
        if st >= 2:
            w.append(xev_toks[st - 2][0])
        xpose_toks[st][0] = p.step("tensor", em_tp, waits=w, inc="t")

        def em_ev(v, st=st):
            v.tensor_copy(xnT[:, :, 128 * st:128 * (st + 1)], pxn[st % 2][:])
            v.drain()
            return v.nop()
        xev_toks[st][0] = p.step("vector", em_ev, waits=[xpose_toks[st][0]],
                                 inc="v")
        while gamma_ptr[0] < min(NGC, 2 * (st + 1)):
            reg_gamma(gamma_ptr[0])
            gamma_ptr[0] += 1

    tok_xnT = xev_toks[NT - 1][0]
    p1_end = [tok_xnT, xpose_toks[NT - 1][0], sqrt_toks[NT - 1],
              xn_toks[NT - 1]]


    p1.close()
    p2a = contextlib.ExitStack()
    U32 = mybir.dt.uint32
    wvs2 = sbuf("wvs2", [128, 2, DT, 512], BF16, p2a)
    vtile = sbuf("vtile", [128, 4, 512], BF16, p2a)
    keysw = sbuf("keysw", [128, 3, S], F32, p2a)
    wqTb = sbuf("wqTb", [128, DT, H], BF16, p2a)
    wkTb = sbuf("wkTb", [128, DT, H], BF16, p2a)
    idx = sbuf("idx", [128, 64], U16, p2a)
    idxcf = sbuf("idxcf", [128, 64], F32, p2a)

    kvn = sbuf("kvn", [128, 32], F32, p2a)
    identF = sbuf("identF", [128, 128], F32, p2a)
    pqk = [psum("pqk%d" % i, [128, 512], F32, p2a) for i in range(4)]
    pidx = psum("pidx", [64, 128], F32, p2a)

    vflat = v_scr.rearrange("s (hh c) -> (s hh) c", c=16)

    t_idf = p.step("vector",
                   lambda v: v.tensor_scalar(identF[:], iota_r[:], hcol[:],
                                             None, op0=AL.is_equal),
                   waits=[tok_iota], inc="v")

    t_cq = p.step("gpsimd",
                  lambda gp: gp.dma_start(out=wqTb[:], in_=wqs_d[:]),
                  waits=p1_end, inc="dG", inc_by=16)
    t_ck = p.step("gpsimd",
                  lambda gp: gp.dma_start(out=wkTb[:], in_=wks_d[:]),
                  waits=p1_end, inc="dG", inc_by=16)

    qk_mm = [None] * (2 * NCH)
    qk_ev = [None] * (2 * NCH)
    for wi in range(2):
        for ch in range(NCH):
            u = wi * NCH + ch

            def em_mm(t, wi=wi, ch=ch, u=u):
                wt = wqTb if wi == 0 else wkTb
                last = None
                for kt in range(DT):
                    last = t.matmul(pqk[u % 4][:], wt[:, kt, :],
                                    xnT[:, kt, 512 * ch:512 * (ch + 1)],
                                    start=(kt == 0), stop=(kt == DT - 1))
                return last
            w = [t_cq if wi == 0 else t_ck, tok_xnT]
            if u >= 4:
                w.append(qk_ev[u - 4])
            qk_mm[u] = p.step("tensor", em_mm, waits=w, inc="t")

            if wi == 0:
                def em_ev(v, ch=ch, u=u):
                    v.tensor_copy(qb[:, 512 * ch:512 * (ch + 1)],
                                  pqk[u % 4][:])
                    v.drain()
                    v.tensor_scalar(gq[:, 512 * ch:512 * (ch + 1)],
                                    pqk[u % 4][:], 0.0, None,
                                    op0=AL.is_gt)
                    v.drain()
                    return v.nop()
            else:
                def em_ev(v, ch=ch, u=u):
                    v.tensor_copy(keysw[:, 0, 512 * ch:512 * (ch + 1)],
                                  pqk[u % 4][:])
                    v.drain()
                    return v.nop()
            qk_ev[u] = p.step("vector", em_ev, waits=[qk_mm[u]], inc="v")

    NQK = 2 * NCH
    NVE = D // 512
    vload_toks = [None] * NVE
    vcast_toks = [None] * NVE
    vmm_toks = [[None] * NT for _ in range(NVE)]
    vev_toks = [[None] * NT for _ in range(NVE)]
    vsp_toks = [[None] * NT for _ in range(NVE)]

    for ech in range(NVE):
        def em_vload(gp, ech=ech):
            return gp.dma_start(out=wvs2[:, ech % 2, :, :], in_=wvs_d[ech])
        w = []
        if ech >= 2:
            w.append(vmm_toks[ech - 2][NT - 1])
        vcast_toks[ech] = vload_toks[ech] = p.step(
            "gpsimd", em_vload, waits=w, inc="dG", inc_by=16)

        for mt in range(NT):
            uu = NQK + ech * NT + mt

            def em_vmm(t, ech=ech, mt=mt, uu=uu):
                last = None
                for kt in range(DT):
                    last = t.matmul(pqk[uu % 4][:],
                                    xnT[:, kt, 128 * mt:128 * (mt + 1)],
                                    wvs2[:, ech % 2, kt, :],
                                    start=(kt == 0), stop=(kt == DT - 1))
                return last
            w = [vcast_toks[ech]]
            pu = uu - 4
            if pu >= NQK:
                pe, pm = divmod(pu - NQK, NT)
                w.append(vev_toks[pe][pm])
            else:
                w.append(qk_ev[pu])
            vmm_toks[ech][mt] = p.step("tensor", em_vmm, waits=w, inc="t")

            def em_vev(v, ech=ech, mt=mt, uu=uu):
                v.tensor_copy(vtile[:, uu % 4, :], pqk[uu % 4][:])
                v.drain()
                return v.nop()
            w = [vmm_toks[ech][mt]]
            if uu - NQK >= 4:
                pe, pm = divmod(uu - NQK - 4, NT)
                w.append(vsp_toks[pe][pm])
            vev_toks[ech][mt] = p.step("vector", em_vev, waits=w, inc="v")

            def em_vsp(gp, ech=ech, mt=mt, uu=uu):
                return gp.dma_start(
                    out=v_scr[128 * mt:128 * (mt + 1),
                              512 * ech:512 * (ech + 1)],
                    in_=vtile[:, uu % 4, :])
            vsp_toks[ech][mt] = p.step("gpsimd", em_vsp,
                                       waits=[vev_toks[ech][mt]],
                                       inc="dG", inc_by=16)
            while gamma_ptr[0] < min(NGC, 2 * (ech * NT + mt)):
                reg_gamma(gamma_ptr[0])
                gamma_ptr[0] += 1

    def em_sel(v):
        seq = [0, 1, 2, 1]
        for r in range(4):
            src = keysw[:, seq[r], :]
            v.max(kvals[:, 8 * r:8 * r + 8], src)
            v.drain()
            v.max_index(idx[:, 8 * r:8 * r + 8], kvals[:, 8 * r:8 * r + 8],
                        src)
            if r < 3:
                v.match_replace(keysw[:, seq[r + 1], :],
                                kvals[:, 8 * r:8 * r + 8], src, NEG_BIG)
            v.drain()
        v.tensor_scalar(keysw[:, 1, :], keysw[:, 0, :], -1.0, None,
                        op0=AL.mult)
        v.drain()
        seq = [1, 2, 1, 2]
        for r in range(4):
            src = keysw[:, seq[r], :]
            v.max(kvn[:, 8 * r:8 * r + 8], src)
            v.drain()
            v.max_index(idx[:, 32 + 8 * r:32 + 8 * r + 8],
                        kvn[:, 8 * r:8 * r + 8], src)
            if r < 3:
                v.match_replace(keysw[:, seq[r + 1], :],
                                kvn[:, 8 * r:8 * r + 8], src, NEG_BIG)
            v.drain()
        v.tensor_scalar(kvals[:, 32:64], kvn[:], -1.0, None, op0=AL.mult)
        v.drain()
        return v.nop()
    tok_sel = p.step("vector", em_sel, waits=[qk_ev[2 * NCH - 1]], inc="v")

    def em_kv2(v):
        v.tensor_tensor(KV2.ap().rearrange("p (two j) -> p two j", j=64),
                        kvals[:, None, :].broadcast_to((128, 2, 64)),
                        Pm.ap().rearrange("p (two j) -> p two j", j=64),
                        op=AL.mult)
        v.drain()
        return v.nop()
    tok_kv2 = p.step("vector", em_kv2, waits=[tok_sel, tok_masks], inc="v")

    def em_idxc(v):
        v.tensor_copy(idxcf[:], idx[:])
        v.drain()
        v.tensor_scalar(idxcf[:], idxcf[:], 128.0, hcol[:],
                        op0=AL.mult, op1=AL.add)
        v.drain()
        return v.nop()
    t_ic = p.step("vector", em_idxc, waits=[tok_sel], inc="v")

    t_it = p.step("tensor",
                  lambda t: t.transpose(pidx[:], idxcf[:], identF[:]),
                  waits=[t_ic, t_idf], inc="t")

    def em_ib(v):
        v.tensor_copy(idxb0[:], pidx[:, 0::2])
        v.drain()
        v.tensor_copy(idxb1[:], pidx[:, 1::2])
        v.drain()
        return v.nop()
    t_ib = p.step("vector", em_ib, waits=[t_it], inc="v")

    sp_all = [vsp_toks[e][m] for e in range(NVE) for m in range(NT)]
    gv_toks = [None] * (2 * G)
    for g in range(G):
        def em_g0(gp, g=g):
            return gp.indirect_dma_start(
                out=Vsel[0:64, g, 0:16], out_offset=None, in_=vflat,
                in_offset=bass.IndirectOffsetOnAxis(ap=idxb0[:, g:g + 1],
                                                    axis=0))
        w = [t_ib, tok_masks] + (sp_all if g == 0 else [])
        gv_toks[2 * g] = p.step("gpsimd", em_g0, waits=w, inc="dG",
                                inc_by=16)

        def em_g1(gp, g=g):
            return gp.indirect_dma_start(
                out=Vsel[64:128, g, 16:32], out_offset=None, in_=vflat,
                in_offset=bass.IndirectOffsetOnAxis(ap=idxb1[:, g:g + 1],
                                                    axis=0))
        gv_toks[2 * g + 1] = p.step("gpsimd", em_g1, waits=[gv_toks[2 * g]],
                                    inc="dG", inc_by=16)
    tok_vsel = gv_toks[2 * G - 1]

    while gamma_ptr[0] < NGC:
        reg_gamma(gamma_ptr[0])
        gamma_ptr[0] += 1

    p2a_end = [tok_kv2, tok_sel, t_ck, t_it, t_ib,
               vmm_toks[NVE - 1][NT - 1], vev_toks[NVE - 1][NT - 1],
               qk_ev[2 * NCH - 1], gam_toks[NGC - 1]] \
        + p.tails("dX", "dY")

    p2a.close()
    gsc.close()
    p2b = contextlib.ExitStack()
    Esl = sbuf("Esl", [128, 2, GB, S], BF16, p2b)
    Lblk = sbuf("Lblk", [128, 2, GB, 128], BF16, p2b)
    Cblk = sbuf("Cblk", [128, 2, GB, 128], BF16, p2b)
    a_sE = sbuf("a_sE", [128, NT, 32 * GB], BF16, p2b)
    aT_slab = sbuf("aT_slab", [128, 2, NT, 128], BF16, p2b)
    rden = sbuf("rden", [128, GB, 2], F32, p2b)
    SCW = min(1024, S)
    NHF = S // SCW
    psc = [psum("psc%d" % i, [128, SCW], F32, p2b) for i in range(2)]
    pav = [psum("pav%d" % i, [128, GB, 34], F32, p2b) for i in range(2)]
    pat = [psum("pat%d" % i, [128, 128], BF16, p2b) for i in range(2)]

    lb_toks = [None] * NBLK
    sc_toks = [[None] * (NHF * GB) for _ in range(NBLK)]
    ex_toks = [[None] * (NHF * GB) for _ in range(NBLK)]
    av_toks = [[None] * NT for _ in range(NBLK)]
    dv_toks = [[None] * NT for _ in range(NBLK)]
    ta_toks = [[None] * (2 * NT) for _ in range(NBLK)]
    ae_toks = [[None] * (2 * NT) for _ in range(NBLK)]
    sp_toks = [None] * NBLK

    at_view = at_scr.rearrange("(bb eb p) (st j) -> bb p eb st j",
                               bb=NBLK, p=128, j=128)

    def reg_lb(bkt):
        def em_lb(v, b=bkt):
            gsl = slice(GB * b, GB * (b + 1))
            v.tensor_tensor(Lblk[:, b % 2, :, :],
                            KV2[:, None, :].broadcast_to((128, GB, 128)),
                            Gm[:, gsl, None].broadcast_to((128, GB, 128)),
                            op=AL.mult)
            v.drain()
            v.tensor_tensor(Cblk[:, b % 2, :, :],
                            CP2[:, None, :].broadcast_to((128, GB, 128)),
                            Gm[:, gsl, None].broadcast_to((128, GB, 128)),
                            op=AL.mult)
            v.drain()
            return v.nop()
        w = [tok_kv2] + (p2a_end if bkt == 0 else [])
        if bkt >= 2:
            w.append(sc_toks[bkt - 2][NHF * GB - 1])
        lb_toks[bkt] = p.step("vector", em_lb, waits=w, inc="v")

    reg_lb(0)
    for bkt in range(NBLK):
        if bkt + 1 < NBLK:
            reg_lb(bkt + 1)
        for gi in range(GB):
            for hf in range(NHF):
                u = (bkt * GB + gi) * NHF + hf
                uu = u % 2

                def em_sc_mm(t, b=bkt, gi=gi, hf=hf, uu=uu):
                    last = None
                    for cc in range(SCW // 512):
                        n0 = SCW * hf + 512 * cc
                        t.matmul(psc[uu][:, 512 * cc:512 * (cc + 1)],
                                 Lblk[:, b % 2, gi, :], qb[:, n0:n0 + 512],
                                 start=True, stop=False)
                        last = t.matmul(psc[uu][:, 512 * cc:512 * (cc + 1)],
                                        Cblk[:, b % 2, gi, :],
                                        gq[:, n0:n0 + 512],
                                        start=False, stop=True)
                    return last
                w = [lb_toks[bkt]]
                if u >= 2:
                    w.append(ex_toks[(u - 2) // (NHF * GB)]
                             [(u - 2) % (NHF * GB)])
                if gi == 0 and hf == 0 and bkt >= 2:
                    w.append(av_toks[bkt - 2][NT - 1])
                sc_toks[bkt][NHF * gi + hf] = p.step("tensor", em_sc_mm,
                                                     waits=w, inc="t")

                def em_exp(sc, b=bkt, gi=gi, hf=hf, uu=uu):
                    return sc.activation(
                        Esl[:, b % 2, gi, SCW * hf:SCW * (hf + 1)],
                        psc[uu][:], AF.Exp, bias=biascol[:], scale=1.0)
                ex_toks[bkt][NHF * gi + hf] = p.step(
                    "scalar", em_exp, waits=[sc_toks[bkt][NHF * gi + hf]],
                    inc="sEx")

        for st in range(NT):
            def em_av(t, b=bkt, st=st):
                last = None
                for gi in range(GB):
                    last = t.matmul(pav[st % 2][:, gi, :],
                                    Esl[:, b % 2, gi,
                                        128 * st:128 * (st + 1)],
                                    Vsel[:, b * GB + gi, :],
                                    start=True, stop=True)
                return last
            hi = 2 * (bkt + 1) * GB
            w = [ex_toks[bkt][NHF * GB - 1]] + gv_toks[max(0, hi - 6):hi]
            u = bkt * NT + st
            if u >= 2:
                pb, pst = divmod(u - 2, NT)
                w.append(dv_toks[pb][pst])
            av_toks[bkt][st] = p.step("tensor", em_av, waits=w, inc="t")

            def em_dv(v, st=st):
                v.reciprocal(rden[:], pav[st % 2][:, :, 32:34])
                v.drain()
                v.tensor_tensor(
                    a_sE[:, st, :].rearrange("p (g par c) -> p g par c",
                                             g=GB, c=16),
                    pav[st % 2][:, :, 0:32].rearrange(
                        "p g (par c) -> p g par c", c=16),
                    rden[:, :, :, None].broadcast_to((128, GB, 2, 16)),
                    op=AL.mult)
                v.drain()
                return v.nop()
            dv_toks[bkt][st] = p.step("vector", em_dv,
                                      waits=[av_toks[bkt][st]], inc="v")

        for st in range(NT):
            for eb in range(2):
                u2 = st * 2 + eb

                def em_ta(t, st=st, eb=eb, u2=u2):
                    return t.transpose(pat[u2 % 2][:],
                                       a_sE[:, st, 128 * eb:128 * (eb + 1)],
                                       ident[:])
                w = [dv_toks[bkt][st]]
                gu = bkt * 2 * NT + u2
                if gu >= 2:
                    pb, pu2 = divmod(gu - 2, 2 * NT)
                    w.append(ae_toks[pb][pu2])
                ta_toks[bkt][u2] = p.step("tensor", em_ta, waits=w, inc="t")

                def em_ae(v, st=st, eb=eb, u2=u2):
                    v.tensor_copy(aT_slab[:, eb, st, :], pat[u2 % 2][:])
                    v.drain()
                    return v.nop()
                w = [ta_toks[bkt][u2]]
                if bkt >= 1 and u2 == 0:
                    w.append(sp_toks[bkt - 1])
                ae_toks[bkt][u2] = p.step("vector", em_ae, waits=w, inc="v")

        def em_spill(sc, b=bkt):
            return sc.dma_start(out=at_view[b], in_=aT_slab[:])
        sp_toks[bkt] = p.step("scalar", em_spill,
                              waits=[ae_toks[bkt][2 * NT - 1]],
                              inc="dX", inc_by=16)

    p2b_end = [sp_toks[NBLK - 1], gam_toks[NGC - 1],
               av_toks[NBLK - 1][NT - 1], ta_toks[NBLK - 1][2 * NT - 1],
               ae_toks[NBLK - 1][2 * NT - 1],
               ex_toks[NBLK - 1][NHF * GB - 1]] + p.tails("dX", "dY", "dG")

    p2b.close()
    p3 = contextlib.ExitStack()
    wraw = sbuf("wraw", [128, 3, DT, 128], F32, p3)
    sa1 = sbuf("sa1", [128, DT, 128], BF16, p3)
    sb1 = sbuf("sb1", [128, DT, 128], BF16, p3)
    wq1 = sbuf("wq1", [128, 2, DT, 128], BF16, p3)
    wq2 = sbuf("wq2", [128, 2, DT, 128], BF16, p3)
    wq3 = sbuf("wq3", [128, 2, SFT, 128], BF16, p3)
    sigb = sbuf("sigb", [128, 2, S], BF16, p3)
    hid = sbuf("hid", [128, SFT, S], BF16, p3)
    o3st = sbuf("o3st", [128, 2, S], BF16, p3)
    pth_sc = contextlib.ExitStack()
    pth = psum("pth", [128, 4], F32, pth_sc)

    def em_gred(v):
        v.tensor_reduce(gam_acc[:, 0:1], gam_cols[:, 0:r1], axis=AX.X,
                        op=AL.add)
        v.drain()
        v.tensor_reduce(gam_acc[:, 1:2], gam_cols[:, r1:r2], axis=AX.X,
                        op=AL.add)
        v.drain()
        v.tensor_reduce(gam_acc[:, 2:3], gam_cols[:, r2:r2 + (NGC - r2)],
                        axis=AX.X, op=AL.add)
        v.drain()
        return v.nop()
    t_gr = p.step("vector", em_gred, waits=[gam_toks[NGC - 1]], inc="v")

    def em_gpr(gp):
        gp.tensor_reduce(gam_sc[:], gam_acc[:], axis=AX.C, op=AL.add)
        gp.drain()
        return gp.nop()
    t_gp = p.step("gpsimd", em_gpr, waits=[t_gr], inc="gp")

    def em_thr(v):
        v.tensor_scalar(thr_sc[:], gam_sc[:], 0.5 / n_w_elems, 0.5e-8,
                        op0=AL.mult, op1=AL.add)
        v.drain()
        return v.nop()
    t_th = p.step("vector", em_thr, waits=[t_gp], inc="v")

    t_tm = p.step("tensor",
                  lambda t: t.matmul(pth[:, 0:3], ones_row[:], thr_sc[:],
                                     start=True, stop=True),
                  waits=[t_th] + p2b_end, inc="t")

    def em_thev(v):
        v.tensor_copy(thr[:], pth[:, 0:3])
        v.drain()
        v.tensor_scalar(thrn[:], pth[:, 0:3], -1.0, None, op0=AL.mult)
        v.drain()
        return v.nop()
    t_te = p.step("vector", em_thev, waits=[t_tm], inc="v")
    pth_sc.close()
    pf1 = psum("pf1", [128, S], F32, p3)
    pf2 = psum("pf2", [128, S], F32, p3)


    slab_seq = []
    for sti in range(NST):
        for fi in range(SFT):
            f = sti * SFT + fi
            slab_seq.append((0, f))
            slab_seq.append((1, f))
        for d in range(DT):
            slab_seq.append((2, (sti, d)))
    NSLAB = len(slab_seq)
    sl_load = [None] * NSLAB
    sl_q = [None] * NSLAB
    sl_a = [None] * NSLAB
    slab_of = {}
    for i, ka in enumerate(slab_seq):
        slab_of[ka] = i

    f1_toks = [None] * FT
    f2_toks = [None] * FT
    sg_toks = [None] * FT
    hd_toks = [None] * FT
    f3_toks = [None] * (NST * DT)
    o3_toks = [None] * (NST * DT)
    od_toks = [None] * (NST * DT)

    def reg_slab(i):
        kind, arg = slab_seq[i]
        if kind == 0:
            src = w1s_d[arg]
            nk, bias_i = DT, 0
        elif kind == 1:
            src = w2s_d[arg]
            nk, bias_i = DT, 1
        else:
            sti, d = arg
            src = w3s_d[sti, d]
            nk, bias_i = SFT, 2

        def em_l(sy, src=src, i=i, nk=nk):
            return sy.dma_start(out=wraw[:, i % 3, :nk, :], in_=src)
        w = list(p2b_end) if i < 3 else [sl_q[i - 3]]
        sl_load[i] = p.step("sync", em_l, waits=w, inc="dY", inc_by=16)

        def em_q(sc, i=i, nk=nk, bias_i=bias_i):
            sc.activation(sa1[:, :nk, :], wraw[:, i % 3, :nk, :], AF.Sign,
                          bias=thr[:, bias_i:bias_i + 1], scale=1.0)
            sc.drain()
            sc.activation(sb1[:, :nk, :], wraw[:, i % 3, :nk, :], AF.Sign,
                          bias=thrn[:, bias_i:bias_i + 1], scale=1.0)
            sc.drain()
            return sc.nop()
        w = [sl_load[i], t_te]
        if i >= 1:
            w.append(sl_a[i - 1])
        sl_q[i] = p.step("scalar", em_q, waits=w, inc="sc")

        if kind == 0:
            dst, ridx, nk2 = wq1, arg % 2, DT
        elif kind == 1:
            dst, ridx, nk2 = wq2, arg % 2, DT
        else:
            dst, ridx, nk2 = wq3, arg[1] % 2, SFT

        def em_a(v, dst=dst, ridx=ridx, nk2=nk2):
            v.tensor_tensor(dst[:, ridx, :nk2, :], sa1[:, :nk2, :],
                            sb1[:, :nk2, :], op=AL.add)
            v.drain()
            return v.nop()
        w = [sl_q[i]]
        if kind == 0 and arg >= 2:
            w.append(f1_toks[arg - 2])
        if kind == 1 and arg >= 2:
            w.append(f2_toks[arg - 2])
        if kind == 2:
            u3 = arg[0] * DT + arg[1]
            if u3 >= 2:
                w.append(f3_toks[u3 - 2])
        sl_a[i] = p.step("vector", em_a, waits=w, inc="v")

    def reg_slab_upto(i):
        while reg_ptr[0] <= min(i, NSLAB - 1):
            reg_slab(reg_ptr[0])
            reg_ptr[0] += 1

    reg_ptr = [0]
    reg_slab_upto(1)
    for sti in range(NST):
        for fi in range(SFT):
            f = sti * SFT + fi
            if fi + 1 < SFT:
                reg_slab_upto(slab_of[(1, f + 1)])
            else:
                reg_slab_upto(slab_of[(2, (sti, 0))])

            def em_f1(t, f=f):
                last = None
                for kt in range(DT):
                    for ch in range(NCH):
                        last = t.matmul(pf1[:, 512 * ch:512 * (ch + 1)],
                                        wq1[:, f % 2, kt, :],
                                        xnT[:, kt, 512 * ch:512 * (ch + 1)],
                                        start=(kt == 0), stop=(kt == DT - 1))
                return last
            w = [sl_a[slab_of[(0, f)]]]
            if f >= 1:
                w.append(sg_toks[f - 1])
            if fi == 0 and sti >= 1:
                w.append(o3_toks[sti * DT - 1])
                w.append(o3_toks[sti * DT - 2])
            f1_toks[f] = p.step("tensor", em_f1, waits=w, inc="t")

            def em_sg(sc, f=f):
                return sc.activation(sigb[:, f % 2, :], pf1[:], AF.Sigmoid,
                                     scale=0.5)
            w = [f1_toks[f]]
            if f >= 2:
                w.append(hd_toks[f - 2])
            sg_toks[f] = p.step("scalar", em_sg, waits=w, inc="sc")

            def em_f2(t, f=f):
                last = None
                for kt in range(DT):
                    for ch in range(NCH):
                        last = t.matmul(pf2[:, 512 * ch:512 * (ch + 1)],
                                        wq2[:, f % 2, kt, :],
                                        xnT[:, kt, 512 * ch:512 * (ch + 1)],
                                        start=(kt == 0), stop=(kt == DT - 1))
                return last
            w = [sl_a[slab_of[(1, f)]]]
            if f >= 1:
                w.append(hd_toks[f - 1])
            if fi == 0 and sti >= 1:
                w.append(o3_toks[sti * DT - 1])
            f2_toks[f] = p.step("tensor", em_f2, waits=w, inc="t")

            def em_hd(v, f=f, fi=fi):
                v.scalar_tensor_tensor(hid[:, fi, :], pf2[:], 0.5,
                                       sigb[:, f % 2, :],
                                       op0=AL.mult, op1=AL.mult)
                v.drain()
                return v.nop()
            hd_toks[f] = p.step("vector", em_hd,
                                waits=[f2_toks[f], sg_toks[f]], inc="v")

        for d in range(DT):
            u = sti * DT + d
            if d + 1 < DT:
                reg_slab_upto(slab_of[(2, (sti, d + 1))])
            elif sti + 1 < NST:
                reg_slab_upto(slab_of[(1, (sti + 1) * SFT)])

            def em_f3(t, sti=sti, d=d):
                pp = pf1 if d % 2 == 0 else pf2
                last = None
                for ki in range(SFT):
                    for ch in range(NCH):
                        last = t.matmul(pp[:, 512 * ch:512 * (ch + 1)],
                                        wq3[:, d % 2, ki, :],
                                        hid[:, ki, 512 * ch:512 * (ch + 1)],
                                        start=(ki == 0),
                                        stop=(ki == SFT - 1))
                return last
            w = [sl_a[slab_of[(2, (sti, d))]], hd_toks[sti * SFT + SFT - 1]]
            if d <= 1:
                w.append(sg_toks[sti * SFT + SFT - 1])
            if u >= 2:
                w.append(o3_toks[u - 2])
            f3_toks[u] = p.step("tensor", em_f3, waits=w, inc="t")

            def em_o3(v, d=d, u=u):
                v.tensor_copy(o3st[:, u % 2, :],
                              (pf1 if d % 2 == 0 else pf2)[:])
                v.drain()
                return v.nop()
            w = [f3_toks[u]]
            if u >= 2:
                w.append(od_toks[u - 2])
            o3_toks[u] = p.step("vector", em_o3, waits=w, inc="v")

            def em_od(gp, sti=sti, d=d, u=u):
                kw = {} if sti == 0 else {"accum_op": AL.add}
                return gp.dma_start(out=o3_scr[128 * d:128 * (d + 1), :],
                                    in_=o3st[:, u % 2, :], **kw)
            od_toks[u] = p.step("gpsimd", em_od, waits=[o3_toks[u]],
                                inc="dG", inc_by=16)

    p3_end = [od_toks[NST * DT - 1], f3_toks[NST * DT - 1],
              o3_toks[NST * DT - 1], sl_q[NSLAB - 1],
              sl_a[NSLAB - 1], f2_toks[FT - 1]] + p.tails("dX", "dY", "dG")

    p3.close()
    xnT_sc.close()
    p4 = contextlib.ExitStack()
    attnF = sbuf("attnF", [128, DT, S], BF16, p4)
    woraw = sbuf("woraw", [128, 2, DT, 128], F32, p4)
    wob = sbuf("wob", [128, 2, DT, 128], BF16, p4)
    xTt = sbuf("xTt", [128, 2, S], F32, p4)
    o3ld = sbuf("o3ld", [128, 2, S], BF16, p4)
    outA = sbuf("outA", [128, S], F32, p4)
    outB = sbuf("outB", [128, 2, S], F32, p4)
    pwo = [psum("pwo%d" % i, [128, S], F32, p4) for i in range(2)]

    atv = at_scr.rearrange("(kt p) s -> p kt s", p=128)
    o3v = o3_scr.rearrange("(mt p) s -> p mt s", p=128)
    xTv = xT_d.rearrange("(mt p) s -> p mt s", p=128)

    at_toks = []
    for kt in range(DT):
        def em_atl(sc, kt=kt):
            return sc.dma_start(out=attnF[:, kt, :], in_=atv[:, kt, :])
        w = (list(p3_end) + [sp_toks[NBLK - 1]]) if kt == 0 else []
        at_toks.append(p.step("scalar", em_atl, waits=w, inc="dX", inc_by=16))
    tok_atF = at_toks[-1]

    wol_toks = [None] * DT
    woc_toks = [None] * DT
    wo_toks = [None] * DT
    xl_toks = [None] * DT
    ol_toks = [None] * DT
    fA_toks = [None] * DT
    fB_toks = [None] * DT
    out_toks = [None] * DT

    for mt in range(DT):
        def em_wol(sc, mt=mt):
            return sc.dma_start(out=woraw[:, mt % 2, :, :], in_=wos_d[mt])
        w = list(p3_end) if mt == 0 else []
        if mt >= 2:
            w.append(woc_toks[mt - 2])
        wol_toks[mt] = p.step("scalar", em_wol, waits=w, inc="dX", inc_by=16)

        def em_woc(v, mt=mt):
            v.tensor_copy(wob[:, mt % 2, :, :], woraw[:, mt % 2, :, :])
            v.drain()
            return v.nop()
        w = [wol_toks[mt]] + (list(p3_end) if mt == 0 else [])
        if mt >= 2:
            w.append(wo_toks[mt - 2])
        woc_toks[mt] = p.step("vector", em_woc, waits=w, inc="v")

        def em_xl(sy, mt=mt):
            return sy.dma_start(out=xTt[:, mt % 2, :], in_=xTv[:, mt, :])
        w = [fA_toks[mt - 2]] if mt >= 2 else list(p3_end)
        xl_toks[mt] = p.step("sync", em_xl, waits=w, inc="dY", inc_by=16)

        def em_ol(sy, mt=mt):
            return sy.dma_start(out=o3ld[:, mt % 2, :], in_=o3v[:, mt, :])
        w = [od_toks[(NST - 1) * DT + mt]]
        if mt >= 2:
            w.append(fB_toks[mt - 2])
        else:
            w += list(p3_end)
        ol_toks[mt] = p.step("sync", em_ol, waits=w, inc="dY", inc_by=16)

        def em_wo(t, mt=mt):
            last = None
            for kt in range(DT):
                for ch in range(NCH):
                    last = t.matmul(pwo[mt % 2][:, 512 * ch:512 * (ch + 1)],
                                    wob[:, mt % 2, kt, :],
                                    attnF[:, kt, 512 * ch:512 * (ch + 1)],
                                    start=(kt == 0), stop=(kt == DT - 1))
            return last
        w = [woc_toks[mt]] + at_toks
        if mt >= 2:
            w.append(fA_toks[mt - 2])
        wo_toks[mt] = p.step("tensor", em_wo, waits=w, inc="t")

        def em_fA(v, mt=mt):
            v.scalar_tensor_tensor(outA[:], pwo[mt % 2][:], 1.0,
                                   xTt[:, mt % 2, :], op0=AL.mult, op1=AL.add)
            v.drain()
            return v.nop()
        w = [wo_toks[mt], xl_toks[mt]]
        if mt >= 1:
            w.append(fB_toks[mt - 1])
        fA_toks[mt] = p.step("vector", em_fA, waits=w, inc="v")

        def em_fB(v, mt=mt):
            v.scalar_tensor_tensor(outB[:, mt % 2, :], o3ld[:, mt % 2, :],
                                   0.5, outA[:], op0=AL.mult, op1=AL.add)
            v.drain()
            return v.nop()
        w = [fA_toks[mt], ol_toks[mt]]
        if mt >= 2:
            w.append(out_toks[mt - 2])
        fB_toks[mt] = p.step("vector", em_fB, waits=w, inc="v")

        def em_out(sc, mt=mt):
            return sc.dma_start(out=out_d[128 * mt:128 * (mt + 1), :],
                                in_=outB[:, mt % 2, :])
        out_toks[mt] = p.step("scalar", em_out, waits=[fB_toks[mt]],
                              inc="dX", inc_by=16)

    p.step("scalar", lambda sc: None, waits=[out_toks[DT - 1]])
    p.step("sync", lambda sy: None, waits=[sl_load[NSLAB - 1]])
    p.step("gpsimd", lambda gp: None, waits=[od_toks[NST * DT - 1]])

    with nc.Block() as blk:
        p.emit(blk)
    p4.close()

    return nc


_CACHED = {}


def _get_nc():
    if "nc" not in _CACHED:
        _CACHED["nc"] = build_program()
    return _CACHED["nc"]


def kernel(L0_attn_x, L0_attn_Wq, L0_attn_Wk, L0_attn_Wv, L0_attn_Wo,
           L0_ffn_w1, L0_ffn_w2, L0_ffn_w3):
    B = L0_attn_x.shape[0]
    nc = _get_nc()

    def C(a):
        return np.ascontiguousarray(np.asarray(a), dtype=np.float32)

    D = L0_attn_x.shape[2]
    S = L0_attn_x.shape[1]
    H = L0_attn_Wq.shape[0]
    F = L0_ffn_w1.shape[0]
    DT, FT, NVE = D // 128, F // 128, D // 512
    NST = 4 if FT % 4 == 0 else (2 if FT % 2 == 0 else 1)
    SFT = FT // NST
    wqs = C(np.asarray(L0_attn_Wq).T.reshape(DT, 128, H).transpose(1, 0, 2))
    wks = C(np.asarray(L0_attn_Wk).T.reshape(DT, 128, H).transpose(1, 0, 2))
    wvs = C(np.asarray(L0_attn_Wv).T.reshape(DT, 128, NVE, 512)
            .transpose(2, 1, 0, 3))
    wos = C(np.asarray(L0_attn_Wo).T.reshape(DT, 128, DT, 128)
            .transpose(2, 1, 0, 3))
    w1s = C(np.asarray(L0_ffn_w1).T.reshape(DT, 128, FT, 128)
            .transpose(2, 1, 0, 3))
    w2s = C(np.asarray(L0_ffn_w2).T.reshape(DT, 128, FT, 128)
            .transpose(2, 1, 0, 3))
    w3s = C(np.asarray(L0_ffn_w3).T.reshape(NST, SFT, 128, DT, 128)
            .transpose(0, 3, 2, 1, 4))

    in_maps = []
    for b in range(B):
        xb = C(L0_attn_x[b])
        in_maps.append({
            "x_tok": xb, "xT": C(xb.T),
            "wqs": wqs, "wks": wks, "wvs": wvs, "wos": wos,
            "w1s": w1s, "w2s": w2s, "w3s": w3s,
        })

    res = run_bass_kernel_spmd(nc, in_maps, core_ids=list(range(B)))
    out = np.empty((B, L0_attn_x.shape[1], L0_attn_x.shape[2]),
                   dtype=np.float32)
    for b in range(B):
        out[b] = np.asarray(res.results[b]["out"]).T
    return out


# version 17
# speedup vs baseline: 1.0670x; 1.0670x over previous
import numpy as np
import contextlib
import concourse.bass as bass
import concourse.mybir as mybir
from concourse.bass_utils import run_bass_kernel_spmd
from concourse import library_config

F32 = mybir.dt.float32
BF16 = mybir.dt.bfloat16
U16 = mybir.dt.uint16
I16 = mybir.dt.int16
AL = mybir.AluOpType
AF = mybir.ActivationFunctionType
AX = mybir.AxisListType

BIGC = float(2 ** 23)
NEG_BIG = -1e30
ENGINES = ("sync", "scalar", "vector", "tensor", "gpsimd")


class Prog:

    def __init__(self, nc):
        self.nc = nc
        self.steps = {e: [] for e in ENGINES}
        self.sems = {}
        self.counts = {}
        self.last_dma = {}
        self._es = contextlib.ExitStack()

    def sem(self, name):
        if name not in self.sems:
            self.sems[name] = self._es.enter_context(self.nc.semaphore(name))
            self.counts[name] = 0
        return self.sems[name]

    NSUB = 6

    def step(self, engine, emit, waits=(), inc=None, inc_by=1):
        tok = None
        waits = [w for w in waits if w is not None]
        if inc is not None:
            if inc_by == 16:
                jname = inc + "#j"
                j = self.counts.get(jname, 0)
                self.counts[jname] = j + 1
                sub = "%s%d" % (inc, j % self.NSUB)
                self.sem(sub)
                self.counts[sub] += 16
                if self.counts[sub] > 16:
                    waits.append((sub, self.counts[sub] - 16))
                tok = (sub, self.counts[sub], 16)
                self.last_dma.setdefault(inc, {})[sub] = tok
            else:
                self.sem(inc)
                self.counts[inc] += inc_by
                tok = (inc, self.counts[inc], inc_by)
        self.steps[engine].append((tuple(waits), emit, tok))
        return tok

    def tails(self, *queues):
        out = []
        for q in queues:
            out.extend(self.last_dma.get(q, {}).values())
        return out

    def emit(self, block):
        for engine in ENGINES:
            steps = self.steps[engine]
            if not steps:
                continue

            def body(eng, engine_steps=steps):
                waited = {}
                for waits, emit_fn, tok in engine_steps:
                    for w in waits:
                        sname, val = w[0], w[1]
                        if waited.get(sname, -1) < val:
                            eng.wait_ge(self.sems[sname], val)
                            waited[sname] = val
                    last = emit_fn(eng)
                    if tok is not None:
                        if last is None:
                            last = eng.nop()
                        last.then_inc(self.sems[tok[0]], tok[2])

            getattr(block, engine)(body)


def build_program(S=2048, F=5632):
    D, H = 2048, 128
    NT = S // 128
    DT = D // 128
    FT = F // 128
    NCH = S // 512
    G = H // 2
    GB = 8
    NBLK = G // GB
    NST = 4 if FT % 4 == 0 else (2 if FT % 2 == 0 else 1)
    SFT = FT // NST
    n_w_elems = float(F * D)

    nc = bass.Bass("TRN2", target_bir_lowering=False)
    p = Prog(nc)

    NVE_ = D // 512
    NST_ = 4 if FT % 4 == 0 else (2 if FT % 2 == 0 else 1)
    SFT_ = FT // NST_
    x_tok = nc.dram_tensor("x_tok", [S, D], F32, kind="ExternalInput").ap()
    xT_d = nc.dram_tensor("xT", [D, S], F32, kind="ExternalInput").ap()
    wqs_d = nc.dram_tensor("wqs", [128, DT, H], F32, kind="ExternalInput").ap()
    wks_d = nc.dram_tensor("wks", [128, DT, H], F32, kind="ExternalInput").ap()
    wvs_d = nc.dram_tensor("wvs", [NVE_, 128, DT, 512], F32,
                           kind="ExternalInput").ap()
    wos_d = nc.dram_tensor("wos", [DT, 128, DT, 128], F32,
                           kind="ExternalInput").ap()
    w1s_d = nc.dram_tensor("w1s", [FT, 128, DT, 128], F32,
                           kind="ExternalInput").ap()
    w2s_d = nc.dram_tensor("w2s", [FT, 128, DT, 128], F32,
                           kind="ExternalInput").ap()
    w3s_d = nc.dram_tensor("w3s", [NST_, DT, 128, SFT_, 128], F32,
                           kind="ExternalInput").ap()
    out_d = nc.dram_tensor("out", [D, S], F32, kind="ExternalOutput").ap()
    at_scr = nc.dram_tensor("at_scr", [D, S], BF16, kind="Internal").ap()
    o3_scr = nc.dram_tensor("o3_scr", [D, S], BF16, kind="Internal").ap()
    v_scr = nc.dram_tensor("v_scr", [S, D], BF16, kind="Internal").ap()

    es = contextlib.ExitStack()

    def sbuf(name, shape, dtype, stack=None, side=None):
        kw = {} if side is None else {"side": side}
        return (stack or es).enter_context(
            nc.sbuf_tensor(name, shape, dtype, **kw))

    def psum(name, shape, dtype=F32, stack=None):
        return (stack or es).enter_context(
            nc.psum_tensor(name, shape, dtype))

    xnT_sc = contextlib.ExitStack()
    xnT = sbuf("xnT", [128, DT, S], BF16, xnT_sc, side="right")
    ident = sbuf("ident", [128, 128], BF16)
    iota_r = sbuf("iota_r", [128, 128], F32)
    iota_g = sbuf("iota_g", [128, G], F32)
    iota_rp = sbuf("iota_rp", [128, 128], F32)
    hcol = sbuf("hcol", [128, 1], F32)
    hcol_i = sbuf("hcol_i", [128, 1], mybir.dt.int32)
    hbit_i = sbuf("hbit_i", [128, 1], mybir.dt.int32)
    hm2 = sbuf("hm2", [128, 1], F32)
    h2col = sbuf("h2col", [128, 1], F32)
    rlt64 = sbuf("rlt64", [128, 1], F32)
    nt0 = sbuf("nt0", [128, 1], F32)
    biascol = sbuf("biascol", [128, 1], F32)
    Gm = sbuf("Gm", [128, G], BF16)
    Pm = sbuf("Pm", [128, 128], BF16)
    csr = sbuf("csr", [128, 128], F32)
    KV2 = sbuf("KV2", [128, 128], BF16)
    CP2 = sbuf("CP2", [128, 128], BF16)
    gam_cols = sbuf("gam_cols", [128, 192], F32)
    gam_acc = sbuf("gam_acc", [128, 3], F32)
    gam_sc = sbuf("gam_sc", [1, 3], F32)
    thr_sc = sbuf("thr_sc", [1, 3], F32)
    ones_row = sbuf("ones_row", [1, 128], F32)
    thr = sbuf("thr", [128, 3], F32)
    thrn = sbuf("thrn", [128, 3], F32)
    qb = sbuf("qb", [128, S], BF16)
    gq = sbuf("gq", [128, S], BF16)
    kvals = sbuf("kvals", [128, 64], F32)
    Vsel = sbuf("Vsel", [128, G, 34], BF16)
    idxb0 = sbuf("idxb0", [64, G], mybir.dt.uint32)
    idxb1 = sbuf("idxb1", [64, G], mybir.dt.uint32)

    gsc = contextlib.ExitStack()
    GCW = 4096
    gslab = sbuf("gslab", [128, 2, GCW], F32, gsc)
    gdum = sbuf("gdum", [128, GCW], BF16, gsc)

    gchunks = []
    nper = []
    for wd in (w1s_d, w2s_d, w3s_d):
        flat = wd.rearrange("a b c d -> (a b c d)") if len(wd.shape) == 4 \
            else wd.rearrange("a b c d e -> (a b c d e)")
        nch_ = flat.shape[0] // (128 * GCW)
        assert nch_ * 128 * GCW == flat.shape[0]
        chv = flat.rearrange("(n p q) -> n p q", p=128, q=GCW)
        for i_ in range(nch_):
            gchunks.append((chv[i_], GCW))
        nper.append(nch_)
    NGC = len(gchunks)
    assert NGC <= 192
    r1 = nper[0]
    r2 = nper[0] + nper[1]

    gam_toks = [None] * NGC
    gslab_toks = [None] * NGC

    def reg_gamma(i):
        ap, nel = gchunks[i]

        def em_dma(sy, ap=ap, nel=nel, i=i):
            return sy.dma_start(out=gslab[:, i % 2, :nel], in_=ap)
        w = [gam_toks[i - 2]] if i >= 2 else []
        gslab_toks[i] = p.step("sync", em_dma, waits=w, inc="dY", inc_by=16)

        def em_abs(sc, i=i, nel=nel):
            sc.activation(gdum[:, :nel], gslab[:, i % 2, :nel],
                          AF.Abs, accum_out=gam_cols[:, i:i + 1])
            sc.drain()
            return sc.nop()
        gam_toks[i] = p.step("scalar", em_abs, waits=[gslab_toks[i]],
                             inc="sGa")

    def em_iotas(gp):
        gp.iota(iota_r[:], pattern=[[1, 128]], base=0, channel_multiplier=0,
                allow_small_or_imprecise_dtypes=True)
        gp.iota(iota_g[:], pattern=[[1, G]], base=0, channel_multiplier=0,
                allow_small_or_imprecise_dtypes=True)
        gp.iota(iota_rp[:], pattern=[[1, 2], [0, 64]], base=0,
                channel_multiplier=0, allow_small_or_imprecise_dtypes=True)
        gp.iota(hcol[:], pattern=[[0, 1]], base=0, channel_multiplier=1,
                allow_small_or_imprecise_dtypes=True)
        gp.iota(hcol_i[:], pattern=[[0, 1]], base=0, channel_multiplier=1)
        gp.iota(csr[:], pattern=[[0, 2], [1, 2], [0, 32]], base=0,
                channel_multiplier=0, allow_small_or_imprecise_dtypes=True)
        gp.memset(gam_cols[:], 0.0)
        gp.memset(ones_row[:], 1.0)
        gp.drain()
        return gp.nop()
    tok_iota = p.step("gpsimd", em_iotas, inc="gp")

    def em_ident(v):
        v.tensor_scalar(ident[:], iota_r[:], hcol[:], None, op0=AL.is_equal)
        v.drain()
        return v.nop()
    tok_ident = p.step("vector", em_ident, waits=[tok_iota], inc="v")

    def em_masks(v):
        v.tensor_scalar(hbit_i[:], hcol_i[:], 1, None, op0=AL.bitwise_and)
        v.drain()
        v.tensor_copy(hm2[:], hbit_i[:])
        v.drain()
        v.tensor_tensor(h2col[:], hcol[:], hm2[:], op=AL.subtract)
        v.drain()
        v.tensor_scalar(h2col[:], h2col[:], 0.5, None, op0=AL.mult)
        v.drain()
        v.tensor_scalar(Gm[:], iota_g[:], h2col[:], None, op0=AL.is_equal)
        v.drain()
        v.tensor_scalar(Pm[:], iota_rp[:], hm2[:], None, op0=AL.is_equal)
        v.drain()
        v.tensor_scalar(csr[:], csr[:], -2.0 * BIGC, BIGC,
                        op0=AL.mult, op1=AL.add)
        v.drain()
        v.tensor_tensor(CP2[:], csr[:], Pm[:], op=AL.mult)
        v.drain()
        v.tensor_scalar(rlt64[:], hcol[:], 64.0, None, op0=AL.is_lt)
        v.drain()
        v.tensor_scalar(hbit_i[:], hcol_i[:], 32, None, op0=AL.bitwise_and)
        v.drain()
        v.tensor_copy(nt0[:], hbit_i[:])
        v.drain()
        v.tensor_scalar(nt0[:], nt0[:], 0.0, None, op0=AL.is_equal)
        v.drain()
        v.tensor_scalar(biascol[:], nt0[:], -BIGC, None, op0=AL.mult)
        v.drain()
        v.memset(Vsel[:], 0.0)
        v.drain()
        v.tensor_scalar(Vsel[:, :, 32:33],
                        rlt64[:, None, :].broadcast_to((128, G, 1)),
                        1.0, None, op0=AL.mult)
        v.drain()
        v.tensor_scalar(Vsel[:, :, 33:34],
                        rlt64[:, None, :].broadcast_to((128, G, 1)),
                        -1.0, 1.0, op0=AL.mult, op1=AL.add)
        v.drain()
        return v.nop()
    tok_masks = p.step("vector", em_masks, inc="v")

    p1 = contextlib.ExitStack()
    XR = 3
    x_tile = sbuf("x_tile", [128, XR, D], F32, p1)
    xn_stage = sbuf("xn_stage", [128, XR, D], BF16, p1)
    stats = sbuf("stats", [128, XR, 24], F32, p1)
    mv = sbuf("mv", [128, XR, 2], F32, p1)
    veps = sbuf("veps", [128, XR, 1], F32, p1)
    rs0 = sbuf("rs0", [128, XR, 1], F32, p1)
    rs1 = sbuf("rs1", [128, XR, 1], F32, p1)
    nt1 = sbuf("nt1", [128, XR, 1], F32, p1)
    pxn = [psum("pxn%d" % i, [128, DT, 128], BF16, p1) for i in range(2)]

    gamma_ptr = [0]
    xn_toks = [None] * NT
    xload_toks = [None] * NT
    xpose_toks = [[None] * DT for _ in range(NT)]
    xev_toks = [[None] * DT for _ in range(NT)]
    sqrt_toks = [None] * NT

    for st in range(NT):
        b = st % XR

        def em_xload(gp, st=st, b=b):
            return gp.dma_start(out=x_tile[:, b, :],
                                in_=x_tok[128 * st:128 * (st + 1), :])
        w = [xn_toks[st - XR]] if st >= XR else []
        xload_toks[st] = p.step("gpsimd", em_xload, waits=w, inc="dG",
                                inc_by=16)

        def em_stats(v, b=b):
            for c in range(4):
                v.bn_stats(stats[:, b, 6 * c:6 * c + 6],
                           x_tile[:, b, 512 * c:512 * (c + 1)])
            v.drain()
            v.bn_aggr(mv[:, b, :], stats[:, b, :])
            v.drain()
            v.tensor_scalar(veps[:, b, :], mv[:, b, 1:2], 1e-5, None,
                            op0=AL.add)
            v.drain()
            return v.nop()
        t_stats = p.step("vector", em_stats, waits=[xload_toks[st]], inc="v")

        def em_sqrt(sc, b=b):
            return sc.activation(rs0[:, b, :], veps[:, b, :], AF.Sqrt)
        sqrt_toks[st] = p.step("scalar", em_sqrt, waits=[t_stats], inc="sc")

        def em_xn(v, b=b):
            v.reciprocal(rs1[:, b, :], rs0[:, b, :])
            v.drain()
            v.tensor_tensor(nt1[:, b, :], rs1[:, b, :], rs1[:, b, :],
                            op=AL.mult)
            v.drain()
            v.tensor_tensor(nt1[:, b, :], nt1[:, b, :], veps[:, b, :],
                            op=AL.mult)
            v.drain()
            v.tensor_scalar(nt1[:, b, :], nt1[:, b, :], -0.5, 1.5,
                            op0=AL.mult, op1=AL.add)
            v.drain()
            v.tensor_tensor(rs1[:, b, :], rs1[:, b, :], nt1[:, b, :],
                            op=AL.mult)
            v.drain()
            v.tensor_scalar(xn_stage[:, b, :], x_tile[:, b, :],
                            mv[:, b, 0:1], rs1[:, b, :],
                            op0=AL.subtract, op1=AL.mult)
            v.drain()
            return v.nop()
        xn_toks[st] = p.step("vector", em_xn, waits=[sqrt_toks[st]], inc="v")

        def em_tp(t, st=st, b=b):
            last = None
            for dt in range(DT):
                last = t.transpose(pxn[st % 2][:, dt, :],
                                   xn_stage[:, b, 128 * dt:128 * (dt + 1)],
                                   ident[:])
            return last
        w = [xn_toks[st], tok_ident]
        if st >= 2:
            w.append(xev_toks[st - 2][0])
        xpose_toks[st][0] = p.step("tensor", em_tp, waits=w, inc="t")

        def em_ev(v, st=st):
            v.tensor_copy(xnT[:, :, 128 * st:128 * (st + 1)], pxn[st % 2][:])
            v.drain()
            return v.nop()
        xev_toks[st][0] = p.step("vector", em_ev, waits=[xpose_toks[st][0]],
                                 inc="v")
        while gamma_ptr[0] < min(NGC, 2 * (st + 1)):
            reg_gamma(gamma_ptr[0])
            gamma_ptr[0] += 1

    tok_xnT = xev_toks[NT - 1][0]
    p1_end = [tok_xnT, xpose_toks[NT - 1][0], sqrt_toks[NT - 1],
              xn_toks[NT - 1]]


    p1.close()
    p2a = contextlib.ExitStack()
    U32 = mybir.dt.uint32
    wvs2 = sbuf("wvs2", [128, 2, DT, 512], BF16, p2a)
    vtile = sbuf("vtile", [128, 4, 512], BF16, p2a)
    keysw = sbuf("keysw", [128, 3, S], F32, p2a)
    wqTb = sbuf("wqTb", [128, DT, H], BF16, p2a)
    wkTb = sbuf("wkTb", [128, DT, H], BF16, p2a)
    idx = sbuf("idx", [128, 64], U16, p2a)
    idxcf = sbuf("idxcf", [128, 64], F32, p2a)

    kvn = sbuf("kvn", [128, 32], F32, p2a)
    identF = sbuf("identF", [128, 128], F32, p2a)
    pqk = [psum("pqk%d" % i, [128, 512], F32, p2a) for i in range(4)]
    pidx = psum("pidx", [64, 128], F32, p2a)

    vflat = v_scr.rearrange("s (hh c) -> (s hh) c", c=16)

    t_idf = p.step("vector",
                   lambda v: v.tensor_scalar(identF[:], iota_r[:], hcol[:],
                                             None, op0=AL.is_equal),
                   waits=[tok_iota], inc="v")

    t_cq = p.step("gpsimd",
                  lambda gp: gp.dma_start(out=wqTb[:], in_=wqs_d[:]),
                  waits=p1_end, inc="dG", inc_by=16)
    t_ck = p.step("gpsimd",
                  lambda gp: gp.dma_start(out=wkTb[:], in_=wks_d[:]),
                  waits=p1_end, inc="dG", inc_by=16)

    qk_mm = [None] * (2 * NCH)
    qk_ev = [None] * (2 * NCH)
    for wi in range(2):
        for ch in range(NCH):
            u = wi * NCH + ch

            def em_mm(t, wi=wi, ch=ch, u=u):
                wt = wqTb if wi == 0 else wkTb
                last = None
                for kt in range(DT):
                    last = t.matmul(pqk[u % 4][:], wt[:, kt, :],
                                    xnT[:, kt, 512 * ch:512 * (ch + 1)],
                                    start=(kt == 0), stop=(kt == DT - 1))
                return last
            w = [t_cq if wi == 0 else t_ck, tok_xnT]
            if u >= 4:
                w.append(qk_ev[u - 4])
            qk_mm[u] = p.step("tensor", em_mm, waits=w, inc="t")

            if wi == 0:
                def em_ev(v, ch=ch, u=u):
                    v.tensor_copy(qb[:, 512 * ch:512 * (ch + 1)],
                                  pqk[u % 4][:])
                    v.drain()
                    v.tensor_scalar(gq[:, 512 * ch:512 * (ch + 1)],
                                    pqk[u % 4][:], 0.0, None,
                                    op0=AL.is_gt)
                    v.drain()
                    return v.nop()
            else:
                def em_ev(v, ch=ch, u=u):
                    v.tensor_copy(keysw[:, 0, 512 * ch:512 * (ch + 1)],
                                  pqk[u % 4][:])
                    v.drain()
                    return v.nop()
            qk_ev[u] = p.step("vector", em_ev, waits=[qk_mm[u]], inc="v")

    NQK = 2 * NCH
    NVE = D // 512
    vload_toks = [None] * NVE
    vcast_toks = [None] * NVE
    vmm_toks = [[None] * NT for _ in range(NVE)]
    vev_toks = [[None] * NT for _ in range(NVE)]
    vsp_toks = [[None] * NT for _ in range(NVE)]

    for ech in range(NVE):
        def em_vload(gp, ech=ech):
            return gp.dma_start(out=wvs2[:, ech % 2, :, :], in_=wvs_d[ech])
        w = []
        if ech >= 2:
            w.append(vmm_toks[ech - 2][NT - 1])
        vcast_toks[ech] = vload_toks[ech] = p.step(
            "gpsimd", em_vload, waits=w, inc="dG", inc_by=16)

        for mt in range(NT):
            uu = NQK + ech * NT + mt

            def em_vmm(t, ech=ech, mt=mt, uu=uu):
                last = None
                for kt in range(DT):
                    last = t.matmul(pqk[uu % 4][:],
                                    xnT[:, kt, 128 * mt:128 * (mt + 1)],
                                    wvs2[:, ech % 2, kt, :],
                                    start=(kt == 0), stop=(kt == DT - 1))
                return last
            w = [vcast_toks[ech]]
            pu = uu - 4
            if pu >= NQK:
                pe, pm = divmod(pu - NQK, NT)
                w.append(vev_toks[pe][pm])
            else:
                w.append(qk_ev[pu])
            vmm_toks[ech][mt] = p.step("tensor", em_vmm, waits=w, inc="t")

            def em_vev(v, ech=ech, mt=mt, uu=uu):
                v.tensor_copy(vtile[:, uu % 4, :], pqk[uu % 4][:])
                v.drain()
                return v.nop()
            w = [vmm_toks[ech][mt]]
            if uu - NQK >= 4:
                pe, pm = divmod(uu - NQK - 4, NT)
                w.append(vsp_toks[pe][pm])
            vev_toks[ech][mt] = p.step("vector", em_vev, waits=w, inc="v")

            def em_vsp(gp, ech=ech, mt=mt, uu=uu):
                return gp.dma_start(
                    out=v_scr[128 * mt:128 * (mt + 1),
                              512 * ech:512 * (ech + 1)],
                    in_=vtile[:, uu % 4, :])
            vsp_toks[ech][mt] = p.step("gpsimd", em_vsp,
                                       waits=[vev_toks[ech][mt]],
                                       inc="dG", inc_by=16)
            while gamma_ptr[0] < min(NGC, 2 * (ech * NT + mt)):
                reg_gamma(gamma_ptr[0])
                gamma_ptr[0] += 1

    def em_sel(v):
        seq = [0, 1, 2, 1]
        for r in range(4):
            src = keysw[:, seq[r], :]
            v.max(kvals[:, 8 * r:8 * r + 8], src)
            v.drain()
            v.max_index(idx[:, 8 * r:8 * r + 8], kvals[:, 8 * r:8 * r + 8],
                        src)
            if r < 3:
                v.match_replace(keysw[:, seq[r + 1], :],
                                kvals[:, 8 * r:8 * r + 8], src, NEG_BIG)
            v.drain()
        v.tensor_scalar(keysw[:, 1, :], keysw[:, 0, :], -1.0, None,
                        op0=AL.mult)
        v.drain()
        seq = [1, 2, 1, 2]
        for r in range(4):
            src = keysw[:, seq[r], :]
            v.max(kvn[:, 8 * r:8 * r + 8], src)
            v.drain()
            v.max_index(idx[:, 32 + 8 * r:32 + 8 * r + 8],
                        kvn[:, 8 * r:8 * r + 8], src)
            if r < 3:
                v.match_replace(keysw[:, seq[r + 1], :],
                                kvn[:, 8 * r:8 * r + 8], src, NEG_BIG)
            v.drain()
        v.tensor_scalar(kvals[:, 32:64], kvn[:], -1.0, None, op0=AL.mult)
        v.drain()
        return v.nop()
    tok_sel = p.step("vector", em_sel, waits=[qk_ev[2 * NCH - 1]], inc="v")

    def em_kv2(v):
        v.tensor_tensor(KV2.ap().rearrange("p (two j) -> p two j", j=64),
                        kvals[:, None, :].broadcast_to((128, 2, 64)),
                        Pm.ap().rearrange("p (two j) -> p two j", j=64),
                        op=AL.mult)
        v.drain()
        return v.nop()
    tok_kv2 = p.step("vector", em_kv2, waits=[tok_sel, tok_masks], inc="v")

    def em_idxc(v):
        v.tensor_copy(idxcf[:], idx[:])
        v.drain()
        v.tensor_scalar(idxcf[:], idxcf[:], 128.0, hcol[:],
                        op0=AL.mult, op1=AL.add)
        v.drain()
        return v.nop()
    t_ic = p.step("vector", em_idxc, waits=[tok_sel], inc="v")

    t_it = p.step("tensor",
                  lambda t: t.transpose(pidx[:], idxcf[:], identF[:]),
                  waits=[t_ic, t_idf], inc="t")

    def em_ib(v):
        v.tensor_copy(idxb0[:], pidx[:, 0::2])
        v.drain()
        v.tensor_copy(idxb1[:], pidx[:, 1::2])
        v.drain()
        return v.nop()
    t_ib = p.step("vector", em_ib, waits=[t_it], inc="v")

    sp_all = [vsp_toks[e][m] for e in range(NVE) for m in range(NT)]
    gv_toks = [None] * (2 * G)
    for g in range(G):
        def em_g0(gp, g=g):
            return gp.indirect_dma_start(
                out=Vsel[0:64, g, 0:16], out_offset=None, in_=vflat,
                in_offset=bass.IndirectOffsetOnAxis(ap=idxb0[:, g:g + 1],
                                                    axis=0))
        w = [t_ib, tok_masks] + (sp_all if g == 0 else [])
        gv_toks[2 * g] = p.step("gpsimd", em_g0, waits=w, inc="dG",
                                inc_by=16)

        def em_g1(gp, g=g):
            return gp.indirect_dma_start(
                out=Vsel[64:128, g, 16:32], out_offset=None, in_=vflat,
                in_offset=bass.IndirectOffsetOnAxis(ap=idxb1[:, g:g + 1],
                                                    axis=0))
        gv_toks[2 * g + 1] = p.step("gpsimd", em_g1, waits=[gv_toks[2 * g]],
                                    inc="dG", inc_by=16)
    tok_vsel = gv_toks[2 * G - 1]

    while gamma_ptr[0] < NGC:
        reg_gamma(gamma_ptr[0])
        gamma_ptr[0] += 1

    p2a_end = [tok_kv2, tok_sel, t_ck, t_it, t_ib,
               vmm_toks[NVE - 1][NT - 1], vev_toks[NVE - 1][NT - 1],
               qk_ev[2 * NCH - 1], gam_toks[NGC - 1]] \
        + p.tails("dX", "dY")

    p2a.close()
    gsc.close()
    p2b = contextlib.ExitStack()
    Esl = sbuf("Esl", [128, 2, GB, S], BF16, p2b)
    Lblk = sbuf("Lblk", [128, 2, GB, 128], BF16, p2b)
    Cblk = sbuf("Cblk", [128, 2, GB, 128], BF16, p2b)
    a_sE = sbuf("a_sE", [128, NT, 32 * GB], BF16, p2b)
    aT_slab = sbuf("aT_slab", [128, 2, NT, 128], BF16, p2b)
    rden = sbuf("rden", [128, GB, 2], F32, p2b)
    SCW = min(1024, S)
    NHF = S // SCW
    psc = [psum("psc%d" % i, [128, SCW], F32, p2b) for i in range(2)]
    pav = [psum("pav%d" % i, [128, GB, 34], F32, p2b) for i in range(2)]
    pat = [psum("pat%d" % i, [128, 128], BF16, p2b) for i in range(2)]

    lb_toks = [None] * NBLK
    sc_toks = [[None] * (NHF * GB) for _ in range(NBLK)]
    ex_toks = [[None] * (NHF * GB) for _ in range(NBLK)]
    av_toks = [[None] * NT for _ in range(NBLK)]
    dv_toks = [[None] * NT for _ in range(NBLK)]
    ta_toks = [[None] * (2 * NT) for _ in range(NBLK)]
    ae_toks = [[None] * (2 * NT) for _ in range(NBLK)]
    sp_toks = [None] * NBLK

    at_view = at_scr.rearrange("(bb eb p) (st j) -> bb p eb st j",
                               bb=NBLK, p=128, j=128)

    def reg_lb(bkt):
        def em_lb(v, b=bkt):
            gsl = slice(GB * b, GB * (b + 1))
            v.tensor_tensor(Lblk[:, b % 2, :, :],
                            KV2[:, None, :].broadcast_to((128, GB, 128)),
                            Gm[:, gsl, None].broadcast_to((128, GB, 128)),
                            op=AL.mult)
            v.drain()
            v.tensor_tensor(Cblk[:, b % 2, :, :],
                            CP2[:, None, :].broadcast_to((128, GB, 128)),
                            Gm[:, gsl, None].broadcast_to((128, GB, 128)),
                            op=AL.mult)
            v.drain()
            return v.nop()
        w = [tok_kv2] + (p2a_end if bkt == 0 else [])
        if bkt >= 2:
            w.append(sc_toks[bkt - 2][NHF * GB - 1])
        lb_toks[bkt] = p.step("vector", em_lb, waits=w, inc="v")

    reg_lb(0)
    for bkt in range(NBLK):
        if bkt + 1 < NBLK:
            reg_lb(bkt + 1)
        for gi in range(GB):
            for hf in range(NHF):
                u = (bkt * GB + gi) * NHF + hf
                uu = u % 2

                def em_sc_mm(t, b=bkt, gi=gi, hf=hf, uu=uu):
                    last = None
                    for cc in range(SCW // 512):
                        n0 = SCW * hf + 512 * cc
                        t.matmul(psc[uu][:, 512 * cc:512 * (cc + 1)],
                                 Lblk[:, b % 2, gi, :], qb[:, n0:n0 + 512],
                                 start=True, stop=False)
                        last = t.matmul(psc[uu][:, 512 * cc:512 * (cc + 1)],
                                        Cblk[:, b % 2, gi, :],
                                        gq[:, n0:n0 + 512],
                                        start=False, stop=True)
                    return last
                w = [lb_toks[bkt]]
                if u >= 2:
                    w.append(ex_toks[(u - 2) // (NHF * GB)]
                             [(u - 2) % (NHF * GB)])
                if gi == 0 and hf == 0 and bkt >= 2:
                    w.append(av_toks[bkt - 2][NT - 1])
                sc_toks[bkt][NHF * gi + hf] = p.step("tensor", em_sc_mm,
                                                     waits=w, inc="t")

                def em_exp(sc, b=bkt, gi=gi, hf=hf, uu=uu):
                    return sc.activation(
                        Esl[:, b % 2, gi, SCW * hf:SCW * (hf + 1)],
                        psc[uu][:], AF.Exp, bias=biascol[:], scale=1.0)
                ex_toks[bkt][NHF * gi + hf] = p.step(
                    "scalar", em_exp, waits=[sc_toks[bkt][NHF * gi + hf]],
                    inc="sEx")

        for st in range(NT):
            def em_av(t, b=bkt, st=st):
                last = None
                for gi in range(GB):
                    last = t.matmul(pav[st % 2][:, gi, :],
                                    Esl[:, b % 2, gi,
                                        128 * st:128 * (st + 1)],
                                    Vsel[:, b * GB + gi, :],
                                    start=True, stop=True)
                return last
            hi = 2 * (bkt + 1) * GB
            w = [ex_toks[bkt][NHF * GB - 1]] + gv_toks[max(0, hi - 6):hi]
            u = bkt * NT + st
            if u >= 2:
                pb, pst = divmod(u - 2, NT)
                w.append(dv_toks[pb][pst])
            av_toks[bkt][st] = p.step("tensor", em_av, waits=w, inc="t")

            def em_dv(v, st=st):
                v.reciprocal(rden[:], pav[st % 2][:, :, 32:34])
                v.drain()
                v.tensor_tensor(
                    a_sE[:, st, :].rearrange("p (g par c) -> p g par c",
                                             g=GB, c=16),
                    pav[st % 2][:, :, 0:32].rearrange(
                        "p g (par c) -> p g par c", c=16),
                    rden[:, :, :, None].broadcast_to((128, GB, 2, 16)),
                    op=AL.mult)
                v.drain()
                return v.nop()
            dv_toks[bkt][st] = p.step("vector", em_dv,
                                      waits=[av_toks[bkt][st]], inc="v")

        for st in range(NT):
            for eb in range(2):
                u2 = st * 2 + eb

                def em_ta(t, st=st, eb=eb, u2=u2):
                    return t.transpose(pat[u2 % 2][:],
                                       a_sE[:, st, 128 * eb:128 * (eb + 1)],
                                       ident[:])
                w = [dv_toks[bkt][st]]
                gu = bkt * 2 * NT + u2
                if gu >= 2:
                    pb, pu2 = divmod(gu - 2, 2 * NT)
                    w.append(ae_toks[pb][pu2])
                ta_toks[bkt][u2] = p.step("tensor", em_ta, waits=w, inc="t")

                def em_ae(v, st=st, eb=eb, u2=u2):
                    v.tensor_copy(aT_slab[:, eb, st, :], pat[u2 % 2][:])
                    v.drain()
                    return v.nop()
                w = [ta_toks[bkt][u2]]
                if bkt >= 1 and u2 == 0:
                    w.append(sp_toks[bkt - 1])
                ae_toks[bkt][u2] = p.step("vector", em_ae, waits=w, inc="v")

        def em_spill(sc, b=bkt):
            return sc.dma_start(out=at_view[b], in_=aT_slab[:])
        sp_toks[bkt] = p.step("scalar", em_spill,
                              waits=[ae_toks[bkt][2 * NT - 1]],
                              inc="dX", inc_by=16)

    p2b_end = [sp_toks[NBLK - 1], gam_toks[NGC - 1],
               av_toks[NBLK - 1][NT - 1], ta_toks[NBLK - 1][2 * NT - 1],
               ae_toks[NBLK - 1][2 * NT - 1],
               ex_toks[NBLK - 1][NHF * GB - 1]] + p.tails("dX", "dY", "dG")

    p2b.close()
    p3 = contextlib.ExitStack()
    wraw = sbuf("wraw", [128, 3, DT, 128], F32, p3)
    sa1 = sbuf("sa1", [128, DT, 128], BF16, p3)
    sb1 = sbuf("sb1", [128, DT, 128], BF16, p3)
    wq1 = sbuf("wq1", [128, 2, DT, 128], BF16, p3)
    wq2 = sbuf("wq2", [128, 2, DT, 128], BF16, p3)
    wq3 = sbuf("wq3", [128, 2, SFT, 128], BF16, p3)
    sigb = sbuf("sigb", [128, 2, S], BF16, p3)
    hid = sbuf("hid", [128, SFT, S], BF16, p3)
    o3st = sbuf("o3st", [128, 2, S], BF16, p3)
    pth_sc = contextlib.ExitStack()
    pth = psum("pth", [128, 4], F32, pth_sc)

    def em_gred(v):
        v.tensor_reduce(gam_acc[:, 0:1], gam_cols[:, 0:r1], axis=AX.X,
                        op=AL.add)
        v.drain()
        v.tensor_reduce(gam_acc[:, 1:2], gam_cols[:, r1:r2], axis=AX.X,
                        op=AL.add)
        v.drain()
        v.tensor_reduce(gam_acc[:, 2:3], gam_cols[:, r2:r2 + (NGC - r2)],
                        axis=AX.X, op=AL.add)
        v.drain()
        return v.nop()
    t_gr = p.step("vector", em_gred, waits=[gam_toks[NGC - 1]], inc="v")

    def em_gpr(gp):
        gp.tensor_reduce(gam_sc[:], gam_acc[:], axis=AX.C, op=AL.add)
        gp.drain()
        return gp.nop()
    t_gp = p.step("gpsimd", em_gpr, waits=[t_gr], inc="gp")

    def em_thr(v):
        v.tensor_scalar(thr_sc[:], gam_sc[:], 0.5 / n_w_elems, 0.5e-8,
                        op0=AL.mult, op1=AL.add)
        v.drain()
        return v.nop()
    t_th = p.step("vector", em_thr, waits=[t_gp], inc="v")

    t_tm = p.step("tensor",
                  lambda t: t.matmul(pth[:, 0:3], ones_row[:], thr_sc[:],
                                     start=True, stop=True),
                  waits=[t_th] + p2b_end, inc="t")

    def em_thev(v):
        v.tensor_copy(thr[:], pth[:, 0:3])
        v.drain()
        v.tensor_scalar(thrn[:], pth[:, 0:3], -1.0, None, op0=AL.mult)
        v.drain()
        return v.nop()
    t_te = p.step("vector", em_thev, waits=[t_tm], inc="v")
    pth_sc.close()
    pf1 = psum("pf1", [128, S], F32, p3)
    pf2 = psum("pf2", [128, S], F32, p3)


    slab_seq = []
    for sti in range(NST):
        for fi in range(SFT):
            f = sti * SFT + fi
            slab_seq.append((0, f))
            slab_seq.append((1, f))
        for d in range(DT):
            slab_seq.append((2, (sti, d)))
    NSLAB = len(slab_seq)
    sl_load = [None] * NSLAB
    sl_q = [None] * NSLAB
    sl_a = [None] * NSLAB
    slab_of = {}
    for i, ka in enumerate(slab_seq):
        slab_of[ka] = i

    f1_toks = [None] * FT
    f2_toks = [None] * FT
    sg_toks = [None] * FT
    hd_toks = [None] * FT
    f3_toks = [None] * (NST * DT)
    o3_toks = [None] * (NST * DT)
    od_toks = [None] * (NST * DT)

    def reg_slab(i):
        kind, arg = slab_seq[i]
        if kind == 0:
            src = w1s_d[arg]
            nk, bias_i = DT, 0
        elif kind == 1:
            src = w2s_d[arg]
            nk, bias_i = DT, 1
        else:
            sti, d = arg
            src = w3s_d[sti, d]
            nk, bias_i = SFT, 2

        def em_l(sy, src=src, i=i, nk=nk):
            return sy.dma_start(out=wraw[:, i % 3, :nk, :], in_=src)
        w = list(p2b_end) if i < 3 else [sl_q[i - 3]]
        sl_load[i] = p.step("sync", em_l, waits=w, inc="dY", inc_by=16)

        def em_q(sc, i=i, nk=nk, bias_i=bias_i):
            sc.activation(sa1[:, :nk, :], wraw[:, i % 3, :nk, :], AF.Sign,
                          bias=thr[:, bias_i:bias_i + 1], scale=1.0)
            sc.drain()
            sc.activation(sb1[:, :nk, :], wraw[:, i % 3, :nk, :], AF.Sign,
                          bias=thrn[:, bias_i:bias_i + 1], scale=1.0)
            sc.drain()
            return sc.nop()
        w = [sl_load[i], t_te]
        if i >= 1:
            w.append(sl_a[i - 1])
        sl_q[i] = p.step("scalar", em_q, waits=w, inc="sc")

        if kind == 0:
            dst, ridx, nk2 = wq1, arg % 2, DT
        elif kind == 1:
            dst, ridx, nk2 = wq2, arg % 2, DT
        else:
            dst, ridx, nk2 = wq3, arg[1] % 2, SFT

        def em_a(v, dst=dst, ridx=ridx, nk2=nk2):
            v.tensor_tensor(dst[:, ridx, :nk2, :], sa1[:, :nk2, :],
                            sb1[:, :nk2, :], op=AL.add)
            v.drain()
            return v.nop()
        w = [sl_q[i]]
        if kind == 0 and arg >= 2:
            w.append(f1_toks[arg - 2])
        if kind == 1 and arg >= 2:
            w.append(f2_toks[arg - 2])
        if kind == 2:
            u3 = arg[0] * DT + arg[1]
            if u3 >= 2:
                w.append(f3_toks[u3 - 2])
        sl_a[i] = p.step("vector", em_a, waits=w, inc="v")

    def reg_slab_upto(i):
        while reg_ptr[0] <= min(i, NSLAB - 1):
            reg_slab(reg_ptr[0])
            reg_ptr[0] += 1

    reg_ptr = [0]
    reg_slab_upto(1)
    for sti in range(NST):
        for fi in range(SFT):
            f = sti * SFT + fi
            if fi + 1 < SFT:
                reg_slab_upto(slab_of[(1, f + 1)])
            else:
                reg_slab_upto(slab_of[(2, (sti, 0))])

            def em_f1(t, f=f):
                last = None
                for kt in range(DT):
                    for ch in range(NCH):
                        last = t.matmul(pf1[:, 512 * ch:512 * (ch + 1)],
                                        wq1[:, f % 2, kt, :],
                                        xnT[:, kt, 512 * ch:512 * (ch + 1)],
                                        start=(kt == 0), stop=(kt == DT - 1))
                return last
            w = [sl_a[slab_of[(0, f)]]]
            if f >= 1:
                w.append(sg_toks[f - 1])
            if fi == 0 and sti >= 1:
                w.append(o3_toks[sti * DT - 1])
                w.append(o3_toks[sti * DT - 2])
            f1_toks[f] = p.step("tensor", em_f1, waits=w, inc="t")

            def em_sg(sc, f=f):
                return sc.activation(sigb[:, f % 2, :], pf1[:], AF.Sigmoid,
                                     scale=0.5)
            w = [f1_toks[f]]
            if f >= 2:
                w.append(hd_toks[f - 2])
            sg_toks[f] = p.step("scalar", em_sg, waits=w, inc="sc")

            def em_f2(t, f=f):
                last = None
                for kt in range(DT):
                    for ch in range(NCH):
                        last = t.matmul(pf2[:, 512 * ch:512 * (ch + 1)],
                                        wq2[:, f % 2, kt, :],
                                        xnT[:, kt, 512 * ch:512 * (ch + 1)],
                                        start=(kt == 0), stop=(kt == DT - 1))
                return last
            w = [sl_a[slab_of[(1, f)]]]
            if f >= 1:
                w.append(hd_toks[f - 1])
            if fi == 0 and sti >= 1:
                w.append(o3_toks[sti * DT - 1])
            f2_toks[f] = p.step("tensor", em_f2, waits=w, inc="t")

            def em_hd(v, f=f, fi=fi):
                v.scalar_tensor_tensor(hid[:, fi, :], pf2[:], 0.5,
                                       sigb[:, f % 2, :],
                                       op0=AL.mult, op1=AL.mult)
                v.drain()
                return v.nop()
            hd_toks[f] = p.step("vector", em_hd,
                                waits=[f2_toks[f], sg_toks[f]], inc="v")

        for d in range(DT):
            u = sti * DT + d
            if d + 1 < DT:
                reg_slab_upto(slab_of[(2, (sti, d + 1))])
            elif sti + 1 < NST:
                reg_slab_upto(slab_of[(1, (sti + 1) * SFT)])

            def em_f3(t, sti=sti, d=d):
                pp = pf1 if d % 2 == 0 else pf2
                last = None
                for ki in range(SFT):
                    for ch in range(NCH):
                        last = t.matmul(pp[:, 512 * ch:512 * (ch + 1)],
                                        wq3[:, d % 2, ki, :],
                                        hid[:, ki, 512 * ch:512 * (ch + 1)],
                                        start=(ki == 0),
                                        stop=(ki == SFT - 1))
                return last
            w = [sl_a[slab_of[(2, (sti, d))]], hd_toks[sti * SFT + SFT - 1]]
            if d <= 1:
                w.append(sg_toks[sti * SFT + SFT - 1])
            if u >= 2:
                w.append(o3_toks[u - 2])
            f3_toks[u] = p.step("tensor", em_f3, waits=w, inc="t")

            def em_o3(v, d=d, u=u):
                v.tensor_copy(o3st[:, u % 2, :],
                              (pf1 if d % 2 == 0 else pf2)[:])
                v.drain()
                return v.nop()
            w = [f3_toks[u]]
            if u >= 2:
                w.append(od_toks[u - 2])
            o3_toks[u] = p.step("vector", em_o3, waits=w, inc="v")

            def em_od(gp, sti=sti, d=d, u=u):
                kw = {} if sti == 0 else {"accum_op": AL.add}
                return gp.dma_start(out=o3_scr[128 * d:128 * (d + 1), :],
                                    in_=o3st[:, u % 2, :], **kw)
            od_toks[u] = p.step("gpsimd", em_od, waits=[o3_toks[u]],
                                inc="dG", inc_by=16)

    p3_end = [od_toks[NST * DT - 1], f3_toks[NST * DT - 1],
              o3_toks[NST * DT - 1], sl_q[NSLAB - 1],
              sl_a[NSLAB - 1], f2_toks[FT - 1]] + p.tails("dX", "dY", "dG")

    p3.close()
    xnT_sc.close()
    p4 = contextlib.ExitStack()
    attnF = sbuf("attnF", [128, DT, S], BF16, p4)
    woraw = sbuf("woraw", [128, 3, DT, 128], F32, p4)
    wob = sbuf("wob", [128, 3, DT, 128], BF16, p4)
    xTt = sbuf("xTt", [128, 4, S], F32, p4)
    o3ld = sbuf("o3ld", [128, 4, S], BF16, p4)
    outA = sbuf("outA", [128, 2, S], F32, p4)
    outB = sbuf("outB", [128, 2, S], F32, p4)
    pwo = [psum("pwo%d" % i, [128, S], F32, p4) for i in range(2)]

    atv = at_scr.rearrange("(kt p) s -> p kt s", p=128)
    o3v = o3_scr.rearrange("(mt p) s -> p mt s", p=128)
    xTv = xT_d.rearrange("(mt p) s -> p mt s", p=128)

    at_toks = []
    for kt in range(DT):
        def em_atl(sc, kt=kt):
            return sc.dma_start(out=attnF[:, kt, :], in_=atv[:, kt, :])
        w = (list(p3_end) + [sp_toks[NBLK - 1]]) if kt == 0 else []
        at_toks.append(p.step("scalar", em_atl, waits=w, inc="dX", inc_by=16))
    tok_atF = at_toks[-1]

    wol_toks = [None] * DT
    woc_toks = [None] * DT
    wo_toks = [None] * DT
    xl_toks = [None] * DT
    ol_toks = [None] * DT
    fA_toks = [None] * DT
    fB_toks = [None] * DT
    out_toks = [None] * DT

    for mt in range(DT):
        def em_wol(sc, mt=mt):
            return sc.dma_start(out=woraw[:, mt % 3, :, :], in_=wos_d[mt])
        w = list(p3_end) if mt == 0 else []
        if mt >= 3:
            w.append(woc_toks[mt - 3])
        wol_toks[mt] = p.step("scalar", em_wol, waits=w, inc="dX", inc_by=16)

        def em_woc(v, mt=mt):
            v.tensor_copy(wob[:, mt % 3, :, :], woraw[:, mt % 3, :, :])
            v.drain()
            return v.nop()
        w = [wol_toks[mt]] + (list(p3_end) if mt == 0 else [])
        if mt >= 3:
            w.append(wo_toks[mt - 3])
        woc_toks[mt] = p.step("vector", em_woc, waits=w, inc="v")

        def em_xl(sy, mt=mt):
            return sy.dma_start(out=xTt[:, mt % 4, :], in_=xTv[:, mt, :])
        w = [fA_toks[mt - 4]] if mt >= 4 else list(p3_end)
        xl_toks[mt] = p.step("sync", em_xl, waits=w, inc="dY", inc_by=16)

        def em_ol(sy, mt=mt):
            return sy.dma_start(out=o3ld[:, mt % 4, :], in_=o3v[:, mt, :])
        w = [od_toks[(NST - 1) * DT + mt]]
        if mt >= 4:
            w.append(fB_toks[mt - 4])
        else:
            w += list(p3_end)
        ol_toks[mt] = p.step("sync", em_ol, waits=w, inc="dY", inc_by=16)

        def em_wo(t, mt=mt):
            last = None
            for kt in range(DT):
                for ch in range(NCH):
                    last = t.matmul(pwo[mt % 2][:, 512 * ch:512 * (ch + 1)],
                                    wob[:, mt % 2, kt, :],
                                    attnF[:, kt, 512 * ch:512 * (ch + 1)],
                                    start=(kt == 0), stop=(kt == DT - 1))
            return last
        w = [woc_toks[mt]] + at_toks
        if mt >= 2:
            w.append(fA_toks[mt - 2])
        wo_toks[mt] = p.step("tensor", em_wo, waits=w, inc="t")

        def em_fA(v, mt=mt):
            v.scalar_tensor_tensor(outA[:, mt % 2, :], pwo[mt % 2][:], 1.0,
                                   xTt[:, mt % 4, :], op0=AL.mult, op1=AL.add)
            v.drain()
            return v.nop()
        w = [wo_toks[mt], xl_toks[mt]]
        if mt >= 2:
            w.append(fB_toks[mt - 2])
        fA_toks[mt] = p.step("vector", em_fA, waits=w, inc="v")

        def em_fB(v, mt=mt):
            v.scalar_tensor_tensor(outB[:, mt % 2, :], o3ld[:, mt % 4, :],
                                   0.5, outA[:, mt % 2, :],
                                   op0=AL.mult, op1=AL.add)
            v.drain()
            return v.nop()
        w = [fA_toks[mt], ol_toks[mt]]
        if mt >= 2:
            w.append(out_toks[mt - 2])
        fB_toks[mt] = p.step("vector", em_fB, waits=w, inc="v")

        def em_out(sc, mt=mt):
            return sc.dma_start(out=out_d[128 * mt:128 * (mt + 1), :],
                                in_=outB[:, mt % 2, :])
        out_toks[mt] = p.step("scalar", em_out, waits=[fB_toks[mt]],
                              inc="dX", inc_by=16)

    p.step("scalar", lambda sc: None, waits=[out_toks[DT - 1]])
    p.step("sync", lambda sy: None, waits=[sl_load[NSLAB - 1]])
    p.step("gpsimd", lambda gp: None, waits=[od_toks[NST * DT - 1]])

    with nc.Block() as blk:
        p.emit(blk)
    p4.close()

    return nc


_CACHED = {}


def _get_nc():
    if "nc" not in _CACHED:
        _CACHED["nc"] = build_program()
    return _CACHED["nc"]


def kernel(L0_attn_x, L0_attn_Wq, L0_attn_Wk, L0_attn_Wv, L0_attn_Wo,
           L0_ffn_w1, L0_ffn_w2, L0_ffn_w3):
    B = L0_attn_x.shape[0]
    nc = _get_nc()

    def C(a):
        return np.ascontiguousarray(np.asarray(a), dtype=np.float32)

    D = L0_attn_x.shape[2]
    S = L0_attn_x.shape[1]
    H = L0_attn_Wq.shape[0]
    F = L0_ffn_w1.shape[0]
    DT, FT, NVE = D // 128, F // 128, D // 512
    NST = 4 if FT % 4 == 0 else (2 if FT % 2 == 0 else 1)
    SFT = FT // NST
    wqs = C(np.asarray(L0_attn_Wq).T.reshape(DT, 128, H).transpose(1, 0, 2))
    wks = C(np.asarray(L0_attn_Wk).T.reshape(DT, 128, H).transpose(1, 0, 2))
    wvs = C(np.asarray(L0_attn_Wv).T.reshape(DT, 128, NVE, 512)
            .transpose(2, 1, 0, 3))
    wos = C(np.asarray(L0_attn_Wo).T.reshape(DT, 128, DT, 128)
            .transpose(2, 1, 0, 3))
    w1s = C(np.asarray(L0_ffn_w1).T.reshape(DT, 128, FT, 128)
            .transpose(2, 1, 0, 3))
    w2s = C(np.asarray(L0_ffn_w2).T.reshape(DT, 128, FT, 128)
            .transpose(2, 1, 0, 3))
    w3s = C(np.asarray(L0_ffn_w3).T.reshape(NST, SFT, 128, DT, 128)
            .transpose(0, 3, 2, 1, 4))

    in_maps = []
    for b in range(B):
        xb = C(L0_attn_x[b])
        in_maps.append({
            "x_tok": xb, "xT": C(xb.T),
            "wqs": wqs, "wks": wks, "wvs": wvs, "wos": wos,
            "w1s": w1s, "w2s": w2s, "w3s": w3s,
        })

    res = run_bass_kernel_spmd(nc, in_maps, core_ids=list(range(B)))
    out = np.empty((B, L0_attn_x.shape[1], L0_attn_x.shape[2]),
                   dtype=np.float32)
    for b in range(B):
        out[b] = np.asarray(res.results[b]["out"]).T
    return out


# version 19
# speedup vs baseline: 1.1344x; 1.0632x over previous
import numpy as np
import contextlib
import concourse.bass as bass
import concourse.mybir as mybir
from concourse.bass_utils import run_bass_kernel_spmd
from concourse import library_config

F32 = mybir.dt.float32
BF16 = mybir.dt.bfloat16
U16 = mybir.dt.uint16
I16 = mybir.dt.int16
AL = mybir.AluOpType
AF = mybir.ActivationFunctionType
AX = mybir.AxisListType

BIGC = float(2 ** 23)
NEG_BIG = -1e30
ENGINES = ("sync", "scalar", "vector", "tensor", "gpsimd")


class Prog:

    def __init__(self, nc):
        self.nc = nc
        self.steps = {e: [] for e in ENGINES}
        self.sems = {}
        self.counts = {}
        self.last_dma = {}
        self._es = contextlib.ExitStack()

    def sem(self, name):
        if name not in self.sems:
            self.sems[name] = self._es.enter_context(self.nc.semaphore(name))
            self.counts[name] = 0
        return self.sems[name]

    NSUB = 6

    def step(self, engine, emit, waits=(), inc=None, inc_by=1):
        tok = None
        waits = [w for w in waits if w is not None]
        if inc is not None:
            if inc_by == 16:
                jname = inc + "#j"
                j = self.counts.get(jname, 0)
                self.counts[jname] = j + 1
                sub = "%s%d" % (inc, j % self.NSUB)
                self.sem(sub)
                self.counts[sub] += 16
                if self.counts[sub] > 16:
                    waits.append((sub, self.counts[sub] - 16))
                tok = (sub, self.counts[sub], 16)
                self.last_dma.setdefault(inc, {})[sub] = tok
            else:
                self.sem(inc)
                self.counts[inc] += inc_by
                tok = (inc, self.counts[inc], inc_by)
        self.steps[engine].append((tuple(waits), emit, tok))
        return tok

    def tails(self, *queues):
        out = []
        for q in queues:
            out.extend(self.last_dma.get(q, {}).values())
        return out

    def emit(self, block):
        for engine in ENGINES:
            steps = self.steps[engine]
            if not steps:
                continue

            def body(eng, engine_steps=steps):
                waited = {}
                for waits, emit_fn, tok in engine_steps:
                    for w in waits:
                        sname, val = w[0], w[1]
                        if waited.get(sname, -1) < val:
                            eng.wait_ge(self.sems[sname], val)
                            waited[sname] = val
                    last = emit_fn(eng)
                    if tok is not None:
                        if last is None:
                            last = eng.nop()
                        last.then_inc(self.sems[tok[0]], tok[2])

            getattr(block, engine)(body)


def build_program(S=2048, F=5632):
    D, H = 2048, 128
    NT = S // 128
    DT = D // 128
    FT = F // 128
    NCH = S // 512
    G = H // 2
    GB = 8
    NBLK = G // GB
    NST = 4 if FT % 4 == 0 else (2 if FT % 2 == 0 else 1)
    SFT = FT // NST
    n_w_elems = float(F * D)

    nc = bass.Bass("TRN2", target_bir_lowering=False)
    p = Prog(nc)

    NVE_ = D // 512
    NST_ = 4 if FT % 4 == 0 else (2 if FT % 2 == 0 else 1)
    SFT_ = FT // NST_
    x_tok = nc.dram_tensor("x_tok", [S, D], F32, kind="ExternalInput").ap()
    xT_d = nc.dram_tensor("xT", [D, S], F32, kind="ExternalInput").ap()
    wqs_d = nc.dram_tensor("wqs", [128, DT, H], F32, kind="ExternalInput").ap()
    wks_d = nc.dram_tensor("wks", [128, DT, H], F32, kind="ExternalInput").ap()
    wvs_d = nc.dram_tensor("wvs", [NVE_, 128, DT, 512], F32,
                           kind="ExternalInput").ap()
    wos_d = nc.dram_tensor("wos", [DT, 128, DT, 128], F32,
                           kind="ExternalInput").ap()
    w1s_d = nc.dram_tensor("w1s", [FT, 128, DT, 128], F32,
                           kind="ExternalInput").ap()
    w2s_d = nc.dram_tensor("w2s", [FT, 128, DT, 128], F32,
                           kind="ExternalInput").ap()
    w3s_d = nc.dram_tensor("w3s", [NST_, DT, 128, SFT_, 128], F32,
                           kind="ExternalInput").ap()
    out_d = nc.dram_tensor("out", [D, S], F32, kind="ExternalOutput").ap()
    at_scr = nc.dram_tensor("at_scr", [D, S], BF16, kind="Internal").ap()
    o3_scr = nc.dram_tensor("o3_scr", [D, S], BF16, kind="Internal").ap()
    v_scr = nc.dram_tensor("v_scr", [S, D], BF16, kind="Internal").ap()

    es = contextlib.ExitStack()

    def sbuf(name, shape, dtype, stack=None, side=None):
        kw = {} if side is None else {"side": side}
        return (stack or es).enter_context(
            nc.sbuf_tensor(name, shape, dtype, **kw))

    def psum(name, shape, dtype=F32, stack=None):
        return (stack or es).enter_context(
            nc.psum_tensor(name, shape, dtype))

    xnT_sc = contextlib.ExitStack()
    xnT = sbuf("xnT", [128, DT, S], BF16, xnT_sc, side="right")
    ident = sbuf("ident", [128, 128], BF16)
    iota_r = sbuf("iota_r", [128, 128], F32)
    iota_g = sbuf("iota_g", [128, G], F32)
    iota_rp = sbuf("iota_rp", [128, 128], F32)
    hcol = sbuf("hcol", [128, 1], F32)
    hcol_i = sbuf("hcol_i", [128, 1], mybir.dt.int32)
    hbit_i = sbuf("hbit_i", [128, 1], mybir.dt.int32)
    hm2 = sbuf("hm2", [128, 1], F32)
    h2col = sbuf("h2col", [128, 1], F32)
    rlt64 = sbuf("rlt64", [128, 1], F32)
    nt0 = sbuf("nt0", [128, 1], F32)
    biascol = sbuf("biascol", [128, 1], F32)
    Gm = sbuf("Gm", [128, G], BF16)
    Pm = sbuf("Pm", [128, 128], BF16)
    csr = sbuf("csr", [128, 128], F32)
    KV2 = sbuf("KV2", [128, 128], BF16)
    CP2 = sbuf("CP2", [128, 128], BF16)
    gam_cols = sbuf("gam_cols", [128, 192], F32)
    gam_acc = sbuf("gam_acc", [128, 3], F32)
    gam_sc = sbuf("gam_sc", [1, 3], F32)
    thr_sc = sbuf("thr_sc", [1, 3], F32)
    ones_row = sbuf("ones_row", [1, 128], F32)
    thr = sbuf("thr", [128, 3], F32)
    thrn = sbuf("thrn", [128, 3], F32)
    qb = sbuf("qb", [128, S], BF16)
    gq = sbuf("gq", [128, S], BF16)
    kvals = sbuf("kvals", [128, 64], F32)
    Vsel = sbuf("Vsel", [128, G, 34], BF16)
    idxb0 = sbuf("idxb0", [64, G], mybir.dt.uint32)
    idxb1 = sbuf("idxb1", [64, G], mybir.dt.uint32)

    gsc = contextlib.ExitStack()
    GCW = 4096
    gslab = sbuf("gslab", [128, 2, GCW], F32, gsc)
    gdum = sbuf("gdum", [128, GCW], BF16, gsc)

    gchunks = []
    nper = []
    for wd in (w1s_d, w2s_d, w3s_d):
        flat = wd.rearrange("a b c d -> (a b c d)") if len(wd.shape) == 4 \
            else wd.rearrange("a b c d e -> (a b c d e)")
        nch_ = flat.shape[0] // (128 * GCW)
        assert nch_ * 128 * GCW == flat.shape[0]
        chv = flat.rearrange("(n p q) -> n p q", p=128, q=GCW)
        for i_ in range(nch_):
            gchunks.append((chv[i_], GCW))
        nper.append(nch_)
    NGC = len(gchunks)
    assert NGC <= 192
    r1 = nper[0]
    r2 = nper[0] + nper[1]

    gam_toks = [None] * NGC
    gslab_toks = [None] * NGC

    def reg_gamma(i):
        ap, nel = gchunks[i]

        def em_dma(sy, ap=ap, nel=nel, i=i):
            return sy.dma_start(out=gslab[:, i % 2, :nel], in_=ap)
        w = [gam_toks[i - 2]] if i >= 2 else []
        gslab_toks[i] = p.step("sync", em_dma, waits=w, inc="dY", inc_by=16)

        def em_abs(sc, i=i, nel=nel):
            sc.activation(gdum[:, :nel], gslab[:, i % 2, :nel],
                          AF.Abs, accum_out=gam_cols[:, i:i + 1])
            sc.drain()
            return sc.nop()
        gam_toks[i] = p.step("scalar", em_abs, waits=[gslab_toks[i]],
                             inc="sGa")

    def em_iotas(gp):
        gp.iota(iota_r[:], pattern=[[1, 128]], base=0, channel_multiplier=0,
                allow_small_or_imprecise_dtypes=True)
        gp.iota(iota_g[:], pattern=[[1, G]], base=0, channel_multiplier=0,
                allow_small_or_imprecise_dtypes=True)
        gp.iota(iota_rp[:], pattern=[[1, 2], [0, 64]], base=0,
                channel_multiplier=0, allow_small_or_imprecise_dtypes=True)
        gp.iota(hcol[:], pattern=[[0, 1]], base=0, channel_multiplier=1,
                allow_small_or_imprecise_dtypes=True)
        gp.iota(hcol_i[:], pattern=[[0, 1]], base=0, channel_multiplier=1)
        gp.iota(csr[:], pattern=[[0, 2], [1, 2], [0, 32]], base=0,
                channel_multiplier=0, allow_small_or_imprecise_dtypes=True)
        gp.memset(gam_cols[:], 0.0)
        gp.memset(ones_row[:], 1.0)
        gp.drain()
        return gp.nop()
    tok_iota = p.step("gpsimd", em_iotas, inc="gp")

    def em_ident(v):
        v.tensor_scalar(ident[:], iota_r[:], hcol[:], None, op0=AL.is_equal)
        v.drain()
        return v.nop()
    tok_ident = p.step("vector", em_ident, waits=[tok_iota], inc="v")

    def em_masks(v):
        v.tensor_scalar(hbit_i[:], hcol_i[:], 1, None, op0=AL.bitwise_and)
        v.drain()
        v.tensor_copy(hm2[:], hbit_i[:])
        v.drain()
        v.tensor_tensor(h2col[:], hcol[:], hm2[:], op=AL.subtract)
        v.drain()
        v.tensor_scalar(h2col[:], h2col[:], 0.5, None, op0=AL.mult)
        v.drain()
        v.tensor_scalar(Gm[:], iota_g[:], h2col[:], None, op0=AL.is_equal)
        v.drain()
        v.tensor_scalar(Pm[:], iota_rp[:], hm2[:], None, op0=AL.is_equal)
        v.drain()
        v.tensor_scalar(csr[:], csr[:], -2.0 * BIGC, BIGC,
                        op0=AL.mult, op1=AL.add)
        v.drain()
        v.tensor_tensor(CP2[:], csr[:], Pm[:], op=AL.mult)
        v.drain()
        v.tensor_scalar(rlt64[:], hcol[:], 64.0, None, op0=AL.is_lt)
        v.drain()
        v.tensor_scalar(hbit_i[:], hcol_i[:], 32, None, op0=AL.bitwise_and)
        v.drain()
        v.tensor_copy(nt0[:], hbit_i[:])
        v.drain()
        v.tensor_scalar(nt0[:], nt0[:], 0.0, None, op0=AL.is_equal)
        v.drain()
        v.tensor_scalar(biascol[:], nt0[:], -BIGC, None, op0=AL.mult)
        v.drain()
        v.memset(Vsel[:], 0.0)
        v.drain()
        v.tensor_scalar(Vsel[:, :, 32:33],
                        rlt64[:, None, :].broadcast_to((128, G, 1)),
                        1.0, None, op0=AL.mult)
        v.drain()
        v.tensor_scalar(Vsel[:, :, 33:34],
                        rlt64[:, None, :].broadcast_to((128, G, 1)),
                        -1.0, 1.0, op0=AL.mult, op1=AL.add)
        v.drain()
        return v.nop()
    tok_masks = p.step("vector", em_masks, inc="v")

    p1 = contextlib.ExitStack()
    XR = 3
    x_tile = sbuf("x_tile", [128, XR, D], F32, p1)
    xn_stage = sbuf("xn_stage", [128, XR, D], BF16, p1)
    stats = sbuf("stats", [128, XR, 24], F32, p1)
    mv = sbuf("mv", [128, XR, 2], F32, p1)
    veps = sbuf("veps", [128, XR, 1], F32, p1)
    rs0 = sbuf("rs0", [128, XR, 1], F32, p1)
    rs1 = sbuf("rs1", [128, XR, 1], F32, p1)
    nt1 = sbuf("nt1", [128, XR, 1], F32, p1)
    pxn = [psum("pxn%d" % i, [128, DT, 128], BF16, p1) for i in range(2)]

    gamma_ptr = [0]
    xn_toks = [None] * NT
    xload_toks = [None] * NT
    xpose_toks = [[None] * DT for _ in range(NT)]
    xev_toks = [[None] * DT for _ in range(NT)]
    sqrt_toks = [None] * NT

    for st in range(NT):
        b = st % XR

        def em_xload(gp, st=st, b=b):
            return gp.dma_start(out=x_tile[:, b, :],
                                in_=x_tok[128 * st:128 * (st + 1), :])
        w = [xn_toks[st - XR]] if st >= XR else []
        xload_toks[st] = p.step("gpsimd", em_xload, waits=w, inc="dG",
                                inc_by=16)

        def em_stats(v, b=b):
            for c in range(4):
                v.bn_stats(stats[:, b, 6 * c:6 * c + 6],
                           x_tile[:, b, 512 * c:512 * (c + 1)])
            v.drain()
            v.bn_aggr(mv[:, b, :], stats[:, b, :])
            v.drain()
            v.tensor_scalar(veps[:, b, :], mv[:, b, 1:2], 1e-5, None,
                            op0=AL.add)
            v.drain()
            return v.nop()
        t_stats = p.step("vector", em_stats, waits=[xload_toks[st]], inc="v")

        def em_sqrt(sc, b=b):
            sc.activation(nt1[:, b, :], veps[:, b, :], AF.Ln)
            sc.drain()
            sc.activation(rs1[:, b, :], nt1[:, b, :], AF.Exp, scale=-0.5)
            sc.drain()
            return sc.nop()
        sqrt_toks[st] = p.step("scalar", em_sqrt, waits=[t_stats], inc="sc")

        def em_xn(v, b=b):
            v.tensor_scalar(xn_stage[:, b, :], x_tile[:, b, :],
                            mv[:, b, 0:1], rs1[:, b, :],
                            op0=AL.subtract, op1=AL.mult)
            v.drain()
            return v.nop()
        xn_toks[st] = p.step("vector", em_xn, waits=[sqrt_toks[st]], inc="v")

        def em_tp(t, st=st, b=b):
            last = None
            for dt in range(DT):
                last = t.transpose(pxn[st % 2][:, dt, :],
                                   xn_stage[:, b, 128 * dt:128 * (dt + 1)],
                                   ident[:])
            return last
        w = [xn_toks[st], tok_ident]
        if st >= 2:
            w.append(xev_toks[st - 2][0])
        xpose_toks[st][0] = p.step("tensor", em_tp, waits=w, inc="t")

        def em_ev(v, st=st):
            v.tensor_copy(xnT[:, :, 128 * st:128 * (st + 1)], pxn[st % 2][:])
            v.drain()
            return v.nop()
        xev_toks[st][0] = p.step("vector", em_ev, waits=[xpose_toks[st][0]],
                                 inc="v")
        while gamma_ptr[0] < min(NGC, 2 * (st + 1)):
            reg_gamma(gamma_ptr[0])
            gamma_ptr[0] += 1

    tok_xnT = xev_toks[NT - 1][0]
    p1_end = [tok_xnT, xpose_toks[NT - 1][0], sqrt_toks[NT - 1],
              xn_toks[NT - 1]]


    p1.close()
    p2a = contextlib.ExitStack()
    U32 = mybir.dt.uint32
    wvs2 = sbuf("wvs2", [128, 2, DT, 512], BF16, p2a)
    vtile = sbuf("vtile", [128, 4, 512], BF16, p2a)
    keysw = sbuf("keysw", [128, 3, S], F32, p2a)
    wqTb = sbuf("wqTb", [128, DT, H], BF16, p2a)
    wkTb = sbuf("wkTb", [128, DT, H], BF16, p2a)
    idx = sbuf("idx", [128, 64], U16, p2a)
    idxcf = sbuf("idxcf", [128, 64], F32, p2a)

    kvn = sbuf("kvn", [128, 32], F32, p2a)
    identF = sbuf("identF", [128, 128], F32, p2a)
    pqk = [psum("pqk%d" % i, [128, 512], F32, p2a) for i in range(4)]
    pidx = psum("pidx", [64, 128], F32, p2a)

    vflat = v_scr.rearrange("s (hh c) -> (s hh) c", c=16)

    t_idf = p.step("vector",
                   lambda v: v.tensor_scalar(identF[:], iota_r[:], hcol[:],
                                             None, op0=AL.is_equal),
                   waits=[tok_iota], inc="v")

    t_cq = p.step("gpsimd",
                  lambda gp: gp.dma_start(out=wqTb[:], in_=wqs_d[:]),
                  waits=p1_end, inc="dG", inc_by=16)
    t_ck = p.step("gpsimd",
                  lambda gp: gp.dma_start(out=wkTb[:], in_=wks_d[:]),
                  waits=p1_end, inc="dG", inc_by=16)

    qk_mm = [None] * (2 * NCH)
    qk_ev = [None] * (2 * NCH)
    for wi in range(2):
        for ch in range(NCH):
            u = wi * NCH + ch

            def em_mm(t, wi=wi, ch=ch, u=u):
                wt = wqTb if wi == 0 else wkTb
                last = None
                for kt in range(DT):
                    last = t.matmul(pqk[u % 4][:], wt[:, kt, :],
                                    xnT[:, kt, 512 * ch:512 * (ch + 1)],
                                    start=(kt == 0), stop=(kt == DT - 1))
                return last
            w = [t_cq if wi == 0 else t_ck, tok_xnT]
            if u >= 4:
                w.append(qk_ev[u - 4])
            qk_mm[u] = p.step("tensor", em_mm, waits=w, inc="t")

            if wi == 0:
                def em_ev(v, ch=ch, u=u):
                    v.tensor_copy(qb[:, 512 * ch:512 * (ch + 1)],
                                  pqk[u % 4][:])
                    v.drain()
                    v.tensor_scalar(gq[:, 512 * ch:512 * (ch + 1)],
                                    pqk[u % 4][:], 0.0, None,
                                    op0=AL.is_gt)
                    v.drain()
                    return v.nop()
            else:
                def em_ev(v, ch=ch, u=u):
                    v.tensor_copy(keysw[:, 0, 512 * ch:512 * (ch + 1)],
                                  pqk[u % 4][:])
                    v.drain()
                    return v.nop()
            qk_ev[u] = p.step("vector", em_ev, waits=[qk_mm[u]], inc="v")

    NQK = 2 * NCH
    NVE = D // 512
    vload_toks = [None] * NVE
    vcast_toks = [None] * NVE
    vmm_toks = [[None] * NT for _ in range(NVE)]
    vev_toks = [[None] * NT for _ in range(NVE)]
    vsp_toks = [[None] * NT for _ in range(NVE)]

    for ech in range(NVE):
        def em_vload(gp, ech=ech):
            return gp.dma_start(out=wvs2[:, ech % 2, :, :], in_=wvs_d[ech])
        w = []
        if ech >= 2:
            w.append(vmm_toks[ech - 2][NT - 1])
        vcast_toks[ech] = vload_toks[ech] = p.step(
            "gpsimd", em_vload, waits=w, inc="dG", inc_by=16)

        for mt in range(NT):
            uu = NQK + ech * NT + mt

            def em_vmm(t, ech=ech, mt=mt, uu=uu):
                last = None
                for kt in range(DT):
                    last = t.matmul(pqk[uu % 4][:],
                                    xnT[:, kt, 128 * mt:128 * (mt + 1)],
                                    wvs2[:, ech % 2, kt, :],
                                    start=(kt == 0), stop=(kt == DT - 1))
                return last
            w = [vcast_toks[ech]]
            pu = uu - 4
            if pu >= NQK:
                pe, pm = divmod(pu - NQK, NT)
                w.append(vev_toks[pe][pm])
            else:
                w.append(qk_ev[pu])
            vmm_toks[ech][mt] = p.step("tensor", em_vmm, waits=w, inc="t")

            def em_vev(v, ech=ech, mt=mt, uu=uu):
                v.tensor_copy(vtile[:, uu % 4, :], pqk[uu % 4][:])
                v.drain()
                return v.nop()
            w = [vmm_toks[ech][mt]]
            if uu - NQK >= 4:
                pe, pm = divmod(uu - NQK - 4, NT)
                w.append(vsp_toks[pe][pm])
            vev_toks[ech][mt] = p.step("vector", em_vev, waits=w, inc="v")

            def em_vsp(gp, ech=ech, mt=mt, uu=uu):
                return gp.dma_start(
                    out=v_scr[128 * mt:128 * (mt + 1),
                              512 * ech:512 * (ech + 1)],
                    in_=vtile[:, uu % 4, :])
            vsp_toks[ech][mt] = p.step("gpsimd", em_vsp,
                                       waits=[vev_toks[ech][mt]],
                                       inc="dG", inc_by=16)
            while gamma_ptr[0] < min(NGC, 2 * (ech * NT + mt)):
                reg_gamma(gamma_ptr[0])
                gamma_ptr[0] += 1

    def em_sel(v):
        seq = [0, 1, 2, 1]
        for r in range(4):
            src = keysw[:, seq[r], :]
            v.max(kvals[:, 8 * r:8 * r + 8], src)
            v.drain()
            v.max_index(idx[:, 8 * r:8 * r + 8], kvals[:, 8 * r:8 * r + 8],
                        src)
            if r < 3:
                v.match_replace(keysw[:, seq[r + 1], :],
                                kvals[:, 8 * r:8 * r + 8], src, NEG_BIG)
            v.drain()
        v.tensor_scalar(keysw[:, 1, :], keysw[:, 0, :], -1.0, None,
                        op0=AL.mult)
        v.drain()
        seq = [1, 2, 1, 2]
        for r in range(4):
            src = keysw[:, seq[r], :]
            v.max(kvn[:, 8 * r:8 * r + 8], src)
            v.drain()
            v.max_index(idx[:, 32 + 8 * r:32 + 8 * r + 8],
                        kvn[:, 8 * r:8 * r + 8], src)
            if r < 3:
                v.match_replace(keysw[:, seq[r + 1], :],
                                kvn[:, 8 * r:8 * r + 8], src, NEG_BIG)
            v.drain()
        v.tensor_scalar(kvals[:, 32:64], kvn[:], -1.0, None, op0=AL.mult)
        v.drain()
        return v.nop()
    tok_sel = p.step("vector", em_sel, waits=[qk_ev[2 * NCH - 1]], inc="v")

    def em_kv2(v):
        v.tensor_tensor(KV2.ap().rearrange("p (two j) -> p two j", j=64),
                        kvals[:, None, :].broadcast_to((128, 2, 64)),
                        Pm.ap().rearrange("p (two j) -> p two j", j=64),
                        op=AL.mult)
        v.drain()
        return v.nop()
    tok_kv2 = p.step("vector", em_kv2, waits=[tok_sel, tok_masks], inc="v")

    def em_idxc(v):
        v.tensor_copy(idxcf[:], idx[:])
        v.drain()
        v.tensor_scalar(idxcf[:], idxcf[:], 128.0, hcol[:],
                        op0=AL.mult, op1=AL.add)
        v.drain()
        return v.nop()
    t_ic = p.step("vector", em_idxc, waits=[tok_sel], inc="v")

    t_it = p.step("tensor",
                  lambda t: t.transpose(pidx[:], idxcf[:], identF[:]),
                  waits=[t_ic, t_idf], inc="t")

    def em_ib(v):
        v.tensor_copy(idxb0[:], pidx[:, 0::2])
        v.drain()
        v.tensor_copy(idxb1[:], pidx[:, 1::2])
        v.drain()
        return v.nop()
    t_ib = p.step("vector", em_ib, waits=[t_it], inc="v")

    sp_all = [vsp_toks[e][m] for e in range(NVE) for m in range(NT)]
    gv_toks = [None] * (2 * G)
    for g in range(G):
        def em_g0(gp, g=g):
            return gp.indirect_dma_start(
                out=Vsel[0:64, g, 0:16], out_offset=None, in_=vflat,
                in_offset=bass.IndirectOffsetOnAxis(ap=idxb0[:, g:g + 1],
                                                    axis=0))
        w = [t_ib, tok_masks] + (sp_all if g == 0 else [])
        gv_toks[2 * g] = p.step("gpsimd", em_g0, waits=w, inc="dG",
                                inc_by=16)

        def em_g1(gp, g=g):
            return gp.indirect_dma_start(
                out=Vsel[64:128, g, 16:32], out_offset=None, in_=vflat,
                in_offset=bass.IndirectOffsetOnAxis(ap=idxb1[:, g:g + 1],
                                                    axis=0))
        gv_toks[2 * g + 1] = p.step("gpsimd", em_g1, waits=[gv_toks[2 * g]],
                                    inc="dG", inc_by=16)
    tok_vsel = gv_toks[2 * G - 1]

    while gamma_ptr[0] < NGC:
        reg_gamma(gamma_ptr[0])
        gamma_ptr[0] += 1

    p2a_end = [tok_kv2, tok_sel, t_ck, t_it, t_ib,
               vmm_toks[NVE - 1][NT - 1], vev_toks[NVE - 1][NT - 1],
               qk_ev[2 * NCH - 1], gam_toks[NGC - 1]] \
        + p.tails("dX", "dY")

    p2a.close()
    gsc.close()
    p2b = contextlib.ExitStack()
    Esl = sbuf("Esl", [128, 2, GB, S], BF16, p2b)
    Lblk = sbuf("Lblk", [128, 2, GB, 128], BF16, p2b)
    Cblk = sbuf("Cblk", [128, 2, GB, 128], BF16, p2b)
    a_sE = sbuf("a_sE", [128, NT, 32 * GB], BF16, p2b)
    aT_slab = sbuf("aT_slab", [128, 2, NT, 128], BF16, p2b)
    rden = sbuf("rden", [128, GB, 2], F32, p2b)
    SCW = min(1024, S)
    NHF = S // SCW
    psc = [psum("psc%d" % i, [128, SCW], F32, p2b) for i in range(2)]
    pav = [psum("pav%d" % i, [128, GB, 34], F32, p2b) for i in range(2)]
    pat = [psum("pat%d" % i, [128, NT // 2, 128], BF16, p2b)
           for i in range(2)]

    lb_toks = [None] * NBLK
    sc_toks = [[None] * (NHF * GB) for _ in range(NBLK)]
    ex_toks = [[None] * (NHF * GB) for _ in range(NBLK)]
    av_toks = [[None] * NT for _ in range(NBLK)]
    dv_toks = [[None] * NT for _ in range(NBLK)]
    ta_toks = [[None] * 4 for _ in range(NBLK)]
    ae_toks = [[None] * 4 for _ in range(NBLK)]
    sp_toks = [None] * NBLK

    at_view = at_scr.rearrange("(bb eb p) (st j) -> bb p eb st j",
                               bb=NBLK, p=128, j=128)

    def reg_lb(bkt):
        def em_lb(v, b=bkt):
            gsl = slice(GB * b, GB * (b + 1))
            v.tensor_tensor(Lblk[:, b % 2, :, :],
                            KV2[:, None, :].broadcast_to((128, GB, 128)),
                            Gm[:, gsl, None].broadcast_to((128, GB, 128)),
                            op=AL.mult)
            v.drain()
            v.tensor_tensor(Cblk[:, b % 2, :, :],
                            CP2[:, None, :].broadcast_to((128, GB, 128)),
                            Gm[:, gsl, None].broadcast_to((128, GB, 128)),
                            op=AL.mult)
            v.drain()
            return v.nop()
        w = [tok_kv2] + (p2a_end if bkt == 0 else [])
        if bkt >= 2:
            w.append(sc_toks[bkt - 2][NHF * GB - 1])
        lb_toks[bkt] = p.step("vector", em_lb, waits=w, inc="v")

    reg_lb(0)
    for bkt in range(NBLK):
        if bkt + 1 < NBLK:
            reg_lb(bkt + 1)
        for gi in range(GB):
            for hf in range(NHF):
                u = (bkt * GB + gi) * NHF + hf
                uu = u % 2

                def em_sc_mm(t, b=bkt, gi=gi, hf=hf, uu=uu):
                    last = None
                    for cc in range(SCW // 512):
                        n0 = SCW * hf + 512 * cc
                        t.matmul(psc[uu][:, 512 * cc:512 * (cc + 1)],
                                 Lblk[:, b % 2, gi, :], qb[:, n0:n0 + 512],
                                 start=True, stop=False)
                        last = t.matmul(psc[uu][:, 512 * cc:512 * (cc + 1)],
                                        Cblk[:, b % 2, gi, :],
                                        gq[:, n0:n0 + 512],
                                        start=False, stop=True)
                    return last
                w = [lb_toks[bkt]]
                if u >= 2:
                    w.append(ex_toks[(u - 2) // (NHF * GB)]
                             [(u - 2) % (NHF * GB)])
                if gi == 0 and hf == 0 and bkt >= 2:
                    w.append(av_toks[bkt - 2][NT - 1])
                sc_toks[bkt][NHF * gi + hf] = p.step("tensor", em_sc_mm,
                                                     waits=w, inc="t")

                def em_exp(sc, b=bkt, gi=gi, hf=hf, uu=uu):
                    return sc.activation(
                        Esl[:, b % 2, gi, SCW * hf:SCW * (hf + 1)],
                        psc[uu][:], AF.Exp, bias=biascol[:], scale=1.0)
                ex_toks[bkt][NHF * gi + hf] = p.step(
                    "scalar", em_exp, waits=[sc_toks[bkt][NHF * gi + hf]],
                    inc="sEx")

        for st in range(NT):
            def em_av(t, b=bkt, st=st):
                last = None
                for gi in range(GB):
                    last = t.matmul(pav[st % 2][:, gi, :],
                                    Esl[:, b % 2, gi,
                                        128 * st:128 * (st + 1)],
                                    Vsel[:, b * GB + gi, :],
                                    start=True, stop=True)
                return last
            hi = 2 * (bkt + 1) * GB
            w = [ex_toks[bkt][NHF * GB - 1]] + gv_toks[max(0, hi - 6):hi]
            u = bkt * NT + st
            if u >= 2:
                pb, pst = divmod(u - 2, NT)
                w.append(dv_toks[pb][pst])
            av_toks[bkt][st] = p.step("tensor", em_av, waits=w, inc="t")

            def em_dv(v, st=st):
                v.reciprocal(rden[:], pav[st % 2][:, :, 32:34])
                v.drain()
                v.tensor_tensor(
                    a_sE[:, st, :].rearrange("p (g par c) -> p g par c",
                                             g=GB, c=16),
                    pav[st % 2][:, :, 0:32].rearrange(
                        "p g (par c) -> p g par c", c=16),
                    rden[:, :, :, None].broadcast_to((128, GB, 2, 16)),
                    op=AL.mult)
                v.drain()
                return v.nop()
            dv_toks[bkt][st] = p.step("vector", em_dv,
                                      waits=[av_toks[bkt][st]], inc="v")

        HB = NT // 2
        for hb in range(2):
            for eb in range(2):
                u2 = hb * 2 + eb

                def em_ta(t, hb=hb, eb=eb, u2=u2, b=bkt):
                    last = None
                    for si in range(HB):
                        st = hb * HB + si
                        last = t.transpose(
                            pat[u2 % 2][:, si, :],
                            a_sE[:, st, 128 * eb:128 * (eb + 1)],
                            ident[:])
                    return last
                w = [dv_toks[bkt][hb * HB + HB - 1]]
                gu = bkt * 4 + u2
                if gu >= 2:
                    pb, pu2 = divmod(gu - 2, 4)
                    w.append(ae_toks[pb][pu2])
                ta_toks[bkt][u2] = p.step("tensor", em_ta, waits=w, inc="t")

                def em_ae(v, hb=hb, eb=eb, u2=u2):
                    v.tensor_copy(
                        aT_slab[:, eb, hb * HB:(hb + 1) * HB, :],
                        pat[u2 % 2][:])
                    v.drain()
                    return v.nop()
                w = [ta_toks[bkt][u2]]
                if bkt >= 1 and u2 == 0:
                    w.append(sp_toks[bkt - 1])
                ae_toks[bkt][u2] = p.step("vector", em_ae, waits=w, inc="v")

        def em_spill(sc, b=bkt):
            return sc.dma_start(out=at_view[b], in_=aT_slab[:])
        sp_toks[bkt] = p.step("scalar", em_spill,
                              waits=[ae_toks[bkt][3]],
                              inc="dX", inc_by=16)

    p2b_end = [sp_toks[NBLK - 1], gam_toks[NGC - 1],
               av_toks[NBLK - 1][NT - 1], ta_toks[NBLK - 1][3],
               ae_toks[NBLK - 1][3],
               ex_toks[NBLK - 1][NHF * GB - 1]] + p.tails("dX", "dY", "dG")

    p2b.close()
    p3 = contextlib.ExitStack()
    wraw = sbuf("wraw", [128, 3, DT, 128], F32, p3)
    sa1 = sbuf("sa1", [128, DT, 128], BF16, p3)
    sb1 = sbuf("sb1", [128, DT, 128], BF16, p3)
    wq1 = sbuf("wq1", [128, 2, DT, 128], BF16, p3)
    wq2 = sbuf("wq2", [128, 2, DT, 128], BF16, p3)
    wq3 = sbuf("wq3", [128, 2, SFT, 128], BF16, p3)
    sigb = sbuf("sigb", [128, 2, S], BF16, p3)
    hid = sbuf("hid", [128, SFT, S], BF16, p3)
    o3st = sbuf("o3st", [128, 2, S], BF16, p3)
    pth_sc = contextlib.ExitStack()
    pth = psum("pth", [128, 4], F32, pth_sc)

    def em_gred(v):
        v.tensor_reduce(gam_acc[:, 0:1], gam_cols[:, 0:r1], axis=AX.X,
                        op=AL.add)
        v.drain()
        v.tensor_reduce(gam_acc[:, 1:2], gam_cols[:, r1:r2], axis=AX.X,
                        op=AL.add)
        v.drain()
        v.tensor_reduce(gam_acc[:, 2:3], gam_cols[:, r2:r2 + (NGC - r2)],
                        axis=AX.X, op=AL.add)
        v.drain()
        return v.nop()
    t_gr = p.step("vector", em_gred, waits=[gam_toks[NGC - 1]], inc="v")

    def em_gpr(gp):
        gp.tensor_reduce(gam_sc[:], gam_acc[:], axis=AX.C, op=AL.add)
        gp.drain()
        return gp.nop()
    t_gp = p.step("gpsimd", em_gpr, waits=[t_gr], inc="gp")

    def em_thr(v):
        v.tensor_scalar(thr_sc[:], gam_sc[:], 0.5 / n_w_elems, 0.5e-8,
                        op0=AL.mult, op1=AL.add)
        v.drain()
        return v.nop()
    t_th = p.step("vector", em_thr, waits=[t_gp], inc="v")

    t_tm = p.step("tensor",
                  lambda t: t.matmul(pth[:, 0:3], ones_row[:], thr_sc[:],
                                     start=True, stop=True),
                  waits=[t_th] + p2b_end, inc="t")

    def em_thev(v):
        v.tensor_copy(thr[:], pth[:, 0:3])
        v.drain()
        v.tensor_scalar(thrn[:], pth[:, 0:3], -1.0, None, op0=AL.mult)
        v.drain()
        return v.nop()
    t_te = p.step("vector", em_thev, waits=[t_tm], inc="v")
    pth_sc.close()
    pf1 = psum("pf1", [128, S], F32, p3)
    pf2 = psum("pf2", [128, S], F32, p3)


    slab_seq = []
    for sti in range(NST):
        for fi in range(SFT):
            f = sti * SFT + fi
            slab_seq.append((0, f))
            slab_seq.append((1, f))
        for d in range(DT):
            slab_seq.append((2, (sti, d)))
    NSLAB = len(slab_seq)
    sl_load = [None] * NSLAB
    sl_q = [None] * NSLAB
    sl_a = [None] * NSLAB
    slab_of = {}
    for i, ka in enumerate(slab_seq):
        slab_of[ka] = i

    f1_toks = [None] * FT
    f2_toks = [None] * FT
    sg_toks = [None] * FT
    hd_toks = [None] * FT
    f3_toks = [None] * (NST * DT)
    o3_toks = [None] * (NST * DT)
    od_toks = [None] * (NST * DT)

    def reg_slab(i):
        kind, arg = slab_seq[i]
        if kind == 0:
            src = w1s_d[arg]
            nk, bias_i = DT, 0
        elif kind == 1:
            src = w2s_d[arg]
            nk, bias_i = DT, 1
        else:
            sti, d = arg
            src = w3s_d[sti, d]
            nk, bias_i = SFT, 2

        def em_l(sy, src=src, i=i, nk=nk):
            return sy.dma_start(out=wraw[:, i % 3, :nk, :], in_=src)
        w = list(p2b_end) if i < 3 else [sl_q[i - 3]]
        sl_load[i] = p.step("sync", em_l, waits=w, inc="dY", inc_by=16)

        def em_q(sc, i=i, nk=nk, bias_i=bias_i):
            sc.activation(sa1[:, :nk, :], wraw[:, i % 3, :nk, :], AF.Sign,
                          bias=thr[:, bias_i:bias_i + 1], scale=1.0)
            sc.drain()
            sc.activation(sb1[:, :nk, :], wraw[:, i % 3, :nk, :], AF.Sign,
                          bias=thrn[:, bias_i:bias_i + 1], scale=1.0)
            sc.drain()
            return sc.nop()
        w = [sl_load[i], t_te]
        if i >= 1:
            w.append(sl_a[i - 1])
        sl_q[i] = p.step("scalar", em_q, waits=w, inc="sc")

        if kind == 0:
            dst, ridx, nk2 = wq1, arg % 2, DT
        elif kind == 1:
            dst, ridx, nk2 = wq2, arg % 2, DT
        else:
            dst, ridx, nk2 = wq3, arg[1] % 2, SFT

        def em_a(v, dst=dst, ridx=ridx, nk2=nk2):
            v.tensor_tensor(dst[:, ridx, :nk2, :], sa1[:, :nk2, :],
                            sb1[:, :nk2, :], op=AL.add)
            v.drain()
            return v.nop()
        w = [sl_q[i]]
        if kind == 0 and arg >= 2:
            w.append(f1_toks[arg - 2])
        if kind == 1 and arg >= 2:
            w.append(f2_toks[arg - 2])
        if kind == 2:
            u3 = arg[0] * DT + arg[1]
            if u3 >= 2:
                w.append(f3_toks[u3 - 2])
        sl_a[i] = p.step("vector", em_a, waits=w, inc="v")

    def reg_slab_upto(i):
        while reg_ptr[0] <= min(i, NSLAB - 1):
            reg_slab(reg_ptr[0])
            reg_ptr[0] += 1

    reg_ptr = [0]
    reg_slab_upto(1)
    for sti in range(NST):
        for fi in range(SFT):
            f = sti * SFT + fi
            if fi + 1 < SFT:
                reg_slab_upto(slab_of[(1, f + 1)])
            else:
                reg_slab_upto(slab_of[(2, (sti, 0))])

            def em_f1(t, f=f):
                last = None
                for kt in range(DT):
                    for ch in range(NCH):
                        last = t.matmul(pf1[:, 512 * ch:512 * (ch + 1)],
                                        wq1[:, f % 2, kt, :],
                                        xnT[:, kt, 512 * ch:512 * (ch + 1)],
                                        start=(kt == 0), stop=(kt == DT - 1))
                return last
            w = [sl_a[slab_of[(0, f)]]]
            if f >= 1:
                w.append(sg_toks[f - 1])
            if fi == 0 and sti >= 1:
                w.append(o3_toks[sti * DT - 1])
                w.append(o3_toks[sti * DT - 2])
            f1_toks[f] = p.step("tensor", em_f1, waits=w, inc="t")

            def em_sg(sc, f=f):
                return sc.activation(sigb[:, f % 2, :], pf1[:], AF.Sigmoid,
                                     scale=0.5)
            w = [f1_toks[f]]
            if f >= 2:
                w.append(hd_toks[f - 2])
            sg_toks[f] = p.step("scalar", em_sg, waits=w, inc="sc")

            def em_f2(t, f=f):
                last = None
                for kt in range(DT):
                    for ch in range(NCH):
                        last = t.matmul(pf2[:, 512 * ch:512 * (ch + 1)],
                                        wq2[:, f % 2, kt, :],
                                        xnT[:, kt, 512 * ch:512 * (ch + 1)],
                                        start=(kt == 0), stop=(kt == DT - 1))
                return last
            w = [sl_a[slab_of[(1, f)]]]
            if f >= 1:
                w.append(hd_toks[f - 1])
            if fi == 0 and sti >= 1:
                w.append(o3_toks[sti * DT - 1])
            f2_toks[f] = p.step("tensor", em_f2, waits=w, inc="t")

            def em_hd(v, f=f, fi=fi):
                v.scalar_tensor_tensor(hid[:, fi, :], pf2[:], 0.5,
                                       sigb[:, f % 2, :],
                                       op0=AL.mult, op1=AL.mult)
                v.drain()
                return v.nop()
            hd_toks[f] = p.step("vector", em_hd,
                                waits=[f2_toks[f], sg_toks[f]], inc="v")

        for d in range(DT):
            u = sti * DT + d
            if d + 1 < DT:
                reg_slab_upto(slab_of[(2, (sti, d + 1))])
            elif sti + 1 < NST:
                reg_slab_upto(slab_of[(1, (sti + 1) * SFT)])

            def em_f3(t, sti=sti, d=d):
                pp = pf1 if d % 2 == 0 else pf2
                last = None
                for ki in range(SFT):
                    for ch in range(NCH):
                        last = t.matmul(pp[:, 512 * ch:512 * (ch + 1)],
                                        wq3[:, d % 2, ki, :],
                                        hid[:, ki, 512 * ch:512 * (ch + 1)],
                                        start=(ki == 0),
                                        stop=(ki == SFT - 1))
                return last
            w = [sl_a[slab_of[(2, (sti, d))]], hd_toks[sti * SFT + SFT - 1]]
            if d <= 1:
                w.append(sg_toks[sti * SFT + SFT - 1])
            if u >= 2:
                w.append(o3_toks[u - 2])
            f3_toks[u] = p.step("tensor", em_f3, waits=w, inc="t")

            def em_o3(v, d=d, u=u):
                v.tensor_copy(o3st[:, u % 2, :],
                              (pf1 if d % 2 == 0 else pf2)[:])
                v.drain()
                return v.nop()
            w = [f3_toks[u]]
            if u >= 2:
                w.append(od_toks[u - 2])
            o3_toks[u] = p.step("vector", em_o3, waits=w, inc="v")

            def em_od(gp, sti=sti, d=d, u=u):
                kw = {} if sti == 0 else {"accum_op": AL.add}
                return gp.dma_start(out=o3_scr[128 * d:128 * (d + 1), :],
                                    in_=o3st[:, u % 2, :], **kw)
            od_toks[u] = p.step("gpsimd", em_od, waits=[o3_toks[u]],
                                inc="dG", inc_by=16)

    p3_end = [od_toks[NST * DT - 1], f3_toks[NST * DT - 1],
              o3_toks[NST * DT - 1], sl_q[NSLAB - 1],
              sl_a[NSLAB - 1], f2_toks[FT - 1]] + p.tails("dX", "dY", "dG")

    p3.close()
    xnT_sc.close()
    p4 = contextlib.ExitStack()
    attnF = sbuf("attnF", [128, DT, S], BF16, p4)
    woraw = sbuf("woraw", [128, 3, DT, 128], F32, p4)
    wob = sbuf("wob", [128, 3, DT, 128], BF16, p4)
    xTt = sbuf("xTt", [128, 4, S], F32, p4)
    o3ld = sbuf("o3ld", [128, 4, S], BF16, p4)
    outA = sbuf("outA", [128, 2, S], F32, p4)
    outB = sbuf("outB", [128, 2, S], F32, p4)
    pwo = [psum("pwo%d" % i, [128, S], F32, p4) for i in range(2)]

    atv = at_scr.rearrange("(kt p) s -> p kt s", p=128)
    o3v = o3_scr.rearrange("(mt p) s -> p mt s", p=128)
    xTv = xT_d.rearrange("(mt p) s -> p mt s", p=128)

    at_toks = []
    for kt in range(DT):
        def em_atl(sc, kt=kt):
            return sc.dma_start(out=attnF[:, kt, :], in_=atv[:, kt, :])
        w = (list(p3_end) + [sp_toks[NBLK - 1]]) if kt == 0 else []
        at_toks.append(p.step("scalar", em_atl, waits=w, inc="dX", inc_by=16))
    tok_atF = at_toks[-1]

    wol_toks = [None] * DT
    woc_toks = [None] * DT
    wo_toks = [None] * DT
    xl_toks = [None] * DT
    ol_toks = [None] * DT
    fA_toks = [None] * DT
    fB_toks = [None] * DT
    out_toks = [None] * DT

    for mt in range(DT):
        def em_wol(gp, mt=mt):
            return gp.dma_start(out=woraw[:, mt % 3, :, :], in_=wos_d[mt])
        w = list(p3_end) if mt == 0 else []
        if mt >= 3:
            w.append(woc_toks[mt - 3])
        wol_toks[mt] = p.step("gpsimd", em_wol, waits=w, inc="dG", inc_by=16)

        def em_woc(v, mt=mt):
            v.tensor_copy(wob[:, mt % 3, :, :], woraw[:, mt % 3, :, :])
            v.drain()
            return v.nop()
        w = [wol_toks[mt]] + (list(p3_end) if mt == 0 else [])
        if mt >= 3:
            w.append(wo_toks[mt - 3])
        woc_toks[mt] = p.step("vector", em_woc, waits=w, inc="v")

        def em_xl(sy, mt=mt):
            return sy.dma_start(out=xTt[:, mt % 4, :], in_=xTv[:, mt, :])
        w = [fA_toks[mt - 4]] if mt >= 4 else list(p3_end)
        xl_toks[mt] = p.step("sync", em_xl, waits=w, inc="dY", inc_by=16)

        def em_ol(sy, mt=mt):
            return sy.dma_start(out=o3ld[:, mt % 4, :], in_=o3v[:, mt, :])
        w = [od_toks[(NST - 1) * DT + mt]]
        if mt >= 4:
            w.append(fB_toks[mt - 4])
        else:
            w += list(p3_end)
        ol_toks[mt] = p.step("sync", em_ol, waits=w, inc="dY", inc_by=16)

        def em_wo(t, mt=mt):
            last = None
            for kt in range(DT):
                for ch in range(NCH):
                    last = t.matmul(pwo[mt % 2][:, 512 * ch:512 * (ch + 1)],
                                    wob[:, mt % 2, kt, :],
                                    attnF[:, kt, 512 * ch:512 * (ch + 1)],
                                    start=(kt == 0), stop=(kt == DT - 1))
            return last
        w = [woc_toks[mt]] + at_toks
        if mt >= 2:
            w.append(fA_toks[mt - 2])
        wo_toks[mt] = p.step("tensor", em_wo, waits=w, inc="t")

        def em_fA(v, mt=mt):
            v.scalar_tensor_tensor(outA[:, mt % 2, :], pwo[mt % 2][:], 1.0,
                                   xTt[:, mt % 4, :], op0=AL.mult, op1=AL.add)
            v.drain()
            return v.nop()
        w = [wo_toks[mt], xl_toks[mt]]
        if mt >= 2:
            w.append(fB_toks[mt - 2])
        fA_toks[mt] = p.step("vector", em_fA, waits=w, inc="v")

        def em_fB(v, mt=mt):
            v.scalar_tensor_tensor(outB[:, mt % 2, :], o3ld[:, mt % 4, :],
                                   0.5, outA[:, mt % 2, :],
                                   op0=AL.mult, op1=AL.add)
            v.drain()
            return v.nop()
        w = [fA_toks[mt], ol_toks[mt]]
        if mt >= 2:
            w.append(out_toks[mt - 2])
        fB_toks[mt] = p.step("vector", em_fB, waits=w, inc="v")

        def em_out(sc, mt=mt):
            return sc.dma_start(out=out_d[128 * mt:128 * (mt + 1), :],
                                in_=outB[:, mt % 2, :])
        out_toks[mt] = p.step("scalar", em_out, waits=[fB_toks[mt]],
                              inc="dX", inc_by=16)

    p.step("scalar", lambda sc: None, waits=[out_toks[DT - 1]])
    p.step("sync", lambda sy: None, waits=[sl_load[NSLAB - 1]])
    p.step("gpsimd", lambda gp: None, waits=[od_toks[NST * DT - 1]])

    with nc.Block() as blk:
        p.emit(blk)
    p4.close()

    return nc


_CACHED = {}


def _get_nc():
    if "nc" not in _CACHED:
        _CACHED["nc"] = build_program()
    return _CACHED["nc"]


def kernel(L0_attn_x, L0_attn_Wq, L0_attn_Wk, L0_attn_Wv, L0_attn_Wo,
           L0_ffn_w1, L0_ffn_w2, L0_ffn_w3):
    B = L0_attn_x.shape[0]
    nc = _get_nc()

    def C(a):
        return np.ascontiguousarray(np.asarray(a), dtype=np.float32)

    D = L0_attn_x.shape[2]
    S = L0_attn_x.shape[1]
    H = L0_attn_Wq.shape[0]
    F = L0_ffn_w1.shape[0]
    DT, FT, NVE = D // 128, F // 128, D // 512
    NST = 4 if FT % 4 == 0 else (2 if FT % 2 == 0 else 1)
    SFT = FT // NST
    wqs = C(np.asarray(L0_attn_Wq).T.reshape(DT, 128, H).transpose(1, 0, 2))
    wks = C(np.asarray(L0_attn_Wk).T.reshape(DT, 128, H).transpose(1, 0, 2))
    wvs = C(np.asarray(L0_attn_Wv).T.reshape(DT, 128, NVE, 512)
            .transpose(2, 1, 0, 3))
    wos = C(np.asarray(L0_attn_Wo).T.reshape(DT, 128, DT, 128)
            .transpose(2, 1, 0, 3))
    w1s = C(np.asarray(L0_ffn_w1).T.reshape(DT, 128, FT, 128)
            .transpose(2, 1, 0, 3))
    w2s = C(np.asarray(L0_ffn_w2).T.reshape(DT, 128, FT, 128)
            .transpose(2, 1, 0, 3))
    w3s = C(np.asarray(L0_ffn_w3).T.reshape(NST, SFT, 128, DT, 128)
            .transpose(0, 3, 2, 1, 4))

    in_maps = []
    for b in range(B):
        xb = C(L0_attn_x[b])
        in_maps.append({
            "x_tok": xb, "xT": C(xb.T),
            "wqs": wqs, "wks": wks, "wvs": wvs, "wos": wos,
            "w1s": w1s, "w2s": w2s, "w3s": w3s,
        })

    res = run_bass_kernel_spmd(nc, in_maps, core_ids=list(range(B)))
    out = np.empty((B, L0_attn_x.shape[1], L0_attn_x.shape[2]),
                   dtype=np.float32)
    for b in range(B):
        out[b] = np.asarray(res.results[b]["out"]).T
    return out


# version 20
# speedup vs baseline: 1.1516x; 1.0151x over previous
import numpy as np
import contextlib
import concourse.bass as bass
import concourse.mybir as mybir
from concourse.bass_utils import run_bass_kernel_spmd
from concourse import library_config

F32 = mybir.dt.float32
BF16 = mybir.dt.bfloat16
U16 = mybir.dt.uint16
I16 = mybir.dt.int16
AL = mybir.AluOpType
AF = mybir.ActivationFunctionType
AX = mybir.AxisListType

BIGC = float(2 ** 23)
NEG_BIG = -1e30
ENGINES = ("sync", "scalar", "vector", "tensor", "gpsimd")


class Prog:

    def __init__(self, nc):
        self.nc = nc
        self.steps = {e: [] for e in ENGINES}
        self.sems = {}
        self.counts = {}
        self.last_dma = {}
        self._es = contextlib.ExitStack()

    def sem(self, name):
        if name not in self.sems:
            self.sems[name] = self._es.enter_context(self.nc.semaphore(name))
            self.counts[name] = 0
        return self.sems[name]

    NSUB = 6

    def step(self, engine, emit, waits=(), inc=None, inc_by=1):
        tok = None
        waits = [w for w in waits if w is not None]
        if inc is not None:
            if inc_by == 16:
                jname = inc + "#j"
                j = self.counts.get(jname, 0)
                self.counts[jname] = j + 1
                sub = "%s%d" % (inc, j % self.NSUB)
                self.sem(sub)
                self.counts[sub] += 16
                if self.counts[sub] > 16:
                    waits.append((sub, self.counts[sub] - 16))
                tok = (sub, self.counts[sub], 16)
                self.last_dma.setdefault(inc, {})[sub] = tok
            else:
                self.sem(inc)
                self.counts[inc] += inc_by
                tok = (inc, self.counts[inc], inc_by)
        self.steps[engine].append((tuple(waits), emit, tok))
        return tok

    def tails(self, *queues):
        out = []
        for q in queues:
            out.extend(self.last_dma.get(q, {}).values())
        return out

    def emit(self, block):
        for engine in ENGINES:
            steps = self.steps[engine]
            if not steps:
                continue

            def body(eng, engine_steps=steps):
                waited = {}
                for waits, emit_fn, tok in engine_steps:
                    for w in waits:
                        sname, val = w[0], w[1]
                        if waited.get(sname, -1) < val:
                            eng.wait_ge(self.sems[sname], val)
                            waited[sname] = val
                    last = emit_fn(eng)
                    if tok is not None:
                        if last is None:
                            last = eng.nop()
                        last.then_inc(self.sems[tok[0]], tok[2])

            getattr(block, engine)(body)


def build_program(S=2048, F=5632):
    D, H = 2048, 128
    NT = S // 128
    DT = D // 128
    FT = F // 128
    NCH = S // 512
    G = H // 2
    GB = 8
    NBLK = G // GB
    NST = 4 if FT % 4 == 0 else (2 if FT % 2 == 0 else 1)
    SFT = FT // NST
    n_w_elems = float(F * D)

    nc = bass.Bass("TRN2", target_bir_lowering=False)
    p = Prog(nc)

    NVE_ = D // 512
    NST_ = 4 if FT % 4 == 0 else (2 if FT % 2 == 0 else 1)
    SFT_ = FT // NST_
    x_tok = nc.dram_tensor("x_tok", [S, D], F32, kind="ExternalInput").ap()
    xT_d = nc.dram_tensor("xT", [D, S], F32, kind="ExternalInput").ap()
    wqs_d = nc.dram_tensor("wqs", [128, DT, H], F32, kind="ExternalInput").ap()
    wks_d = nc.dram_tensor("wks", [128, DT, H], F32, kind="ExternalInput").ap()
    wvs_d = nc.dram_tensor("wvs", [NVE_, 128, DT, 512], F32,
                           kind="ExternalInput").ap()
    wos_d = nc.dram_tensor("wos", [DT, 128, DT, 128], F32,
                           kind="ExternalInput").ap()
    w1s_d = nc.dram_tensor("w1s", [FT, 128, DT, 128], F32,
                           kind="ExternalInput").ap()
    w2s_d = nc.dram_tensor("w2s", [FT, 128, DT, 128], F32,
                           kind="ExternalInput").ap()
    w3s_d = nc.dram_tensor("w3s", [NST_, DT, 128, SFT_, 128], F32,
                           kind="ExternalInput").ap()
    out_d = nc.dram_tensor("out", [D, S], F32, kind="ExternalOutput").ap()
    at_scr = nc.dram_tensor("at_scr", [D, S], BF16, kind="Internal").ap()
    o3_scr = nc.dram_tensor("o3_scr", [D, S], BF16, kind="Internal").ap()
    v_scr = [nc.dram_tensor("v_scr%d" % c, [S, 512], BF16,
                            kind="Internal").ap() for c in range(NVE_)]

    es = contextlib.ExitStack()

    def sbuf(name, shape, dtype, stack=None, side=None):
        kw = {} if side is None else {"side": side}
        return (stack or es).enter_context(
            nc.sbuf_tensor(name, shape, dtype, **kw))

    def psum(name, shape, dtype=F32, stack=None):
        return (stack or es).enter_context(
            nc.psum_tensor(name, shape, dtype))

    xnT_sc = contextlib.ExitStack()
    xnT = sbuf("xnT", [128, DT, S], BF16, xnT_sc, side="right")
    ident = sbuf("ident", [128, 128], BF16)
    iota_r = sbuf("iota_r", [128, 128], F32)
    iota_g = sbuf("iota_g", [128, G], F32)
    iota_rp = sbuf("iota_rp", [128, 128], F32)
    hcol = sbuf("hcol", [128, 1], F32)
    hcol_i = sbuf("hcol_i", [128, 1], mybir.dt.int32)
    hbit_i = sbuf("hbit_i", [128, 1], mybir.dt.int32)
    hm2 = sbuf("hm2", [128, 1], F32)
    h2col = sbuf("h2col", [128, 1], F32)
    rlt64 = sbuf("rlt64", [128, 1], F32)
    nt0 = sbuf("nt0", [128, 1], F32)
    h31col = sbuf("h31col", [128, 1], F32)
    biascol = sbuf("biascol", [128, 1], F32)
    Gm = sbuf("Gm", [128, G], BF16)
    Pm = sbuf("Pm", [128, 128], BF16)
    csr = sbuf("csr", [128, 128], F32)
    KV2 = sbuf("KV2", [128, 128], BF16)
    CP2 = sbuf("CP2", [128, 128], BF16)
    gam_cols = sbuf("gam_cols", [128, 192], F32)
    gam_acc = sbuf("gam_acc", [128, 3], F32)
    gam_sc = sbuf("gam_sc", [1, 3], F32)
    thr_sc = sbuf("thr_sc", [1, 3], F32)
    ones_row = sbuf("ones_row", [1, 128], F32)
    thr = sbuf("thr", [128, 3], F32)
    thrn = sbuf("thrn", [128, 3], F32)
    qb = sbuf("qb", [128, S], BF16)
    gq = sbuf("gq", [128, S], BF16)
    kvals = sbuf("kvals", [128, 64], F32)
    Vsel = sbuf("Vsel", [128, G, 34], BF16)
    idxb0 = sbuf("idxb0", [64, G], mybir.dt.uint32)
    idxb1 = sbuf("idxb1", [64, G], mybir.dt.uint32)

    gsc = contextlib.ExitStack()
    GCW = 2048
    gslab = sbuf("gslab", [128, 3, GCW], F32, gsc)
    gdum = sbuf("gdum", [128, GCW], BF16, gsc)

    gchunks = []
    nper = []
    for wd in (w1s_d, w2s_d, w3s_d):
        flat = wd.rearrange("a b c d -> (a b c d)") if len(wd.shape) == 4 \
            else wd.rearrange("a b c d e -> (a b c d e)")
        nch_ = flat.shape[0] // (128 * GCW)
        assert nch_ * 128 * GCW == flat.shape[0]
        chv = flat.rearrange("(n p q) -> n p q", p=128, q=GCW)
        for i_ in range(nch_):
            gchunks.append((chv[i_], GCW))
        nper.append(nch_)
    NGC = len(gchunks)
    assert NGC <= 192
    r1 = nper[0]
    r2 = nper[0] + nper[1]

    gam_toks = [None] * NGC
    gslab_toks = [None] * NGC

    def reg_gamma(i):
        ap, nel = gchunks[i]

        def em_dma(sy, ap=ap, nel=nel, i=i):
            return sy.dma_start(out=gslab[:, i % 3, :nel], in_=ap)
        w = [gam_toks[i - 3]] if i >= 3 else []
        gslab_toks[i] = p.step("sync", em_dma, waits=w, inc="dY", inc_by=16)

        def em_abs(sc, i=i, nel=nel):
            sc.activation(gdum[:, :nel], gslab[:, i % 3, :nel],
                          AF.Abs, accum_out=gam_cols[:, i:i + 1])
            sc.drain()
            return sc.nop()
        gam_toks[i] = p.step("scalar", em_abs, waits=[gslab_toks[i]],
                             inc="sGa")

    def em_iotas(gp):
        gp.iota(iota_r[:], pattern=[[1, 128]], base=0, channel_multiplier=0,
                allow_small_or_imprecise_dtypes=True)
        gp.iota(iota_g[:], pattern=[[1, G]], base=0, channel_multiplier=0,
                allow_small_or_imprecise_dtypes=True)
        gp.iota(iota_rp[:], pattern=[[1, 2], [0, 64]], base=0,
                channel_multiplier=0, allow_small_or_imprecise_dtypes=True)
        gp.iota(hcol[:], pattern=[[0, 1]], base=0, channel_multiplier=1,
                allow_small_or_imprecise_dtypes=True)
        gp.iota(hcol_i[:], pattern=[[0, 1]], base=0, channel_multiplier=1)
        gp.iota(csr[:], pattern=[[0, 2], [1, 2], [0, 32]], base=0,
                channel_multiplier=0, allow_small_or_imprecise_dtypes=True)
        gp.memset(gam_cols[:], 0.0)
        gp.memset(ones_row[:], 1.0)
        gp.drain()
        return gp.nop()
    tok_iota = p.step("gpsimd", em_iotas, inc="gp")

    def em_ident(v):
        v.tensor_scalar(ident[:], iota_r[:], hcol[:], None, op0=AL.is_equal)
        v.drain()
        return v.nop()
    tok_ident = p.step("vector", em_ident, waits=[tok_iota], inc="v")

    def em_masks(v):
        v.tensor_scalar(hbit_i[:], hcol_i[:], 1, None, op0=AL.bitwise_and)
        v.drain()
        v.tensor_copy(hm2[:], hbit_i[:])
        v.drain()
        v.tensor_tensor(h2col[:], hcol[:], hm2[:], op=AL.subtract)
        v.drain()
        v.tensor_scalar(h2col[:], h2col[:], 0.5, None, op0=AL.mult)
        v.drain()
        v.tensor_scalar(Gm[:], iota_g[:], h2col[:], None, op0=AL.is_equal)
        v.drain()
        v.tensor_scalar(Pm[:], iota_rp[:], hm2[:], None, op0=AL.is_equal)
        v.drain()
        v.tensor_scalar(csr[:], csr[:], -2.0 * BIGC, BIGC,
                        op0=AL.mult, op1=AL.add)
        v.drain()
        v.tensor_tensor(CP2[:], csr[:], Pm[:], op=AL.mult)
        v.drain()
        v.tensor_scalar(rlt64[:], hcol[:], 64.0, None, op0=AL.is_lt)
        v.drain()
        v.tensor_scalar(hbit_i[:], hcol_i[:], 31, None, op0=AL.bitwise_and)
        v.drain()
        v.tensor_copy(h31col[:], hbit_i[:])
        v.drain()
        v.tensor_scalar(hbit_i[:], hcol_i[:], 32, None, op0=AL.bitwise_and)
        v.drain()
        v.tensor_copy(nt0[:], hbit_i[:])
        v.drain()
        v.tensor_scalar(nt0[:], nt0[:], 0.0, None, op0=AL.is_equal)
        v.drain()
        v.tensor_scalar(biascol[:], nt0[:], -BIGC, None, op0=AL.mult)
        v.drain()
        v.memset(Vsel[:], 0.0)
        v.drain()
        v.tensor_scalar(Vsel[:, :, 32:33],
                        rlt64[:, None, :].broadcast_to((128, G, 1)),
                        1.0, None, op0=AL.mult)
        v.drain()
        v.tensor_scalar(Vsel[:, :, 33:34],
                        rlt64[:, None, :].broadcast_to((128, G, 1)),
                        -1.0, 1.0, op0=AL.mult, op1=AL.add)
        v.drain()
        return v.nop()
    tok_masks = p.step("vector", em_masks, inc="v")

    p1 = contextlib.ExitStack()
    XR = 2
    x_tile = sbuf("x_tile", [128, XR, D], F32, p1)
    xn_stage = sbuf("xn_stage", [128, XR, D], BF16, p1)
    stats = sbuf("stats", [128, XR, 24], F32, p1)
    mv = sbuf("mv", [128, XR, 2], F32, p1)
    veps = sbuf("veps", [128, XR, 1], F32, p1)
    rs1 = sbuf("rs1", [128, XR, 1], F32, p1)
    nt1 = sbuf("nt1", [128, XR, 1], F32, p1)
    pxn = [psum("pxn%d" % i, [128, DT, 128], BF16, p1) for i in range(2)]

    p2a = contextlib.ExitStack()
    U32 = mybir.dt.uint32
    wvs2 = sbuf("wvs2", [128, 2, DT, 512], BF16, p2a)
    vtile = sbuf("vtile", [128, 4, 512], BF16, p2a)
    keysw = sbuf("keysw", [128, 3, S], F32, p2a)
    wqTb = sbuf("wqTb", [128, DT, H], BF16, p2a)
    wkTb = sbuf("wkTb", [128, DT, H], BF16, p2a)
    idx = sbuf("idx", [128, 64], U16, p2a)
    idxcf = sbuf("idxcf", [128, 64], F32, p2a)
    kvn = sbuf("kvn", [128, 32], F32, p2a)
    identF = sbuf("identF", [128, 128], F32, p2a)
    RING = 3
    pqk = [psum("pqk%d" % i, [128, 512], F32, p2a) for i in range(RING)]
    pidx = psum("pidx", [64, 128], F32, p2a)

    vflatc = [vc.rearrange("s (hh cc) -> (s hh) cc", cc=16) for vc in v_scr]

    t_idf = p.step("vector",
                   lambda v: v.tensor_scalar(identF[:], iota_r[:], hcol[:],
                                             None, op0=AL.is_equal),
                   waits=[tok_iota], inc="v")

    gamma_ptr = [0]

    def pump_gamma(n):
        while gamma_ptr[0] < min(NGC, n):
            reg_gamma(gamma_ptr[0])
            gamma_ptr[0] += 1

    xn_toks = [None] * NT
    xload_toks = [None] * NT
    xpose_toks = [None] * NT
    xev_toks = [None] * NT
    sqrt_toks = [None] * NT

    mm_evs = []
    NQK = 2 * NCH
    NVE = D // 512
    qk_ev = [None] * NQK
    vmm_toks = [[None] * NT for _ in range(NVE)]
    vsp_flat = []
    vsub_toks = [[None] * 4 for _ in range(NVE)]
    cqk = [None, None]

    def reg_vsub(ech, q, extra=()):
        def em(gp, ech=ech, q=q):
            return gp.dma_start(out=wvs2[:, ech % 2, 4 * q:4 * (q + 1), :],
                                in_=wvs_d[ech][:, 4 * q:4 * (q + 1), :])
        vsub_toks[ech][q] = p.step("gpsimd", em, waits=list(extra),
                                   inc="dG", inc_by=16)

    def reg_qk(wi, ch):
        u01 = wi * NCH + ch
        k = len(mm_evs)
        slot = k % RING

        def em_mm(t, wi=wi, ch=ch, slot=slot):
            wt = wqTb if wi == 0 else wkTb
            last = None
            for kt in range(DT):
                last = t.matmul(pqk[slot][:], wt[:, kt, :],
                                xnT[:, kt, 512 * ch:512 * (ch + 1)],
                                start=(kt == 0), stop=(kt == DT - 1))
            return last
        w = [cqk[wi], xev_toks[4 * ch + 3]]
        if k >= RING:
            w.append(mm_evs[k - RING])
        tok_mm = p.step("tensor", em_mm, waits=w, inc="t")

        if wi == 0:
            def em_ev(v, ch=ch, slot=slot):
                v.tensor_copy(qb[:, 512 * ch:512 * (ch + 1)], pqk[slot][:])
                v.drain()
                v.tensor_scalar(gq[:, 512 * ch:512 * (ch + 1)],
                                pqk[slot][:], 0.0, None, op0=AL.is_gt)
                v.drain()
                return v.nop()
        else:
            def em_ev(v, ch=ch, slot=slot):
                v.tensor_copy(keysw[:, 0, 512 * ch:512 * (ch + 1)],
                              pqk[slot][:])
                v.drain()
                return v.nop()
        qk_ev[u01] = p.step("vector", em_ev, waits=[tok_mm], inc="v")
        mm_evs.append(qk_ev[u01])

    def reg_vunit(ech, mt):
        k = len(mm_evs)
        slot = k % RING
        vu = len(vsp_flat)
        vslot = vu % 4

        def em_vmm(t, ech=ech, mt=mt, slot=slot):
            last = None
            for kt in range(DT):
                last = t.matmul(pqk[slot][:],
                                xnT[:, kt, 128 * mt:128 * (mt + 1)],
                                wvs2[:, ech % 2, kt, :],
                                start=(kt == 0), stop=(kt == DT - 1))
            return last
        w = list(vsub_toks[ech]) + [xev_toks[mt]]
        if k >= RING:
            w.append(mm_evs[k - RING])
        vmm_toks[ech][mt] = p.step("tensor", em_vmm, waits=w, inc="t")

        def em_vev(v, slot=slot, vslot=vslot):
            v.tensor_copy(vtile[:, vslot, :], pqk[slot][:])
            v.drain()
            return v.nop()
        w = [vmm_toks[ech][mt]]
        if vu >= 4:
            w.append(vsp_flat[vu - 4])
        tok_ev = p.step("vector", em_vev, waits=w, inc="v")
        mm_evs.append(tok_ev)

        def em_vsp(sc, ech=ech, mt=mt, vslot=vslot):
            return sc.dma_start(
                out=v_scr[ech][128 * mt:128 * (mt + 1), :],
                in_=vtile[:, vslot, :])
        vsp_flat.append(p.step("scalar", em_vsp, waits=[tok_ev],
                               inc="dX", inc_by=16))

    for st in range(NT):
        b = st % XR

        def em_xload(gp, st=st, b=b):
            return gp.dma_start(out=x_tile[:, b, :],
                                in_=x_tok[128 * st:128 * (st + 1), :])
        w = [xn_toks[st - XR]] if st >= XR else []
        xload_toks[st] = p.step("gpsimd", em_xload, waits=w, inc="dG",
                                inc_by=16)
        if st == 1:
            cqk[0] = p.step(
                "gpsimd",
                lambda gp: gp.dma_start(out=wqTb[:], in_=wqs_d[:]),
                inc="dG", inc_by=16)
            cqk[1] = p.step(
                "gpsimd",
                lambda gp: gp.dma_start(out=wkTb[:], in_=wks_d[:]),
                inc="dG", inc_by=16)
        elif st in (2, 3):
            reg_vsub(0, 2 * (st - 2))
            reg_vsub(0, 2 * (st - 2) + 1)
        elif 4 <= st <= 7:
            reg_vsub(1, st - 4)

        def em_stats(v, b=b):
            for c in range(4):
                v.bn_stats(stats[:, b, 6 * c:6 * c + 6],
                           x_tile[:, b, 512 * c:512 * (c + 1)])
            v.drain()
            v.bn_aggr(mv[:, b, :], stats[:, b, :])
            v.drain()
            v.tensor_scalar(veps[:, b, :], mv[:, b, 1:2], 1e-5, None,
                            op0=AL.add)
            v.drain()
            return v.nop()
        t_stats = p.step("vector", em_stats, waits=[xload_toks[st]], inc="v")

        def em_sqrt(sc, b=b):
            sc.activation(nt1[:, b, :], veps[:, b, :], AF.Ln)
            sc.drain()
            sc.activation(rs1[:, b, :], nt1[:, b, :], AF.Exp, scale=-0.5)
            sc.drain()
            return sc.nop()
        sqrt_toks[st] = p.step("scalar", em_sqrt, waits=[t_stats], inc="sc")

        def em_xn(v, b=b):
            v.tensor_scalar(xn_stage[:, b, :], x_tile[:, b, :],
                            mv[:, b, 0:1], rs1[:, b, :],
                            op0=AL.subtract, op1=AL.mult)
            v.drain()
            return v.nop()
        xn_toks[st] = p.step("vector", em_xn, waits=[sqrt_toks[st]], inc="v")

        def em_tp(t, st=st, b=b):
            last = None
            for dt in range(DT):
                last = t.transpose(pxn[st % 2][:, dt, :],
                                   xn_stage[:, b, 128 * dt:128 * (dt + 1)],
                                   ident[:])
            return last
        w = [xn_toks[st], tok_ident]
        if st >= 2:
            w.append(xev_toks[st - 2])
        xpose_toks[st] = p.step("tensor", em_tp, waits=w, inc="t")

        def em_ev(v, st=st):
            v.tensor_copy(xnT[:, :, 128 * st:128 * (st + 1)], pxn[st % 2][:])
            v.drain()
            return v.nop()
        xev_toks[st] = p.step("vector", em_ev, waits=[xpose_toks[st]],
                              inc="v")
        if st % 4 == 3:
            reg_qk(0, st // 4)
            reg_qk(1, st // 4)
        if st >= 4:
            reg_vunit(0, st - 4)
        pump_gamma(st + 1)

    for ech in (0, 1):
        for q in range(4):
            if vsub_toks[ech][q] is None:
                reg_vsub(ech, q)
    for mt in range(max(0, NT - 4), NT):
        if vmm_toks[0][mt] is None:
            reg_vunit(0, mt)

    seq_top = [0, 1, 2, 1]
    seq_bot = [1, 2, 1, 2]

    def mk_top(r):
        def em(v, r=r):
            src = keysw[:, seq_top[r], :]
            v.max(kvals[:, 8 * r:8 * r + 8], src)
            v.drain()
            v.max_index(idx[:, 8 * r:8 * r + 8], kvals[:, 8 * r:8 * r + 8],
                        src)
            if r < 3:
                v.match_replace(keysw[:, seq_top[r + 1], :],
                                kvals[:, 8 * r:8 * r + 8], src, NEG_BIG)
            v.drain()
            return v.nop()
        return em

    def em_negk(v):
        v.tensor_scalar(keysw[:, 1, :], keysw[:, 0, :], -1.0, None,
                        op0=AL.mult)
        v.drain()
        return v.nop()

    def mk_bot(r):
        def em(v, r=r):
            src = keysw[:, seq_bot[r], :]
            v.max(kvn[:, 8 * r:8 * r + 8], src)
            v.drain()
            v.max_index(idx[:, 32 + 8 * r:32 + 8 * r + 8],
                        kvn[:, 8 * r:8 * r + 8], src)
            if r < 3:
                v.match_replace(keysw[:, seq_bot[r + 1], :],
                                kvn[:, 8 * r:8 * r + 8], src, NEG_BIG)
            v.drain()
            return v.nop()
        return em

    def em_negv(v):
        v.tensor_scalar(kvals[:, 32:64], kvn[:], -1.0, None, op0=AL.mult)
        v.drain()
        return v.nop()

    sel_emits = [mk_top(r) for r in range(4)] + [em_negk] \
        + [mk_bot(r) for r in range(4)] + [em_negv]
    sel_ptr = [0]
    sel_tok = [None]

    def pump_sel(n=1):
        for _ in range(n):
            if sel_ptr[0] >= len(sel_emits):
                return
            sel_tok[0] = p.step("vector", sel_emits[sel_ptr[0]], inc="v")
            sel_ptr[0] += 1

    GTAIL = 48
    for mt in range(NT):
        reg_vunit(1, mt)
        pump_sel(1)
        pump_gamma(min(GTAIL, gamma_ptr[0] + 1))
    pump_sel(len(sel_emits))
    tok_sel = sel_tok[0]

    def em_idxc(v):
        v.tensor_copy(idxcf[:], idx[:])
        v.drain()
        v.tensor_scalar(idxcf[:], idxcf[:], 32.0, h31col[:],
                        op0=AL.mult, op1=AL.add)
        v.drain()
        return v.nop()
    t_ic = p.step("vector", em_idxc, waits=[tok_sel], inc="v")

    def em_kv2(v):
        v.tensor_tensor(KV2.ap().rearrange("p (two j) -> p two j", j=64),
                        kvals[:, None, :].broadcast_to((128, 2, 64)),
                        Pm.ap().rearrange("p (two j) -> p two j", j=64),
                        op=AL.mult)
        v.drain()
        return v.nop()
    tok_kv2 = p.step("vector", em_kv2, waits=[tok_sel, tok_masks], inc="v")

    for q2 in range(4):
        reg_vsub(2, q2, extra=[vmm_toks[0][NT - 1]])
    for mt in range(NT):
        reg_vunit(2, mt)
        pump_gamma(min(GTAIL, gamma_ptr[0] + 1))

    t_it = p.step("tensor",
                  lambda t: t.transpose(pidx[:], idxcf[:], identF[:]),
                  waits=[t_ic, t_idf], inc="t")

    def em_ib(v):
        v.tensor_copy(idxb0[:], pidx[:, 0::2])
        v.drain()
        v.tensor_copy(idxb1[:], pidx[:, 1::2])
        v.drain()
        return v.nop()
    t_ib = p.step("vector", em_ib, waits=[t_it], inc="v")

    for q2 in range(4):
        reg_vsub(3, q2, extra=[vmm_toks[1][NT - 1]])
    for mt in range(NT):
        reg_vunit(3, mt)
        pump_gamma(min(GTAIL, gamma_ptr[0] + 1))

    GPC = G // NVE
    gv_toks = [None] * (2 * G)
    for g in range(G):
        c = g // GPC
        spw = vsp_flat[c * NT + max(0, NT - p.NSUB):(c + 1) * NT]

        def em_g0(gp, g=g, c=c):
            return gp.indirect_dma_start(
                out=Vsel[0:64, g, 0:16], out_offset=None, in_=vflatc[c],
                in_offset=bass.IndirectOffsetOnAxis(ap=idxb0[:, g:g + 1],
                                                    axis=0))
        gv_toks[2 * g] = p.step("gpsimd", em_g0,
                                waits=[t_ib, tok_masks] + spw,
                                inc="dG", inc_by=16)

        def em_g1(gp, g=g, c=c):
            return gp.indirect_dma_start(
                out=Vsel[64:128, g, 16:32], out_offset=None, in_=vflatc[c],
                in_offset=bass.IndirectOffsetOnAxis(ap=idxb1[:, g:g + 1],
                                                    axis=0))
        gv_toks[2 * g + 1] = p.step("gpsimd", em_g1, waits=[gv_toks[2 * g]],
                                    inc="dG", inc_by=16)
    tok_vsel = gv_toks[2 * G - 1]

    pump_gamma(GTAIL)

    p2a_end = [tok_kv2, tok_sel, t_ic, t_it, t_ib,
               vmm_toks[NVE - 1][NT - 1], mm_evs[-1],
               qk_ev[NQK - 1], xn_toks[NT - 1], xev_toks[NT - 1],
               xpose_toks[NT - 1], sqrt_toks[NT - 1]] + p.tails("dX")

    p2a.close()
    p1.close()
    p2b = contextlib.ExitStack()
    Esl = sbuf("Esl", [128, 2, GB, S], BF16, p2b)
    Lblk = sbuf("Lblk", [128, 2, GB, 128], BF16, p2b)
    Cblk = sbuf("Cblk", [128, 2, GB, 128], BF16, p2b)
    a_sE = sbuf("a_sE", [128, NT, 32 * GB], BF16, p2b)
    aT_slab = sbuf("aT_slab", [128, 2, NT, 128], BF16, p2b)
    rden = sbuf("rden", [128, GB, 2], F32, p2b)
    SCW = min(1024, S)
    NHF = S // SCW
    psc = [psum("psc%d" % i, [128, SCW], F32, p2b) for i in range(2)]
    pav = [psum("pav%d" % i, [128, GB, 34], F32, p2b) for i in range(2)]
    pat = [psum("pat%d" % i, [128, NT // 2, 128], BF16, p2b)
           for i in range(2)]

    lb_toks = [None] * NBLK
    sc_toks = [[None] * (NHF * GB) for _ in range(NBLK)]
    ex_toks = [[None] * (NHF * GB) for _ in range(NBLK)]
    av_toks = [[None] * NT for _ in range(NBLK)]
    dv_toks = [[None] * NT for _ in range(NBLK)]
    ta_toks = [[None] * 4 for _ in range(NBLK)]
    ae_toks = [[None] * 4 for _ in range(NBLK)]
    sp_toks = [None] * NBLK

    at_view = at_scr.rearrange("(bb eb p) (st j) -> bb p eb st j",
                               bb=NBLK, p=128, j=128)

    def reg_lb(bkt):
        def em_lb(v, b=bkt):
            gsl = slice(GB * b, GB * (b + 1))
            v.tensor_tensor(Lblk[:, b % 2, :, :],
                            KV2[:, None, :].broadcast_to((128, GB, 128)),
                            Gm[:, gsl, None].broadcast_to((128, GB, 128)),
                            op=AL.mult)
            v.drain()
            v.tensor_tensor(Cblk[:, b % 2, :, :],
                            CP2[:, None, :].broadcast_to((128, GB, 128)),
                            Gm[:, gsl, None].broadcast_to((128, GB, 128)),
                            op=AL.mult)
            v.drain()
            return v.nop()
        w = [tok_kv2] + (p2a_end if bkt == 0 else [])
        if bkt >= 2:
            w.append(sc_toks[bkt - 2][NHF * GB - 1])
        lb_toks[bkt] = p.step("vector", em_lb, waits=w, inc="v")

    reg_lb(0)
    for bkt in range(NBLK):
        if bkt + 1 < NBLK:
            reg_lb(bkt + 1)
        for gi in range(GB):
            for hf in range(NHF):
                u = (bkt * GB + gi) * NHF + hf
                uu = u % 2

                def em_sc_mm(t, b=bkt, gi=gi, hf=hf, uu=uu):
                    last = None
                    for cc in range(SCW // 512):
                        n0 = SCW * hf + 512 * cc
                        t.matmul(psc[uu][:, 512 * cc:512 * (cc + 1)],
                                 Lblk[:, b % 2, gi, :], qb[:, n0:n0 + 512],
                                 start=True, stop=False)
                        last = t.matmul(psc[uu][:, 512 * cc:512 * (cc + 1)],
                                        Cblk[:, b % 2, gi, :],
                                        gq[:, n0:n0 + 512],
                                        start=False, stop=True)
                    return last
                w = [lb_toks[bkt]]
                if u >= 2:
                    w.append(ex_toks[(u - 2) // (NHF * GB)]
                             [(u - 2) % (NHF * GB)])
                if gi == 0 and hf == 0 and bkt >= 2:
                    w.append(av_toks[bkt - 2][NT - 1])
                sc_toks[bkt][NHF * gi + hf] = p.step("tensor", em_sc_mm,
                                                     waits=w, inc="t")

                def em_exp(sc, b=bkt, gi=gi, hf=hf, uu=uu):
                    return sc.activation(
                        Esl[:, b % 2, gi, SCW * hf:SCW * (hf + 1)],
                        psc[uu][:], AF.Exp, bias=biascol[:], scale=1.0)
                ex_toks[bkt][NHF * gi + hf] = p.step(
                    "scalar", em_exp, waits=[sc_toks[bkt][NHF * gi + hf]],
                    inc="sEx")
            pump_gamma(gamma_ptr[0] + 1)

        for st in range(NT):
            def em_av(t, b=bkt, st=st):
                last = None
                for gi in range(GB):
                    last = t.matmul(pav[st % 2][:, gi, :],
                                    Esl[:, b % 2, gi,
                                        128 * st:128 * (st + 1)],
                                    Vsel[:, b * GB + gi, :],
                                    start=True, stop=True)
                return last
            hi = 2 * (bkt + 1) * GB
            w = [ex_toks[bkt][NHF * GB - 1]] + gv_toks[max(0, hi - 6):hi]
            u = bkt * NT + st
            if u >= 2:
                pb, pst = divmod(u - 2, NT)
                w.append(dv_toks[pb][pst])
            av_toks[bkt][st] = p.step("tensor", em_av, waits=w, inc="t")

            def em_dv(v, st=st):
                v.reciprocal(rden[:], pav[st % 2][:, :, 32:34])
                v.drain()
                v.tensor_tensor(
                    a_sE[:, st, :].rearrange("p (g par c) -> p g par c",
                                             g=GB, c=16),
                    pav[st % 2][:, :, 0:32].rearrange(
                        "p g (par c) -> p g par c", c=16),
                    rden[:, :, :, None].broadcast_to((128, GB, 2, 16)),
                    op=AL.mult)
                v.drain()
                return v.nop()
            dv_toks[bkt][st] = p.step("vector", em_dv,
                                      waits=[av_toks[bkt][st]], inc="v")

        HB = NT // 2
        for hb in range(2):
            for eb in range(2):
                u2 = hb * 2 + eb

                def em_ta(t, hb=hb, eb=eb, u2=u2, b=bkt):
                    last = None
                    for si in range(HB):
                        st = hb * HB + si
                        last = t.transpose(
                            pat[u2 % 2][:, si, :],
                            a_sE[:, st, 128 * eb:128 * (eb + 1)],
                            ident[:])
                    return last
                w = [dv_toks[bkt][hb * HB + HB - 1]]
                gu = bkt * 4 + u2
                if gu >= 2:
                    pb, pu2 = divmod(gu - 2, 4)
                    w.append(ae_toks[pb][pu2])
                ta_toks[bkt][u2] = p.step("tensor", em_ta, waits=w, inc="t")

                def em_ae(v, hb=hb, eb=eb, u2=u2):
                    v.tensor_copy(
                        aT_slab[:, eb, hb * HB:(hb + 1) * HB, :],
                        pat[u2 % 2][:])
                    v.drain()
                    return v.nop()
                w = [ta_toks[bkt][u2]]
                if bkt >= 1 and u2 == 0:
                    w.append(sp_toks[bkt - 1])
                ae_toks[bkt][u2] = p.step("vector", em_ae, waits=w, inc="v")

        def em_spill(sc, b=bkt):
            return sc.dma_start(out=at_view[b], in_=aT_slab[:])
        sp_toks[bkt] = p.step("scalar", em_spill,
                              waits=[ae_toks[bkt][3]],
                              inc="dX", inc_by=16)
        pump_gamma(gamma_ptr[0] + 3)

    pump_gamma(NGC)
    p2b_end = [sp_toks[NBLK - 1], gam_toks[NGC - 1],
               av_toks[NBLK - 1][NT - 1], ta_toks[NBLK - 1][3],
               ae_toks[NBLK - 1][3],
               ex_toks[NBLK - 1][NHF * GB - 1]] + p.tails("dX", "dY", "dG")

    p2b.close()
    gsc.close()
    p3 = contextlib.ExitStack()
    wraw = sbuf("wraw", [128, 3, DT, 128], F32, p3)
    sa1 = sbuf("sa1", [128, DT, 128], BF16, p3)
    sb1 = sbuf("sb1", [128, DT, 128], BF16, p3)
    wq1 = sbuf("wq1", [128, 2, DT, 128], BF16, p3)
    wq2 = sbuf("wq2", [128, 2, DT, 128], BF16, p3)
    wq3 = sbuf("wq3", [128, 2, SFT, 128], BF16, p3)
    sigb = sbuf("sigb", [128, 2, S], BF16, p3)
    hid = sbuf("hid", [128, SFT, S], BF16, p3)
    o3st = sbuf("o3st", [128, 2, S], BF16, p3)
    pth_sc = contextlib.ExitStack()
    pth = psum("pth", [128, 4], F32, pth_sc)

    def em_gred(v):
        v.tensor_reduce(gam_acc[:, 0:1], gam_cols[:, 0:r1], axis=AX.X,
                        op=AL.add)
        v.drain()
        v.tensor_reduce(gam_acc[:, 1:2], gam_cols[:, r1:r2], axis=AX.X,
                        op=AL.add)
        v.drain()
        v.tensor_reduce(gam_acc[:, 2:3], gam_cols[:, r2:r2 + (NGC - r2)],
                        axis=AX.X, op=AL.add)
        v.drain()
        return v.nop()
    t_gr = p.step("vector", em_gred, waits=[gam_toks[NGC - 1]], inc="v")

    def em_gpr(gp):
        gp.tensor_reduce(gam_sc[:], gam_acc[:], axis=AX.C, op=AL.add)
        gp.drain()
        return gp.nop()
    t_gp = p.step("gpsimd", em_gpr, waits=[t_gr], inc="gp")

    def em_thr(v):
        v.tensor_scalar(thr_sc[:], gam_sc[:], 0.5 / n_w_elems, 0.5e-8,
                        op0=AL.mult, op1=AL.add)
        v.drain()
        return v.nop()
    t_th = p.step("vector", em_thr, waits=[t_gp], inc="v")

    t_tm = p.step("tensor",
                  lambda t: t.matmul(pth[:, 0:3], ones_row[:], thr_sc[:],
                                     start=True, stop=True),
                  waits=[t_th] + p2b_end, inc="t")

    def em_thev(v):
        v.tensor_copy(thr[:], pth[:, 0:3])
        v.drain()
        v.tensor_scalar(thrn[:], pth[:, 0:3], -1.0, None, op0=AL.mult)
        v.drain()
        return v.nop()
    t_te = p.step("vector", em_thev, waits=[t_tm], inc="v")
    pth_sc.close()
    pf1 = psum("pf1", [128, S], F32, p3)
    pf2 = psum("pf2", [128, S], F32, p3)


    slab_seq = []
    for sti in range(NST):
        for fi in range(SFT):
            f = sti * SFT + fi
            slab_seq.append((0, f))
            slab_seq.append((1, f))
        for d in range(DT):
            slab_seq.append((2, (sti, d)))
    NSLAB = len(slab_seq)
    sl_load = [None] * NSLAB
    sl_q = [None] * NSLAB
    sl_a = [None] * NSLAB
    slab_of = {}
    for i, ka in enumerate(slab_seq):
        slab_of[ka] = i

    f1_toks = [None] * FT
    f2_toks = [None] * FT
    sg_toks = [None] * FT
    hd_toks = [None] * FT
    f3_toks = [None] * (NST * DT)
    o3_toks = [None] * (NST * DT)
    od_toks = [None] * (NST * DT)

    def reg_slab(i):
        kind, arg = slab_seq[i]
        if kind == 0:
            src = w1s_d[arg]
            nk, bias_i = DT, 0
        elif kind == 1:
            src = w2s_d[arg]
            nk, bias_i = DT, 1
        else:
            sti, d = arg
            src = w3s_d[sti, d]
            nk, bias_i = SFT, 2

        def em_l(sy, src=src, i=i, nk=nk):
            return sy.dma_start(out=wraw[:, i % 3, :nk, :], in_=src)
        w = list(p2b_end) if i < 3 else [sl_q[i - 3]]
        sl_load[i] = p.step("sync", em_l, waits=w, inc="dY", inc_by=16)

        def em_q(sc, i=i, nk=nk, bias_i=bias_i):
            sc.activation(sa1[:, :nk, :], wraw[:, i % 3, :nk, :], AF.Sign,
                          bias=thr[:, bias_i:bias_i + 1], scale=1.0)
            sc.drain()
            sc.activation(sb1[:, :nk, :], wraw[:, i % 3, :nk, :], AF.Sign,
                          bias=thrn[:, bias_i:bias_i + 1], scale=1.0)
            sc.drain()
            return sc.nop()
        w = [sl_load[i], t_te]
        if i >= 1:
            w.append(sl_a[i - 1])
        sl_q[i] = p.step("scalar", em_q, waits=w, inc="sc")

        if kind == 0:
            dst, ridx, nk2 = wq1, arg % 2, DT
        elif kind == 1:
            dst, ridx, nk2 = wq2, arg % 2, DT
        else:
            dst, ridx, nk2 = wq3, arg[1] % 2, SFT

        def em_a(v, dst=dst, ridx=ridx, nk2=nk2):
            v.tensor_tensor(dst[:, ridx, :nk2, :], sa1[:, :nk2, :],
                            sb1[:, :nk2, :], op=AL.add)
            v.drain()
            return v.nop()
        w = [sl_q[i]]
        if kind == 0 and arg >= 2:
            w.append(f1_toks[arg - 2])
        if kind == 1 and arg >= 2:
            w.append(f2_toks[arg - 2])
        if kind == 2:
            u3 = arg[0] * DT + arg[1]
            if u3 >= 2:
                w.append(f3_toks[u3 - 2])
        sl_a[i] = p.step("vector", em_a, waits=w, inc="v")

    def reg_slab_upto(i):
        while reg_ptr[0] <= min(i, NSLAB - 1):
            reg_slab(reg_ptr[0])
            reg_ptr[0] += 1

    reg_ptr = [0]
    reg_slab_upto(1)
    for sti in range(NST):
        for fi in range(SFT):
            f = sti * SFT + fi
            if fi + 1 < SFT:
                reg_slab_upto(slab_of[(1, f + 1)])
            else:
                reg_slab_upto(slab_of[(2, (sti, 0))])

            def em_f1(t, f=f):
                last = None
                for kt in range(DT):
                    for ch in range(NCH):
                        last = t.matmul(pf1[:, 512 * ch:512 * (ch + 1)],
                                        wq1[:, f % 2, kt, :],
                                        xnT[:, kt, 512 * ch:512 * (ch + 1)],
                                        start=(kt == 0), stop=(kt == DT - 1))
                return last
            w = [sl_a[slab_of[(0, f)]]]
            if f >= 1:
                w.append(sg_toks[f - 1])
            if fi == 0 and sti >= 1:
                w.append(o3_toks[sti * DT - 1])
                w.append(o3_toks[sti * DT - 2])
            f1_toks[f] = p.step("tensor", em_f1, waits=w, inc="t")

            def em_sg(sc, f=f):
                return sc.activation(sigb[:, f % 2, :], pf1[:], AF.Sigmoid,
                                     scale=0.5)
            w = [f1_toks[f]]
            if f >= 2:
                w.append(hd_toks[f - 2])
            sg_toks[f] = p.step("scalar", em_sg, waits=w, inc="sc")

            def em_f2(t, f=f):
                last = None
                for kt in range(DT):
                    for ch in range(NCH):
                        last = t.matmul(pf2[:, 512 * ch:512 * (ch + 1)],
                                        wq2[:, f % 2, kt, :],
                                        xnT[:, kt, 512 * ch:512 * (ch + 1)],
                                        start=(kt == 0), stop=(kt == DT - 1))
                return last
            w = [sl_a[slab_of[(1, f)]]]
            if f >= 1:
                w.append(hd_toks[f - 1])
            if fi == 0 and sti >= 1:
                w.append(o3_toks[sti * DT - 1])
            f2_toks[f] = p.step("tensor", em_f2, waits=w, inc="t")

            def em_hd(v, f=f, fi=fi):
                v.scalar_tensor_tensor(hid[:, fi, :], pf2[:], 0.5,
                                       sigb[:, f % 2, :],
                                       op0=AL.mult, op1=AL.mult)
                v.drain()
                return v.nop()
            hd_toks[f] = p.step("vector", em_hd,
                                waits=[f2_toks[f], sg_toks[f]], inc="v")

        for d in range(DT):
            u = sti * DT + d
            if d + 1 < DT:
                reg_slab_upto(slab_of[(2, (sti, d + 1))])
            elif sti + 1 < NST:
                reg_slab_upto(slab_of[(1, (sti + 1) * SFT)])

            def em_f3(t, sti=sti, d=d):
                pp = pf1 if d % 2 == 0 else pf2
                last = None
                for ki in range(SFT):
                    for ch in range(NCH):
                        last = t.matmul(pp[:, 512 * ch:512 * (ch + 1)],
                                        wq3[:, d % 2, ki, :],
                                        hid[:, ki, 512 * ch:512 * (ch + 1)],
                                        start=(ki == 0),
                                        stop=(ki == SFT - 1))
                return last
            w = [sl_a[slab_of[(2, (sti, d))]], hd_toks[sti * SFT + SFT - 1]]
            if d <= 1:
                w.append(sg_toks[sti * SFT + SFT - 1])
            if u >= 2:
                w.append(o3_toks[u - 2])
            f3_toks[u] = p.step("tensor", em_f3, waits=w, inc="t")

            def em_o3(v, d=d, u=u):
                v.tensor_copy(o3st[:, u % 2, :],
                              (pf1 if d % 2 == 0 else pf2)[:])
                v.drain()
                return v.nop()
            w = [f3_toks[u]]
            if u >= 2:
                w.append(od_toks[u - 2])
            o3_toks[u] = p.step("vector", em_o3, waits=w, inc="v")

            def em_od(gp, sti=sti, d=d, u=u):
                kw = {} if sti == 0 else {"accum_op": AL.add}
                return gp.dma_start(out=o3_scr[128 * d:128 * (d + 1), :],
                                    in_=o3st[:, u % 2, :], **kw)
            od_toks[u] = p.step("gpsimd", em_od, waits=[o3_toks[u]],
                                inc="dG", inc_by=16)

    p3_end = [od_toks[NST * DT - 1], f3_toks[NST * DT - 1],
              o3_toks[NST * DT - 1], sl_q[NSLAB - 1],
              sl_a[NSLAB - 1], f2_toks[FT - 1]] + p.tails("dX", "dY", "dG")

    p3.close()
    xnT_sc.close()
    p4 = contextlib.ExitStack()
    attnF = sbuf("attnF", [128, DT, S], BF16, p4)
    woraw = sbuf("woraw", [128, 3, DT, 128], F32, p4)
    wob = sbuf("wob", [128, 3, DT, 128], BF16, p4)
    xTt = sbuf("xTt", [128, 4, S], F32, p4)
    o3ld = sbuf("o3ld", [128, 4, S], BF16, p4)
    outA = sbuf("outA", [128, 2, S], F32, p4)
    outB = sbuf("outB", [128, 2, S], F32, p4)
    pwo = [psum("pwo%d" % i, [128, S], F32, p4) for i in range(2)]

    atv = at_scr.rearrange("(kt p) s -> p kt s", p=128)
    o3v = o3_scr.rearrange("(mt p) s -> p mt s", p=128)
    xTv = xT_d.rearrange("(mt p) s -> p mt s", p=128)

    at_toks = []
    for kt in range(DT):
        def em_atl(sc, kt=kt):
            return sc.dma_start(out=attnF[:, kt, :], in_=atv[:, kt, :])
        w = (list(p3_end) + [sp_toks[NBLK - 1]]) if kt == 0 else []
        at_toks.append(p.step("scalar", em_atl, waits=w, inc="dX", inc_by=16))
    tok_atF = at_toks[-1]

    wol_toks = [None] * DT
    woc_toks = [None] * DT
    wo_toks = [None] * DT
    xl_toks = [None] * DT
    ol_toks = [None] * DT
    fA_toks = [None] * DT
    fB_toks = [None] * DT
    out_toks = [None] * DT

    for mt in range(DT):
        def em_wol(gp, mt=mt):
            return gp.dma_start(out=woraw[:, mt % 3, :, :], in_=wos_d[mt])
        w = list(p3_end) if mt == 0 else []
        if mt >= 3:
            w.append(woc_toks[mt - 3])
        wol_toks[mt] = p.step("gpsimd", em_wol, waits=w, inc="dG", inc_by=16)

        def em_woc(v, mt=mt):
            v.tensor_copy(wob[:, mt % 3, :, :], woraw[:, mt % 3, :, :])
            v.drain()
            return v.nop()
        w = [wol_toks[mt]] + (list(p3_end) if mt == 0 else [])
        if mt >= 3:
            w.append(wo_toks[mt - 3])
        woc_toks[mt] = p.step("vector", em_woc, waits=w, inc="v")

        def em_xl(sy, mt=mt):
            return sy.dma_start(out=xTt[:, mt % 4, :], in_=xTv[:, mt, :])
        w = [fA_toks[mt - 4]] if mt >= 4 else list(p3_end)
        xl_toks[mt] = p.step("sync", em_xl, waits=w, inc="dY", inc_by=16)

        def em_ol(sy, mt=mt):
            return sy.dma_start(out=o3ld[:, mt % 4, :], in_=o3v[:, mt, :])
        w = [od_toks[(NST - 1) * DT + mt]]
        if mt >= 4:
            w.append(fB_toks[mt - 4])
        else:
            w += list(p3_end)
        ol_toks[mt] = p.step("sync", em_ol, waits=w, inc="dY", inc_by=16)

        def em_wo(t, mt=mt):
            last = None
            for kt in range(DT):
                for ch in range(NCH):
                    last = t.matmul(pwo[mt % 2][:, 512 * ch:512 * (ch + 1)],
                                    wob[:, mt % 2, kt, :],
                                    attnF[:, kt, 512 * ch:512 * (ch + 1)],
                                    start=(kt == 0), stop=(kt == DT - 1))
            return last
        w = [woc_toks[mt]] + at_toks
        if mt >= 2:
            w.append(fA_toks[mt - 2])
        wo_toks[mt] = p.step("tensor", em_wo, waits=w, inc="t")

        def em_fA(v, mt=mt):
            v.scalar_tensor_tensor(outA[:, mt % 2, :], pwo[mt % 2][:], 1.0,
                                   xTt[:, mt % 4, :], op0=AL.mult, op1=AL.add)
            v.drain()
            return v.nop()
        w = [wo_toks[mt], xl_toks[mt]]
        if mt >= 2:
            w.append(fB_toks[mt - 2])
        fA_toks[mt] = p.step("vector", em_fA, waits=w, inc="v")

        def em_fB(v, mt=mt):
            v.scalar_tensor_tensor(outB[:, mt % 2, :], o3ld[:, mt % 4, :],
                                   0.5, outA[:, mt % 2, :],
                                   op0=AL.mult, op1=AL.add)
            v.drain()
            return v.nop()
        w = [fA_toks[mt], ol_toks[mt]]
        if mt >= 2:
            w.append(out_toks[mt - 2])
        fB_toks[mt] = p.step("vector", em_fB, waits=w, inc="v")

        def em_out(sc, mt=mt):
            return sc.dma_start(out=out_d[128 * mt:128 * (mt + 1), :],
                                in_=outB[:, mt % 2, :])
        out_toks[mt] = p.step("scalar", em_out, waits=[fB_toks[mt]],
                              inc="dX", inc_by=16)

    p.step("scalar", lambda sc: None, waits=[out_toks[DT - 1]])
    p.step("sync", lambda sy: None, waits=[sl_load[NSLAB - 1]])
    p.step("gpsimd", lambda gp: None, waits=[od_toks[NST * DT - 1]])

    with nc.Block() as blk:
        p.emit(blk)
    p4.close()

    return nc


_CACHED = {}


def _get_nc():
    if "nc" not in _CACHED:
        _CACHED["nc"] = build_program()
    return _CACHED["nc"]


def kernel(L0_attn_x, L0_attn_Wq, L0_attn_Wk, L0_attn_Wv, L0_attn_Wo,
           L0_ffn_w1, L0_ffn_w2, L0_ffn_w3):
    B = L0_attn_x.shape[0]
    nc = _get_nc()

    def C(a):
        return np.ascontiguousarray(np.asarray(a), dtype=np.float32)

    D = L0_attn_x.shape[2]
    S = L0_attn_x.shape[1]
    H = L0_attn_Wq.shape[0]
    F = L0_ffn_w1.shape[0]
    DT, FT, NVE = D // 128, F // 128, D // 512
    NST = 4 if FT % 4 == 0 else (2 if FT % 2 == 0 else 1)
    SFT = FT // NST
    wqs = C(np.asarray(L0_attn_Wq).T.reshape(DT, 128, H).transpose(1, 0, 2))
    wks = C(np.asarray(L0_attn_Wk).T.reshape(DT, 128, H).transpose(1, 0, 2))
    wvs = C(np.asarray(L0_attn_Wv).T.reshape(DT, 128, NVE, 512)
            .transpose(2, 1, 0, 3))
    wos = C(np.asarray(L0_attn_Wo).T.reshape(DT, 128, DT, 128)
            .transpose(2, 1, 0, 3))
    w1s = C(np.asarray(L0_ffn_w1).T.reshape(DT, 128, FT, 128)
            .transpose(2, 1, 0, 3))
    w2s = C(np.asarray(L0_ffn_w2).T.reshape(DT, 128, FT, 128)
            .transpose(2, 1, 0, 3))
    w3s = C(np.asarray(L0_ffn_w3).T.reshape(NST, SFT, 128, DT, 128)
            .transpose(0, 3, 2, 1, 4))

    in_maps = []
    for b in range(B):
        xb = C(L0_attn_x[b])
        in_maps.append({
            "x_tok": xb, "xT": C(xb.T),
            "wqs": wqs, "wks": wks, "wvs": wvs, "wos": wos,
            "w1s": w1s, "w2s": w2s, "w3s": w3s,
        })

    res = run_bass_kernel_spmd(nc, in_maps, core_ids=list(range(B)))
    out = np.empty((B, L0_attn_x.shape[1], L0_attn_x.shape[2]),
                   dtype=np.float32)
    for b in range(B):
        out[b] = np.asarray(res.results[b]["out"]).T
    return out


# version 21
# speedup vs baseline: 1.2357x; 1.0731x over previous
import numpy as np
import contextlib
import concourse.bass as bass
import concourse.mybir as mybir
from concourse.bass_utils import run_bass_kernel_spmd
from concourse import library_config

F32 = mybir.dt.float32
BF16 = mybir.dt.bfloat16
U16 = mybir.dt.uint16
I16 = mybir.dt.int16
AL = mybir.AluOpType
AF = mybir.ActivationFunctionType
AX = mybir.AxisListType

BIGC = float(2 ** 23)
NEG_BIG = -1e30
ENGINES = ("sync", "scalar", "vector", "tensor", "gpsimd")


class Prog:

    def __init__(self, nc):
        self.nc = nc
        self.steps = {e: [] for e in ENGINES}
        self.sems = {}
        self.counts = {}
        self.last_dma = {}
        self._es = contextlib.ExitStack()

    def sem(self, name):
        if name not in self.sems:
            self.sems[name] = self._es.enter_context(self.nc.semaphore(name))
            self.counts[name] = 0
        return self.sems[name]

    NSUB = 6

    def step(self, engine, emit, waits=(), inc=None, inc_by=1):
        tok = None
        waits = [w for w in waits if w is not None]
        if inc is not None:
            if inc_by == 16:
                jname = inc + "#j"
                j = self.counts.get(jname, 0)
                self.counts[jname] = j + 1
                sub = "%s%d" % (inc, j % self.NSUB)
                self.sem(sub)
                self.counts[sub] += 16
                if self.counts[sub] > 16:
                    waits.append((sub, self.counts[sub] - 16))
                tok = (sub, self.counts[sub], 16)
                self.last_dma.setdefault(inc, {})[sub] = tok
            else:
                self.sem(inc)
                self.counts[inc] += inc_by
                tok = (inc, self.counts[inc], inc_by)
        self.steps[engine].append((tuple(waits), emit, tok))
        return tok

    def tails(self, *queues):
        out = []
        for q in queues:
            out.extend(self.last_dma.get(q, {}).values())
        return out

    def emit(self, block):
        for engine in ENGINES:
            steps = self.steps[engine]
            if not steps:
                continue

            def body(eng, engine_steps=steps):
                waited = {}
                for waits, emit_fn, tok in engine_steps:
                    for w in waits:
                        sname, val = w[0], w[1]
                        if waited.get(sname, -1) < val:
                            eng.wait_ge(self.sems[sname], val)
                            waited[sname] = val
                    last = emit_fn(eng)
                    if tok is not None:
                        if last is None:
                            last = eng.nop()
                        last.then_inc(self.sems[tok[0]], tok[2])

            getattr(block, engine)(body)


def build_program(S=2048, F=5632):
    D, H = 2048, 128
    NT = S // 128
    DT = D // 128
    FT = F // 128
    NCH = S // 512
    G = H // 2
    GB = 8
    NBLK = G // GB
    NST = 4 if FT % 4 == 0 else (2 if FT % 2 == 0 else 1)
    SFT = FT // NST
    n_w_elems = float(F * D)

    nc = bass.Bass("TRN2", target_bir_lowering=False)
    p = Prog(nc)

    NVE_ = D // 512
    NST_ = 4 if FT % 4 == 0 else (2 if FT % 2 == 0 else 1)
    SFT_ = FT // NST_
    x_tok = nc.dram_tensor("x_tok", [S, D], F32, kind="ExternalInput").ap()
    xT_d = nc.dram_tensor("xT", [D, S], F32, kind="ExternalInput").ap()
    wqs_d = nc.dram_tensor("wqs", [128, DT, H], F32, kind="ExternalInput").ap()
    wks_d = nc.dram_tensor("wks", [128, DT, H], F32, kind="ExternalInput").ap()
    wvs_d = nc.dram_tensor("wvs", [NVE_, 128, DT, 512], F32,
                           kind="ExternalInput").ap()
    wos_d = nc.dram_tensor("wos", [DT, 128, DT, 128], F32,
                           kind="ExternalInput").ap()
    w1s_d = nc.dram_tensor("w1s", [FT, 128, DT, 128], F32,
                           kind="ExternalInput").ap()
    w2s_d = nc.dram_tensor("w2s", [FT, 128, DT, 128], F32,
                           kind="ExternalInput").ap()
    w3s_d = nc.dram_tensor("w3s", [NST_, DT, 128, SFT_, 128], F32,
                           kind="ExternalInput").ap()
    out_d = nc.dram_tensor("out", [D, S], F32, kind="ExternalOutput").ap()
    at_scr = nc.dram_tensor("at_scr", [D, S], BF16, kind="Internal").ap()
    o3_scr = nc.dram_tensor("o3_scr", [D, S], BF16, kind="Internal").ap()
    v_scr = [nc.dram_tensor("v_scr%d" % c, [S, 512], BF16,
                            kind="Internal").ap() for c in range(NVE_)]

    es = contextlib.ExitStack()

    def sbuf(name, shape, dtype, stack=None, side=None):
        kw = {} if side is None else {"side": side}
        return (stack or es).enter_context(
            nc.sbuf_tensor(name, shape, dtype, **kw))

    def psum(name, shape, dtype=F32, stack=None):
        return (stack or es).enter_context(
            nc.psum_tensor(name, shape, dtype))

    xnT_sc = contextlib.ExitStack()
    xnT = sbuf("xnT", [128, DT, S], BF16, xnT_sc, side="right")
    ident = sbuf("ident", [128, 128], BF16)
    iota_r = sbuf("iota_r", [128, 128], F32)
    iota_g = sbuf("iota_g", [128, G], F32)
    iota_rp = sbuf("iota_rp", [128, 128], F32)
    hcol = sbuf("hcol", [128, 1], F32)
    hcol_i = sbuf("hcol_i", [128, 1], mybir.dt.int32)
    hbit_i = sbuf("hbit_i", [128, 1], mybir.dt.int32)
    hm2 = sbuf("hm2", [128, 1], F32)
    h2col = sbuf("h2col", [128, 1], F32)
    rlt64 = sbuf("rlt64", [128, 1], F32)
    nt0 = sbuf("nt0", [128, 1], F32)
    h31col = sbuf("h31col", [128, 1], F32)
    biascol = sbuf("biascol", [128, 1], F32)
    Gm = sbuf("Gm", [128, G], BF16)
    Pm = sbuf("Pm", [128, 128], BF16)
    csr = sbuf("csr", [128, 128], F32)
    KV2 = sbuf("KV2", [128, 128], BF16)
    CP2 = sbuf("CP2", [128, 128], BF16)
    gam_cols = sbuf("gam_cols", [128, 192], F32)
    gam_acc = sbuf("gam_acc", [128, 3], F32)
    gam_sc = sbuf("gam_sc", [1, 3], F32)
    thr_sc = sbuf("thr_sc", [1, 3], F32)
    ones_row = sbuf("ones_row", [1, 128], F32)
    thr = sbuf("thr", [128, 3], F32)
    thrn = sbuf("thrn", [128, 3], F32)
    qb = sbuf("qb", [128, S], BF16)
    gq = sbuf("gq", [128, S], BF16)
    kvals = sbuf("kvals", [128, 64], F32)
    Vsel = sbuf("Vsel", [128, G, 34], BF16)
    idxb0 = sbuf("idxb0", [64, G], mybir.dt.uint32)
    idxb1 = sbuf("idxb1", [64, G], mybir.dt.uint32)

    gsc = contextlib.ExitStack()
    GCW = 2048
    gslab = sbuf("gslab", [128, 3, GCW], F32, gsc)
    gdum = sbuf("gdum", [128, GCW], BF16, gsc)

    gchunks = []
    nper = []
    for wd in (w1s_d, w2s_d, w3s_d):
        flat = wd.rearrange("a b c d -> (a b c d)") if len(wd.shape) == 4 \
            else wd.rearrange("a b c d e -> (a b c d e)")
        nch_ = flat.shape[0] // (128 * GCW)
        assert nch_ * 128 * GCW == flat.shape[0]
        chv = flat.rearrange("(n p q) -> n p q", p=128, q=GCW)
        for i_ in range(nch_):
            gchunks.append((chv[i_], GCW))
        nper.append(nch_)
    NGC = len(gchunks)
    assert NGC <= 192
    r1 = nper[0]
    r2 = nper[0] + nper[1]

    gam_toks = [None] * NGC
    gslab_toks = [None] * NGC

    def reg_gamma(i, eng="scalar"):
        ap, nel = gchunks[i]

        def em_dma(sy, ap=ap, nel=nel, i=i):
            return sy.dma_start(out=gslab[:, i % 3, :nel], in_=ap)
        w = [gam_toks[i - 3]] if i >= 3 else []
        gslab_toks[i] = p.step("sync", em_dma, waits=w, inc="dY", inc_by=16)

        if eng == "scalar":
            def em_abs(sc, i=i, nel=nel):
                sc.activation(gdum[:, :nel], gslab[:, i % 3, :nel],
                              AF.Abs, accum_out=gam_cols[:, i:i + 1])
                sc.drain()
                return sc.nop()
            gam_toks[i] = p.step("scalar", em_abs, waits=[gslab_toks[i]],
                                 inc="sGa")
        else:
            def em_absv(v, i=i, nel=nel):
                v.tensor_reduce(gam_cols[:, i:i + 1],
                                gslab[:, i % 3, :nel], axis=AX.X,
                                op=AL.add, apply_absolute_value=True)
                v.drain()
                return v.nop()
            gam_toks[i] = p.step("vector", em_absv, waits=[gslab_toks[i]],
                                 inc="v")

    def em_iotas(gp):
        gp.iota(iota_r[:], pattern=[[1, 128]], base=0, channel_multiplier=0,
                allow_small_or_imprecise_dtypes=True)
        gp.iota(iota_g[:], pattern=[[1, G]], base=0, channel_multiplier=0,
                allow_small_or_imprecise_dtypes=True)
        gp.iota(iota_rp[:], pattern=[[1, 2], [0, 64]], base=0,
                channel_multiplier=0, allow_small_or_imprecise_dtypes=True)
        gp.iota(hcol[:], pattern=[[0, 1]], base=0, channel_multiplier=1,
                allow_small_or_imprecise_dtypes=True)
        gp.iota(hcol_i[:], pattern=[[0, 1]], base=0, channel_multiplier=1)
        gp.iota(csr[:], pattern=[[0, 2], [1, 2], [0, 32]], base=0,
                channel_multiplier=0, allow_small_or_imprecise_dtypes=True)
        gp.memset(gam_cols[:], 0.0)
        gp.memset(ones_row[:], 1.0)
        gp.drain()
        return gp.nop()
    tok_iota = p.step("gpsimd", em_iotas, inc="gp")

    def em_ident(v):
        v.tensor_scalar(ident[:], iota_r[:], hcol[:], None, op0=AL.is_equal)
        v.drain()
        return v.nop()
    tok_ident = p.step("vector", em_ident, waits=[tok_iota], inc="v")

    def em_masks(v):
        v.tensor_scalar(hbit_i[:], hcol_i[:], 1, None, op0=AL.bitwise_and)
        v.drain()
        v.tensor_copy(hm2[:], hbit_i[:])
        v.drain()
        v.tensor_tensor(h2col[:], hcol[:], hm2[:], op=AL.subtract)
        v.drain()
        v.tensor_scalar(h2col[:], h2col[:], 0.5, None, op0=AL.mult)
        v.drain()
        v.tensor_scalar(Gm[:], iota_g[:], h2col[:], None, op0=AL.is_equal)
        v.drain()
        v.tensor_scalar(Pm[:], iota_rp[:], hm2[:], None, op0=AL.is_equal)
        v.drain()
        v.tensor_scalar(csr[:], csr[:], -2.0 * BIGC, BIGC,
                        op0=AL.mult, op1=AL.add)
        v.drain()
        v.tensor_tensor(CP2[:], csr[:], Pm[:], op=AL.mult)
        v.drain()
        v.tensor_scalar(rlt64[:], hcol[:], 64.0, None, op0=AL.is_lt)
        v.drain()
        v.tensor_scalar(hbit_i[:], hcol_i[:], 31, None, op0=AL.bitwise_and)
        v.drain()
        v.tensor_copy(h31col[:], hbit_i[:])
        v.drain()
        v.tensor_scalar(hbit_i[:], hcol_i[:], 32, None, op0=AL.bitwise_and)
        v.drain()
        v.tensor_copy(nt0[:], hbit_i[:])
        v.drain()
        v.tensor_scalar(nt0[:], nt0[:], 0.0, None, op0=AL.is_equal)
        v.drain()
        v.tensor_scalar(biascol[:], nt0[:], -BIGC, None, op0=AL.mult)
        v.drain()
        v.memset(Vsel[:], 0.0)
        v.drain()
        v.tensor_scalar(Vsel[:, :, 32:33],
                        rlt64[:, None, :].broadcast_to((128, G, 1)),
                        1.0, None, op0=AL.mult)
        v.drain()
        v.tensor_scalar(Vsel[:, :, 33:34],
                        rlt64[:, None, :].broadcast_to((128, G, 1)),
                        -1.0, 1.0, op0=AL.mult, op1=AL.add)
        v.drain()
        return v.nop()
    tok_masks = p.step("vector", em_masks, inc="v")

    p1 = contextlib.ExitStack()
    XR = 2
    x_tile = sbuf("x_tile", [128, XR, D], F32, p1)
    xn_stage = sbuf("xn_stage", [128, XR, D], BF16, p1)
    stats = sbuf("stats", [128, XR, 24], F32, p1)
    mv = sbuf("mv", [128, XR, 2], F32, p1)
    veps = sbuf("veps", [128, XR, 1], F32, p1)
    rs1 = sbuf("rs1", [128, XR, 1], F32, p1)
    nt1 = sbuf("nt1", [128, XR, 1], F32, p1)
    pxn = [psum("pxn%d" % i, [128, DT, 128], BF16, p1) for i in range(2)]

    p2a = contextlib.ExitStack()
    U32 = mybir.dt.uint32
    wvs2 = sbuf("wvs2", [128, 2, DT, 512], BF16, p2a)
    vtile = sbuf("vtile", [128, 4, 512], BF16, p2a)
    keysw = sbuf("keysw", [128, 3, S], F32, p2a)
    wqTb = sbuf("wqTb", [128, DT, H], BF16, p2a)
    wkTb = sbuf("wkTb", [128, DT, H], BF16, p2a)
    idx = sbuf("idx", [128, 64], U16, p2a)
    idxcf = sbuf("idxcf", [128, 64], F32, p2a)
    kvn = sbuf("kvn", [128, 32], F32, p2a)
    identF = sbuf("identF", [128, 128], F32, p2a)
    RING = 3
    pqk = [psum("pqk%d" % i, [128, 512], F32, p2a) for i in range(RING)]
    pidx = psum("pidx", [64, 128], F32, p2a)

    vflatc = [vc.rearrange("s (hh cc) -> (s hh) cc", cc=16) for vc in v_scr]

    t_idf = p.step("vector",
                   lambda v: v.tensor_scalar(identF[:], iota_r[:], hcol[:],
                                             None, op0=AL.is_equal),
                   waits=[tok_iota], inc="v")

    gamma_ptr = [0]

    def pump_gamma(n, eng="scalar"):
        while gamma_ptr[0] < min(NGC, n):
            reg_gamma(gamma_ptr[0], eng)
            gamma_ptr[0] += 1

    xn_toks = [None] * NT
    xload_toks = [None] * NT
    xpose_toks = [None] * NT
    xev_toks = [None] * NT
    sqrt_toks = [None] * NT

    mm_evs = []
    NQK = 2 * NCH
    NVE = D // 512
    qk_ev = [None] * NQK
    vmm_toks = [[None] * NT for _ in range(NVE)]
    vsp_flat = []
    vsub_toks = [[None] * 4 for _ in range(NVE)]
    cqk = [None, None]

    def reg_vsub(ech, q, extra=()):
        def em(gp, ech=ech, q=q):
            return gp.dma_start(out=wvs2[:, ech % 2, 4 * q:4 * (q + 1), :],
                                in_=wvs_d[ech][:, 4 * q:4 * (q + 1), :])
        vsub_toks[ech][q] = p.step("gpsimd", em, waits=list(extra),
                                   inc="dG", inc_by=16)

    def reg_qk(wi, ch):
        u01 = wi * NCH + ch
        k = len(mm_evs)
        slot = k % RING

        def em_mm(t, wi=wi, ch=ch, slot=slot):
            wt = wqTb if wi == 0 else wkTb
            last = None
            for kt in range(DT):
                last = t.matmul(pqk[slot][:], wt[:, kt, :],
                                xnT[:, kt, 512 * ch:512 * (ch + 1)],
                                start=(kt == 0), stop=(kt == DT - 1))
            return last
        w = [cqk[wi], xev_toks[4 * ch + 3]]
        if k >= RING:
            w.append(mm_evs[k - RING])
        tok_mm = p.step("tensor", em_mm, waits=w, inc="t")

        if wi == 0:
            def em_ev(v, ch=ch, slot=slot):
                v.tensor_copy(qb[:, 512 * ch:512 * (ch + 1)], pqk[slot][:])
                v.drain()
                v.tensor_scalar(gq[:, 512 * ch:512 * (ch + 1)],
                                pqk[slot][:], 0.0, None, op0=AL.is_gt)
                v.drain()
                return v.nop()
        else:
            def em_ev(v, ch=ch, slot=slot):
                v.tensor_copy(keysw[:, 0, 512 * ch:512 * (ch + 1)],
                              pqk[slot][:])
                v.drain()
                return v.nop()
        qk_ev[u01] = p.step("vector", em_ev, waits=[tok_mm], inc="v")
        mm_evs.append(qk_ev[u01])

    def reg_vunit(ech, mt):
        k = len(mm_evs)
        slot = k % RING
        vu = len(vsp_flat)
        vslot = vu % 4

        def em_vmm(t, ech=ech, mt=mt, slot=slot):
            last = None
            for kt in range(DT):
                last = t.matmul(pqk[slot][:],
                                xnT[:, kt, 128 * mt:128 * (mt + 1)],
                                wvs2[:, ech % 2, kt, :],
                                start=(kt == 0), stop=(kt == DT - 1))
            return last
        w = list(vsub_toks[ech]) + [xev_toks[mt]]
        if k >= RING:
            w.append(mm_evs[k - RING])
        vmm_toks[ech][mt] = p.step("tensor", em_vmm, waits=w, inc="t")

        def em_vev(v, slot=slot, vslot=vslot):
            v.tensor_copy(vtile[:, vslot, :], pqk[slot][:])
            v.drain()
            return v.nop()
        w = [vmm_toks[ech][mt]]
        if vu >= 4:
            w.append(vsp_flat[vu - 4])
        tok_ev = p.step("vector", em_vev, waits=w, inc="v")
        mm_evs.append(tok_ev)

        def em_vsp(sc, ech=ech, mt=mt, vslot=vslot):
            return sc.dma_start(
                out=v_scr[ech][128 * mt:128 * (mt + 1), :],
                in_=vtile[:, vslot, :])
        vsp_flat.append(p.step("scalar", em_vsp, waits=[tok_ev],
                               inc="dX", inc_by=16))

    for st in range(NT):
        b = st % XR

        def em_xload(gp, st=st, b=b):
            return gp.dma_start(out=x_tile[:, b, :],
                                in_=x_tok[128 * st:128 * (st + 1), :])
        w = [xn_toks[st - XR]] if st >= XR else []
        xload_toks[st] = p.step("gpsimd", em_xload, waits=w, inc="dG",
                                inc_by=16)
        if st == 1:
            cqk[0] = p.step(
                "gpsimd",
                lambda gp: gp.dma_start(out=wqTb[:], in_=wqs_d[:]),
                inc="dG", inc_by=16)
            cqk[1] = p.step(
                "gpsimd",
                lambda gp: gp.dma_start(out=wkTb[:], in_=wks_d[:]),
                inc="dG", inc_by=16)
        elif st in (2, 3):
            reg_vsub(0, 2 * (st - 2))
            reg_vsub(0, 2 * (st - 2) + 1)
        elif 4 <= st <= 7:
            reg_vsub(1, st - 4)

        def em_stats(v, b=b):
            for c in range(4):
                v.bn_stats(stats[:, b, 6 * c:6 * c + 6],
                           x_tile[:, b, 512 * c:512 * (c + 1)])
            v.drain()
            v.bn_aggr(mv[:, b, :], stats[:, b, :])
            v.drain()
            v.tensor_scalar(veps[:, b, :], mv[:, b, 1:2], 1e-5, None,
                            op0=AL.add)
            v.drain()
            return v.nop()
        t_stats = p.step("vector", em_stats, waits=[xload_toks[st]], inc="v")

        def em_sqrt(sc, b=b):
            sc.activation(nt1[:, b, :], veps[:, b, :], AF.Ln)
            sc.drain()
            sc.activation(rs1[:, b, :], nt1[:, b, :], AF.Exp, scale=-0.5)
            sc.drain()
            return sc.nop()
        sqrt_toks[st] = p.step("scalar", em_sqrt, waits=[t_stats], inc="sc")

        def em_xn(v, b=b):
            v.tensor_scalar(xn_stage[:, b, :], x_tile[:, b, :],
                            mv[:, b, 0:1], rs1[:, b, :],
                            op0=AL.subtract, op1=AL.mult)
            v.drain()
            return v.nop()
        xn_toks[st] = p.step("vector", em_xn, waits=[sqrt_toks[st]], inc="v")

        def em_tp(t, st=st, b=b):
            last = None
            for dt in range(DT):
                last = t.transpose(pxn[st % 2][:, dt, :],
                                   xn_stage[:, b, 128 * dt:128 * (dt + 1)],
                                   ident[:])
            return last
        w = [xn_toks[st], tok_ident]
        if st >= 2:
            w.append(xev_toks[st - 2])
        xpose_toks[st] = p.step("tensor", em_tp, waits=w, inc="t")

        def em_ev(v, st=st):
            v.tensor_copy(xnT[:, :, 128 * st:128 * (st + 1)], pxn[st % 2][:])
            v.drain()
            return v.nop()
        xev_toks[st] = p.step("vector", em_ev, waits=[xpose_toks[st]],
                              inc="v")
        if st % 4 == 3:
            reg_qk(0, st // 4)
            reg_qk(1, st // 4)
        if st >= 4:
            reg_vunit(0, st - 4)
        pump_gamma(st + 1)

    for ech in (0, 1):
        for q in range(4):
            if vsub_toks[ech][q] is None:
                reg_vsub(ech, q)
    for mt in range(max(0, NT - 4), NT):
        if vmm_toks[0][mt] is None:
            reg_vunit(0, mt)

    seq_top = [0, 1, 2, 1]
    seq_bot = [1, 2, 1, 2]

    def mk_top(r):
        def em(v, r=r):
            src = keysw[:, seq_top[r], :]
            v.max(kvals[:, 8 * r:8 * r + 8], src)
            v.drain()
            v.max_index(idx[:, 8 * r:8 * r + 8], kvals[:, 8 * r:8 * r + 8],
                        src)
            if r < 3:
                v.match_replace(keysw[:, seq_top[r + 1], :],
                                kvals[:, 8 * r:8 * r + 8], src, NEG_BIG)
            v.drain()
            return v.nop()
        return em

    def em_negk(v):
        v.tensor_scalar(keysw[:, 1, :], keysw[:, 0, :], -1.0, None,
                        op0=AL.mult)
        v.drain()
        return v.nop()

    def mk_bot(r):
        def em(v, r=r):
            src = keysw[:, seq_bot[r], :]
            v.max(kvn[:, 8 * r:8 * r + 8], src)
            v.drain()
            v.max_index(idx[:, 32 + 8 * r:32 + 8 * r + 8],
                        kvn[:, 8 * r:8 * r + 8], src)
            if r < 3:
                v.match_replace(keysw[:, seq_bot[r + 1], :],
                                kvn[:, 8 * r:8 * r + 8], src, NEG_BIG)
            v.drain()
            return v.nop()
        return em

    def em_negv(v):
        v.tensor_scalar(kvals[:, 32:64], kvn[:], -1.0, None, op0=AL.mult)
        v.drain()
        return v.nop()

    sel_emits = [mk_top(r) for r in range(4)] + [em_negk] \
        + [mk_bot(r) for r in range(4)] + [em_negv]
    sel_ptr = [0]
    sel_tok = [None]

    def pump_sel(n=1):
        for _ in range(n):
            if sel_ptr[0] >= len(sel_emits):
                return
            sel_tok[0] = p.step("vector", sel_emits[sel_ptr[0]], inc="v")
            sel_ptr[0] += 1

    GTAIL = 48
    for mt in range(NT):
        reg_vunit(1, mt)
        pump_sel(1)
        pump_gamma(min(GTAIL, gamma_ptr[0] + 1))
    pump_sel(len(sel_emits))
    tok_sel = sel_tok[0]

    def em_idxc(v):
        v.tensor_copy(idxcf[:], idx[:])
        v.drain()
        v.tensor_scalar(idxcf[:], idxcf[:], 32.0, h31col[:],
                        op0=AL.mult, op1=AL.add)
        v.drain()
        return v.nop()
    t_ic = p.step("vector", em_idxc, waits=[tok_sel], inc="v")

    def em_kv2(v):
        v.tensor_tensor(KV2.ap().rearrange("p (two j) -> p two j", j=64),
                        kvals[:, None, :].broadcast_to((128, 2, 64)),
                        Pm.ap().rearrange("p (two j) -> p two j", j=64),
                        op=AL.mult)
        v.drain()
        return v.nop()
    tok_kv2 = p.step("vector", em_kv2, waits=[tok_sel, tok_masks], inc="v")

    for q2 in range(4):
        reg_vsub(2, q2, extra=[vmm_toks[0][NT - 1]])
    for mt in range(NT):
        reg_vunit(2, mt)
        pump_gamma(min(GTAIL, gamma_ptr[0] + 1))

    t_it = p.step("tensor",
                  lambda t: t.transpose(pidx[:], idxcf[:], identF[:]),
                  waits=[t_ic, t_idf], inc="t")

    def em_ib(v):
        v.tensor_copy(idxb0[:], pidx[:, 0::2])
        v.drain()
        v.tensor_copy(idxb1[:], pidx[:, 1::2])
        v.drain()
        return v.nop()
    t_ib = p.step("vector", em_ib, waits=[t_it], inc="v")

    for q2 in range(4):
        reg_vsub(3, q2, extra=[vmm_toks[1][NT - 1]])
    for mt in range(NT):
        reg_vunit(3, mt)
        pump_gamma(min(GTAIL, gamma_ptr[0] + 1))

    GPC = G // NVE
    gv_toks = [None] * (2 * G)
    for g in range(G):
        c = g // GPC
        spw = vsp_flat[c * NT + max(0, NT - p.NSUB):(c + 1) * NT]

        def em_g0(gp, g=g, c=c):
            return gp.indirect_dma_start(
                out=Vsel[0:64, g, 0:16], out_offset=None, in_=vflatc[c],
                in_offset=bass.IndirectOffsetOnAxis(ap=idxb0[:, g:g + 1],
                                                    axis=0))
        gv_toks[2 * g] = p.step("gpsimd", em_g0,
                                waits=[t_ib, tok_masks] + spw,
                                inc="dG", inc_by=16)

        def em_g1(gp, g=g, c=c):
            return gp.indirect_dma_start(
                out=Vsel[64:128, g, 16:32], out_offset=None, in_=vflatc[c],
                in_offset=bass.IndirectOffsetOnAxis(ap=idxb1[:, g:g + 1],
                                                    axis=0))
        gv_toks[2 * g + 1] = p.step("gpsimd", em_g1, waits=[gv_toks[2 * g]],
                                    inc="dG", inc_by=16)
    tok_vsel = gv_toks[2 * G - 1]

    pump_gamma(GTAIL)

    p2a_end = [tok_kv2, tok_sel, t_ic, t_it, t_ib,
               vmm_toks[NVE - 1][NT - 1], mm_evs[-1],
               qk_ev[NQK - 1], xn_toks[NT - 1], xev_toks[NT - 1],
               xpose_toks[NT - 1], sqrt_toks[NT - 1]] + p.tails("dX")

    p2a.close()
    p1.close()
    p2b = contextlib.ExitStack()
    Esl = sbuf("Esl", [128, 2, GB, S], BF16, p2b)
    Lblk = sbuf("Lblk", [128, 2, GB, 128], BF16, p2b)
    Cblk = sbuf("Cblk", [128, 2, GB, 128], BF16, p2b)
    a_sE = sbuf("a_sE", [128, NT, 32 * GB], BF16, p2b)
    aT_slab = sbuf("aT_slab", [128, 2, NT, 128], BF16, p2b)
    rden = sbuf("rden", [128, GB, 2], F32, p2b)
    SCW = min(1024, S)
    NHF = S // SCW
    psc = [psum("psc%d" % i, [128, SCW], F32, p2b) for i in range(2)]
    pav = [psum("pav%d" % i, [128, GB, 34], F32, p2b) for i in range(2)]
    pat = [psum("pat%d" % i, [128, NT // 2, 128], BF16, p2b)
           for i in range(2)]

    lb_toks = [None] * NBLK
    sc_toks = [[None] * (NHF * GB) for _ in range(NBLK)]
    ex_toks = [[None] * (NHF * GB) for _ in range(NBLK)]
    av_toks = [[None] * NT for _ in range(NBLK)]
    dv_toks = [[None] * NT for _ in range(NBLK)]
    ta_toks = [[None] * 4 for _ in range(NBLK)]
    ae_toks = [[None] * 4 for _ in range(NBLK)]
    sp_toks = [None] * NBLK

    at_view = at_scr.rearrange("(bb eb p) (st j) -> bb p eb st j",
                               bb=NBLK, p=128, j=128)

    def reg_lb(bkt):
        def em_lb(v, b=bkt):
            gsl = slice(GB * b, GB * (b + 1))
            v.tensor_tensor(Lblk[:, b % 2, :, :],
                            KV2[:, None, :].broadcast_to((128, GB, 128)),
                            Gm[:, gsl, None].broadcast_to((128, GB, 128)),
                            op=AL.mult)
            v.drain()
            v.tensor_tensor(Cblk[:, b % 2, :, :],
                            CP2[:, None, :].broadcast_to((128, GB, 128)),
                            Gm[:, gsl, None].broadcast_to((128, GB, 128)),
                            op=AL.mult)
            v.drain()
            return v.nop()
        w = [tok_kv2] + (p2a_end if bkt == 0 else [])
        if bkt >= 2:
            w.append(sc_toks[bkt - 2][NHF * GB - 1])
        lb_toks[bkt] = p.step("vector", em_lb, waits=w, inc="v")

    reg_lb(0)
    for bkt in range(NBLK):
        if bkt + 1 < NBLK:
            reg_lb(bkt + 1)
        for gi in range(GB):
            for hf in range(NHF):
                u = (bkt * GB + gi) * NHF + hf
                uu = u % 2

                def em_sc_mm(t, b=bkt, gi=gi, hf=hf, uu=uu):
                    last = None
                    for cc in range(SCW // 512):
                        n0 = SCW * hf + 512 * cc
                        t.matmul(psc[uu][:, 512 * cc:512 * (cc + 1)],
                                 Lblk[:, b % 2, gi, :], qb[:, n0:n0 + 512],
                                 start=True, stop=False)
                        last = t.matmul(psc[uu][:, 512 * cc:512 * (cc + 1)],
                                        Cblk[:, b % 2, gi, :],
                                        gq[:, n0:n0 + 512],
                                        start=False, stop=True)
                    return last
                w = [lb_toks[bkt]]
                if u >= 2:
                    w.append(ex_toks[(u - 2) // (NHF * GB)]
                             [(u - 2) % (NHF * GB)])
                if gi == 0 and hf == 0 and bkt >= 2:
                    w.append(av_toks[bkt - 2][NT - 1])
                sc_toks[bkt][NHF * gi + hf] = p.step("tensor", em_sc_mm,
                                                     waits=w, inc="t")

                def em_exp(sc, b=bkt, gi=gi, hf=hf, uu=uu):
                    return sc.activation(
                        Esl[:, b % 2, gi, SCW * hf:SCW * (hf + 1)],
                        psc[uu][:], AF.Exp, bias=biascol[:], scale=1.0)
                ex_toks[bkt][NHF * gi + hf] = p.step(
                    "scalar", em_exp, waits=[sc_toks[bkt][NHF * gi + hf]],
                    inc="sEx")
            pump_gamma(gamma_ptr[0] + 1, "vector")

        for st in range(NT):
            def em_av(t, b=bkt, st=st):
                last = None
                for gi in range(GB):
                    last = t.matmul(pav[st % 2][:, gi, :],
                                    Esl[:, b % 2, gi,
                                        128 * st:128 * (st + 1)],
                                    Vsel[:, b * GB + gi, :],
                                    start=True, stop=True)
                return last
            hi = 2 * (bkt + 1) * GB
            w = [ex_toks[bkt][NHF * GB - 1]] + gv_toks[max(0, hi - 6):hi]
            u = bkt * NT + st
            if u >= 2:
                pb, pst = divmod(u - 2, NT)
                w.append(dv_toks[pb][pst])
            av_toks[bkt][st] = p.step("tensor", em_av, waits=w, inc="t")

            def em_dv(v, st=st):
                v.reciprocal(rden[:], pav[st % 2][:, :, 32:34])
                v.drain()
                v.tensor_tensor(
                    a_sE[:, st, :].rearrange("p (g par c) -> p g par c",
                                             g=GB, c=16),
                    pav[st % 2][:, :, 0:32].rearrange(
                        "p g (par c) -> p g par c", c=16),
                    rden[:, :, :, None].broadcast_to((128, GB, 2, 16)),
                    op=AL.mult)
                v.drain()
                return v.nop()
            dv_toks[bkt][st] = p.step("vector", em_dv,
                                      waits=[av_toks[bkt][st]], inc="v")

        HB = NT // 2
        for hb in range(2):
            for eb in range(2):
                u2 = hb * 2 + eb

                def em_ta(t, hb=hb, eb=eb, u2=u2, b=bkt):
                    last = None
                    for si in range(HB):
                        st = hb * HB + si
                        last = t.transpose(
                            pat[u2 % 2][:, si, :],
                            a_sE[:, st, 128 * eb:128 * (eb + 1)],
                            ident[:])
                    return last
                w = [dv_toks[bkt][hb * HB + HB - 1]]
                gu = bkt * 4 + u2
                if gu >= 2:
                    pb, pu2 = divmod(gu - 2, 4)
                    w.append(ae_toks[pb][pu2])
                ta_toks[bkt][u2] = p.step("tensor", em_ta, waits=w, inc="t")

                def em_ae(v, hb=hb, eb=eb, u2=u2):
                    v.tensor_copy(
                        aT_slab[:, eb, hb * HB:(hb + 1) * HB, :],
                        pat[u2 % 2][:])
                    v.drain()
                    return v.nop()
                w = [ta_toks[bkt][u2]]
                if bkt >= 1 and u2 == 0:
                    w.append(sp_toks[bkt - 1])
                ae_toks[bkt][u2] = p.step("vector", em_ae, waits=w, inc="v")

        def em_spill(sc, b=bkt):
            return sc.dma_start(out=at_view[b], in_=aT_slab[:])
        sp_toks[bkt] = p.step("scalar", em_spill,
                              waits=[ae_toks[bkt][3]],
                              inc="dX", inc_by=16)
        pump_gamma(gamma_ptr[0] + 3, "vector")

    pump_gamma(NGC, "vector")
    p2b_end = [sp_toks[NBLK - 1], gam_toks[NGC - 1],
               av_toks[NBLK - 1][NT - 1], ta_toks[NBLK - 1][3],
               ae_toks[NBLK - 1][3],
               ex_toks[NBLK - 1][NHF * GB - 1]] + p.tails("dX", "dY", "dG")

    p2b.close()
    gsc.close()
    p3 = contextlib.ExitStack()
    wraw = sbuf("wraw", [128, 3, DT, 128], F32, p3)
    sa1 = sbuf("sa1", [128, DT, 128], BF16, p3)
    sb1 = sbuf("sb1", [128, DT, 128], BF16, p3)
    wq1 = sbuf("wq1", [128, 2, DT, 128], BF16, p3)
    wq2 = sbuf("wq2", [128, 2, DT, 128], BF16, p3)
    wq3 = sbuf("wq3", [128, 2, SFT, 128], BF16, p3)
    sigb = sbuf("sigb", [128, 2, S], BF16, p3)
    hid = sbuf("hid", [128, SFT, S], BF16, p3)
    o3st = sbuf("o3st", [128, 2, S], BF16, p3)
    pth_sc = contextlib.ExitStack()
    pth = psum("pth", [128, 4], F32, pth_sc)

    def em_gred(v):
        v.tensor_reduce(gam_acc[:, 0:1], gam_cols[:, 0:r1], axis=AX.X,
                        op=AL.add)
        v.drain()
        v.tensor_reduce(gam_acc[:, 1:2], gam_cols[:, r1:r2], axis=AX.X,
                        op=AL.add)
        v.drain()
        v.tensor_reduce(gam_acc[:, 2:3], gam_cols[:, r2:r2 + (NGC - r2)],
                        axis=AX.X, op=AL.add)
        v.drain()
        return v.nop()
    t_gr = p.step("vector", em_gred, waits=[gam_toks[NGC - 1]], inc="v")

    def em_gpr(gp):
        gp.tensor_reduce(gam_sc[:], gam_acc[:], axis=AX.C, op=AL.add)
        gp.drain()
        return gp.nop()
    t_gp = p.step("gpsimd", em_gpr, waits=[t_gr], inc="gp")

    def em_thr(v):
        v.tensor_scalar(thr_sc[:], gam_sc[:], 0.5 / n_w_elems, 0.5e-8,
                        op0=AL.mult, op1=AL.add)
        v.drain()
        return v.nop()
    t_th = p.step("vector", em_thr, waits=[t_gp], inc="v")

    t_tm = p.step("tensor",
                  lambda t: t.matmul(pth[:, 0:3], ones_row[:], thr_sc[:],
                                     start=True, stop=True),
                  waits=[t_th] + p2b_end, inc="t")

    def em_thev(v):
        v.tensor_copy(thr[:], pth[:, 0:3])
        v.drain()
        v.tensor_scalar(thrn[:], pth[:, 0:3], -1.0, None, op0=AL.mult)
        v.drain()
        return v.nop()
    t_te = p.step("vector", em_thev, waits=[t_tm], inc="v")
    pth_sc.close()
    pf1 = psum("pf1", [128, S], F32, p3)
    pf2 = psum("pf2", [128, S], F32, p3)


    slab_seq = []
    for sti in range(NST):
        for fi in range(SFT):
            f = sti * SFT + fi
            slab_seq.append((0, f))
            slab_seq.append((1, f))
        for d in range(DT):
            slab_seq.append((2, (sti, d)))
    NSLAB = len(slab_seq)
    sl_load = [None] * NSLAB
    sl_q = [None] * NSLAB
    sl_a = [None] * NSLAB
    slab_of = {}
    for i, ka in enumerate(slab_seq):
        slab_of[ka] = i

    f1_toks = [None] * FT
    f2_toks = [None] * FT
    sg_toks = [None] * FT
    hd_toks = [None] * FT
    f3_toks = [None] * (NST * DT)
    o3_toks = [None] * (NST * DT)
    od_toks = [None] * (NST * DT)

    def reg_slab(i):
        kind, arg = slab_seq[i]
        if kind == 0:
            src = w1s_d[arg]
            nk, bias_i = DT, 0
        elif kind == 1:
            src = w2s_d[arg]
            nk, bias_i = DT, 1
        else:
            sti, d = arg
            src = w3s_d[sti, d]
            nk, bias_i = SFT, 2

        def em_l(sy, src=src, i=i, nk=nk):
            return sy.dma_start(out=wraw[:, i % 3, :nk, :], in_=src)
        w = list(p2b_end) if i < 3 else [sl_q[i - 3]]
        sl_load[i] = p.step("sync", em_l, waits=w, inc="dY", inc_by=16)

        def em_q(sc, i=i, nk=nk, bias_i=bias_i):
            sc.activation(sa1[:, :nk, :], wraw[:, i % 3, :nk, :], AF.Sign,
                          bias=thr[:, bias_i:bias_i + 1], scale=1.0)
            sc.drain()
            sc.activation(sb1[:, :nk, :], wraw[:, i % 3, :nk, :], AF.Sign,
                          bias=thrn[:, bias_i:bias_i + 1], scale=1.0)
            sc.drain()
            return sc.nop()
        w = [sl_load[i], t_te]
        if i >= 1:
            w.append(sl_a[i - 1])
        sl_q[i] = p.step("scalar", em_q, waits=w, inc="sc")

        if kind == 0:
            dst, ridx, nk2 = wq1, arg % 2, DT
        elif kind == 1:
            dst, ridx, nk2 = wq2, arg % 2, DT
        else:
            dst, ridx, nk2 = wq3, arg[1] % 2, SFT

        def em_a(v, dst=dst, ridx=ridx, nk2=nk2):
            v.tensor_tensor(dst[:, ridx, :nk2, :], sa1[:, :nk2, :],
                            sb1[:, :nk2, :], op=AL.add)
            v.drain()
            return v.nop()
        w = [sl_q[i]]
        if kind == 0 and arg >= 2:
            w.append(f1_toks[arg - 2])
        if kind == 1 and arg >= 2:
            w.append(f2_toks[arg - 2])
        if kind == 2:
            u3 = arg[0] * DT + arg[1]
            if u3 >= 2:
                w.append(f3_toks[u3 - 2])
        sl_a[i] = p.step("vector", em_a, waits=w, inc="v")

    def reg_slab_upto(i):
        while reg_ptr[0] <= min(i, NSLAB - 1):
            reg_slab(reg_ptr[0])
            reg_ptr[0] += 1

    reg_ptr = [0]
    atf_sc = contextlib.ExitStack()
    at_toks = []
    reg_slab_upto(1)
    for sti in range(NST):
        for fi in range(SFT):
            f = sti * SFT + fi
            if fi + 1 < SFT:
                reg_slab_upto(slab_of[(1, f + 1)])
            else:
                reg_slab_upto(slab_of[(2, (sti, 0))])

            def em_f1(t, f=f):
                last = None
                for kt in range(DT):
                    for ch in range(NCH):
                        last = t.matmul(pf1[:, 512 * ch:512 * (ch + 1)],
                                        wq1[:, f % 2, kt, :],
                                        xnT[:, kt, 512 * ch:512 * (ch + 1)],
                                        start=(kt == 0), stop=(kt == DT - 1))
                return last
            w = [sl_a[slab_of[(0, f)]]]
            if f >= 1:
                w.append(sg_toks[f - 1])
            if fi == 0 and sti >= 1:
                w.append(o3_toks[sti * DT - 1])
                w.append(o3_toks[sti * DT - 2])
            f1_toks[f] = p.step("tensor", em_f1, waits=w, inc="t")

            def em_sg(sc, f=f):
                return sc.activation(sigb[:, f % 2, :], pf1[:], AF.Sigmoid,
                                     scale=0.5)
            w = [f1_toks[f]]
            if f >= 2:
                w.append(hd_toks[f - 2])
            sg_toks[f] = p.step("scalar", em_sg, waits=w, inc="sc")

            def em_f2(t, f=f):
                last = None
                for kt in range(DT):
                    for ch in range(NCH):
                        last = t.matmul(pf2[:, 512 * ch:512 * (ch + 1)],
                                        wq2[:, f % 2, kt, :],
                                        xnT[:, kt, 512 * ch:512 * (ch + 1)],
                                        start=(kt == 0), stop=(kt == DT - 1))
                return last
            w = [sl_a[slab_of[(1, f)]]]
            if f >= 1:
                w.append(hd_toks[f - 1])
            if fi == 0 and sti >= 1:
                w.append(o3_toks[sti * DT - 1])
            f2_toks[f] = p.step("tensor", em_f2, waits=w, inc="t")

            def em_hd(v, f=f, fi=fi):
                v.scalar_tensor_tensor(hid[:, fi, :], pf2[:], 0.5,
                                       sigb[:, f % 2, :],
                                       op0=AL.mult, op1=AL.mult)
                v.drain()
                return v.nop()
            hd_toks[f] = p.step("vector", em_hd,
                                waits=[f2_toks[f], sg_toks[f]], inc="v")

        if sti == NST - 1:
            xnT_sc.close()
            attnF = sbuf("attnF", [128, DT, S], BF16, atf_sc, side="right")
            atv = at_scr.rearrange("(kt p) s -> p kt s", p=128)
            for kt in range(DT):
                def em_atl(sc, kt=kt):
                    return sc.dma_start(out=attnF[:, kt, :],
                                        in_=atv[:, kt, :])
                w = [f1_toks[FT - 1], f2_toks[FT - 1],
                     sp_toks[NBLK - 1]] if kt == 0 else []
                at_toks.append(p.step("scalar", em_atl, waits=w,
                                      inc="dX", inc_by=16))

        for d in range(DT):
            u = sti * DT + d
            if d + 1 < DT:
                reg_slab_upto(slab_of[(2, (sti, d + 1))])
            elif sti + 1 < NST:
                reg_slab_upto(slab_of[(1, (sti + 1) * SFT)])

            def em_f3(t, sti=sti, d=d):
                pp = pf1 if d % 2 == 0 else pf2
                last = None
                for ki in range(SFT):
                    for ch in range(NCH):
                        last = t.matmul(pp[:, 512 * ch:512 * (ch + 1)],
                                        wq3[:, d % 2, ki, :],
                                        hid[:, ki, 512 * ch:512 * (ch + 1)],
                                        start=(ki == 0),
                                        stop=(ki == SFT - 1))
                return last
            w = [sl_a[slab_of[(2, (sti, d))]], hd_toks[sti * SFT + SFT - 1]]
            if d <= 1:
                w.append(sg_toks[sti * SFT + SFT - 1])
            if u >= 2:
                w.append(o3_toks[u - 2])
            f3_toks[u] = p.step("tensor", em_f3, waits=w, inc="t")

            def em_o3(v, d=d, u=u):
                v.tensor_copy(o3st[:, u % 2, :],
                              (pf1 if d % 2 == 0 else pf2)[:])
                v.drain()
                return v.nop()
            w = [f3_toks[u]]
            if u >= 2:
                w.append(od_toks[u - 2])
            o3_toks[u] = p.step("vector", em_o3, waits=w, inc="v")

            def em_od(gp, sti=sti, d=d, u=u):
                kw = {} if sti == 0 else {"accum_op": AL.add}
                return gp.dma_start(out=o3_scr[128 * d:128 * (d + 1), :],
                                    in_=o3st[:, u % 2, :], **kw)
            od_toks[u] = p.step("gpsimd", em_od, waits=[o3_toks[u]],
                                inc="dG", inc_by=16)

    p3_end = [od_toks[NST * DT - 1], f3_toks[NST * DT - 1],
              o3_toks[NST * DT - 1], sl_q[NSLAB - 1],
              sl_a[NSLAB - 1], f2_toks[FT - 1]] + p.tails("dX", "dY", "dG")

    p3.close()
    p4 = contextlib.ExitStack()
    woraw = sbuf("woraw", [128, 3, DT, 128], F32, p4)
    wob = sbuf("wob", [128, 3, DT, 128], BF16, p4)
    xTt = sbuf("xTt", [128, 4, S], F32, p4)
    o3ld = sbuf("o3ld", [128, 4, S], BF16, p4)
    outA = sbuf("outA", [128, 2, S], F32, p4)
    outB = sbuf("outB", [128, 2, S], F32, p4)
    pwo = [psum("pwo%d" % i, [128, S], F32, p4) for i in range(2)]

    o3v = o3_scr.rearrange("(mt p) s -> p mt s", p=128)
    xTv = xT_d.rearrange("(mt p) s -> p mt s", p=128)

    wol_toks = [None] * DT
    woc_toks = [None] * DT
    wo_toks = [None] * DT
    xl_toks = [None] * DT
    ol_toks = [None] * DT
    fA_toks = [None] * DT
    fB_toks = [None] * DT
    out_toks = [None] * DT

    for mm in range(DT + 2):
        if mm < DT:
            mt = mm

            def em_wol(gp, mt=mt):
                return gp.dma_start(out=woraw[:, mt % 3, :, :],
                                    in_=wos_d[mt])
            w = list(p3_end) if mt == 0 else []
            if mt >= 3:
                w.append(woc_toks[mt - 3])
            wol_toks[mt] = p.step("gpsimd", em_wol, waits=w, inc="dG",
                                  inc_by=16)

            def em_woc(v, mt=mt):
                v.tensor_copy(wob[:, mt % 3, :, :], woraw[:, mt % 3, :, :])
                v.drain()
                return v.nop()
            w = [wol_toks[mt]] + (list(p3_end) if mt == 0 else [])
            if mt >= 3:
                w.append(wo_toks[mt - 3])
            woc_toks[mt] = p.step("vector", em_woc, waits=w, inc="v")

            def em_xl(sy, mt=mt):
                return sy.dma_start(out=xTt[:, mt % 4, :], in_=xTv[:, mt, :])
            w = [fA_toks[mt - 4]] if mt >= 4 else list(p3_end)
            xl_toks[mt] = p.step("sync", em_xl, waits=w, inc="dY", inc_by=16)

            def em_ol(sy, mt=mt):
                return sy.dma_start(out=o3ld[:, mt % 4, :], in_=o3v[:, mt, :])
            w = [od_toks[(NST - 1) * DT + mt]]
            if mt >= 4:
                w.append(fB_toks[mt - 4])
            else:
                w += list(p3_end)
            ol_toks[mt] = p.step("sync", em_ol, waits=w, inc="dY", inc_by=16)

        if mm >= 2:
            m2 = mm - 2

            def em_fA(v, mt=m2):
                v.scalar_tensor_tensor(outA[:, mt % 2, :], pwo[mt % 2][:],
                                       1.0, xTt[:, mt % 4, :],
                                       op0=AL.mult, op1=AL.add)
                v.drain()
                return v.nop()
            w = [wo_toks[m2], xl_toks[m2]]
            if m2 >= 2:
                w.append(fB_toks[m2 - 2])
            fA_toks[m2] = p.step("vector", em_fA, waits=w, inc="v")

            def em_fB(v, mt=m2):
                v.scalar_tensor_tensor(outB[:, mt % 2, :], o3ld[:, mt % 4, :],
                                       0.5, outA[:, mt % 2, :],
                                       op0=AL.mult, op1=AL.add)
                v.drain()
                return v.nop()
            w = [fA_toks[m2], ol_toks[m2]]
            if m2 >= 2:
                w.append(out_toks[m2 - 2])
            fB_toks[m2] = p.step("vector", em_fB, waits=w, inc="v")

            def em_out(sc, mt=m2):
                return sc.dma_start(out=out_d[128 * mt:128 * (mt + 1), :],
                                    in_=outB[:, mt % 2, :])
            out_toks[m2] = p.step("scalar", em_out, waits=[fB_toks[m2]],
                                  inc="dX", inc_by=16)

        if mm < DT:
            mt = mm

            def em_wo(t, mt=mt):
                last = None
                for kt in range(DT):
                    for ch in range(NCH):
                        last = t.matmul(
                            pwo[mt % 2][:, 512 * ch:512 * (ch + 1)],
                            wob[:, mt % 3, kt, :],
                            attnF[:, kt, 512 * ch:512 * (ch + 1)],
                            start=(kt == 0), stop=(kt == DT - 1))
                return last
            w = [woc_toks[mt]] + at_toks
            if mt >= 2:
                w.append(fA_toks[mt - 2])
            wo_toks[mt] = p.step("tensor", em_wo, waits=w, inc="t")

    p.step("scalar", lambda sc: None, waits=[out_toks[DT - 1]])
    p.step("sync", lambda sy: None, waits=[sl_load[NSLAB - 1]])
    p.step("gpsimd", lambda gp: None, waits=[od_toks[NST * DT - 1]])

    with nc.Block() as blk:
        p.emit(blk)
    p4.close()

    return nc


_CACHED = {}


def _get_nc():
    if "nc" not in _CACHED:
        _CACHED["nc"] = build_program()
    return _CACHED["nc"]


def kernel(L0_attn_x, L0_attn_Wq, L0_attn_Wk, L0_attn_Wv, L0_attn_Wo,
           L0_ffn_w1, L0_ffn_w2, L0_ffn_w3):
    B = L0_attn_x.shape[0]
    nc = _get_nc()

    def C(a):
        return np.ascontiguousarray(np.asarray(a), dtype=np.float32)

    D = L0_attn_x.shape[2]
    S = L0_attn_x.shape[1]
    H = L0_attn_Wq.shape[0]
    F = L0_ffn_w1.shape[0]
    DT, FT, NVE = D // 128, F // 128, D // 512
    NST = 4 if FT % 4 == 0 else (2 if FT % 2 == 0 else 1)
    SFT = FT // NST
    wqs = C(np.asarray(L0_attn_Wq).T.reshape(DT, 128, H).transpose(1, 0, 2))
    wks = C(np.asarray(L0_attn_Wk).T.reshape(DT, 128, H).transpose(1, 0, 2))
    wvs = C(np.asarray(L0_attn_Wv).T.reshape(DT, 128, NVE, 512)
            .transpose(2, 1, 0, 3))
    wos = C(np.asarray(L0_attn_Wo).T.reshape(DT, 128, DT, 128)
            .transpose(2, 1, 0, 3))
    w1s = C(np.asarray(L0_ffn_w1).T.reshape(DT, 128, FT, 128)
            .transpose(2, 1, 0, 3))
    w2s = C(np.asarray(L0_ffn_w2).T.reshape(DT, 128, FT, 128)
            .transpose(2, 1, 0, 3))
    w3s = C(np.asarray(L0_ffn_w3).T.reshape(NST, SFT, 128, DT, 128)
            .transpose(0, 3, 2, 1, 4))

    in_maps = []
    for b in range(B):
        xb = C(L0_attn_x[b])
        in_maps.append({
            "x_tok": xb, "xT": C(xb.T),
            "wqs": wqs, "wks": wks, "wvs": wvs, "wos": wos,
            "w1s": w1s, "w2s": w2s, "w3s": w3s,
        })

    res = run_bass_kernel_spmd(nc, in_maps, core_ids=list(range(B)))
    out = np.empty((B, L0_attn_x.shape[1], L0_attn_x.shape[2]),
                   dtype=np.float32)
    for b in range(B):
        out[b] = np.asarray(res.results[b]["out"]).T
    return out


# version 22
# speedup vs baseline: 1.2474x; 1.0094x over previous
import numpy as np
import contextlib
import concourse.bass as bass
import concourse.mybir as mybir
from concourse.bass_utils import run_bass_kernel_spmd
from concourse import library_config

F32 = mybir.dt.float32
BF16 = mybir.dt.bfloat16
U16 = mybir.dt.uint16
I16 = mybir.dt.int16
AL = mybir.AluOpType
AF = mybir.ActivationFunctionType
AX = mybir.AxisListType

BIGC = float(2 ** 23)
NEG_BIG = -1e30
ENGINES = ("sync", "scalar", "vector", "tensor", "gpsimd")


class Prog:

    def __init__(self, nc):
        self.nc = nc
        self.steps = {e: [] for e in ENGINES}
        self.sems = {}
        self.counts = {}
        self.last_dma = {}
        self._es = contextlib.ExitStack()

    def sem(self, name):
        if name not in self.sems:
            self.sems[name] = self._es.enter_context(self.nc.semaphore(name))
            self.counts[name] = 0
        return self.sems[name]

    NSUB = 6

    def step(self, engine, emit, waits=(), inc=None, inc_by=1):
        tok = None
        waits = [w for w in waits if w is not None]
        if inc is not None:
            if inc_by == 16:
                jname = inc + "#j"
                j = self.counts.get(jname, 0)
                self.counts[jname] = j + 1
                sub = "%s%d" % (inc, j % self.NSUB)
                self.sem(sub)
                self.counts[sub] += 16
                if self.counts[sub] > 16:
                    waits.append((sub, self.counts[sub] - 16))
                tok = (sub, self.counts[sub], 16)
                self.last_dma.setdefault(inc, {})[sub] = tok
            else:
                self.sem(inc)
                self.counts[inc] += inc_by
                tok = (inc, self.counts[inc], inc_by)
        self.steps[engine].append((tuple(waits), emit, tok))
        return tok

    def tails(self, *queues):
        out = []
        for q in queues:
            out.extend(self.last_dma.get(q, {}).values())
        return out

    def emit(self, block):
        for engine in ENGINES:
            steps = self.steps[engine]
            if not steps:
                continue

            def body(eng, engine_steps=steps):
                waited = {}
                for waits, emit_fn, tok in engine_steps:
                    for w in waits:
                        sname, val = w[0], w[1]
                        if waited.get(sname, -1) < val:
                            eng.wait_ge(self.sems[sname], val)
                            waited[sname] = val
                    last = emit_fn(eng)
                    if tok is not None:
                        if last is None:
                            last = eng.nop()
                        last.then_inc(self.sems[tok[0]], tok[2])

            getattr(block, engine)(body)


def build_program(S=2048, F=5632):
    D, H = 2048, 128
    NT = S // 128
    DT = D // 128
    FT = F // 128
    NCH = S // 512
    G = H // 2
    GB = 8
    NBLK = G // GB
    NST = 4 if FT % 4 == 0 else (2 if FT % 2 == 0 else 1)
    SFT = FT // NST
    n_w_elems = float(F * D)

    nc = bass.Bass("TRN2", target_bir_lowering=False)
    p = Prog(nc)

    NVE_ = D // 512
    NST_ = 4 if FT % 4 == 0 else (2 if FT % 2 == 0 else 1)
    SFT_ = FT // NST_
    x_tok = nc.dram_tensor("x_tok", [S, D], F32, kind="ExternalInput").ap()
    xT_d = nc.dram_tensor("xT", [D, S], F32, kind="ExternalInput").ap()
    wqs_d = nc.dram_tensor("wqs", [128, DT, H], F32, kind="ExternalInput").ap()
    wks_d = nc.dram_tensor("wks", [128, DT, H], F32, kind="ExternalInput").ap()
    wvs_d = nc.dram_tensor("wvs", [NVE_, 128, DT, 512], F32,
                           kind="ExternalInput").ap()
    wos_d = nc.dram_tensor("wos", [DT, 128, DT, 128], F32,
                           kind="ExternalInput").ap()
    w1s_d = nc.dram_tensor("w1s", [FT, 128, DT, 128], F32,
                           kind="ExternalInput").ap()
    w2s_d = nc.dram_tensor("w2s", [FT, 128, DT, 128], F32,
                           kind="ExternalInput").ap()
    w3s_d = nc.dram_tensor("w3s", [NST_, DT, 128, SFT_, 128], F32,
                           kind="ExternalInput").ap()
    out_d = nc.dram_tensor("out", [D, S], F32, kind="ExternalOutput").ap()
    at_scr = nc.dram_tensor("at_scr", [D, S], BF16, kind="Internal").ap()
    o3_scr = nc.dram_tensor("o3_scr", [D, S], BF16, kind="Internal").ap()
    v_scr = [nc.dram_tensor("v_scr%d" % c, [S, 512], BF16,
                            kind="Internal").ap() for c in range(NVE_)]

    es = contextlib.ExitStack()

    def sbuf(name, shape, dtype, stack=None, side=None):
        kw = {} if side is None else {"side": side}
        return (stack or es).enter_context(
            nc.sbuf_tensor(name, shape, dtype, **kw))

    def psum(name, shape, dtype=F32, stack=None):
        return (stack or es).enter_context(
            nc.psum_tensor(name, shape, dtype))

    xnT_sc = contextlib.ExitStack()
    xnT = sbuf("xnT", [128, DT, S], BF16, xnT_sc, side="right")
    ident = sbuf("ident", [128, 128], BF16)
    iota_r = sbuf("iota_r", [128, 128], F32)
    iota_g = sbuf("iota_g", [128, G], F32)
    iota_rp = sbuf("iota_rp", [128, 128], F32)
    hcol = sbuf("hcol", [128, 1], F32)
    hcol_i = sbuf("hcol_i", [128, 1], mybir.dt.int32)
    hbit_i = sbuf("hbit_i", [128, 1], mybir.dt.int32)
    hm2 = sbuf("hm2", [128, 1], F32)
    h2col = sbuf("h2col", [128, 1], F32)
    rlt64 = sbuf("rlt64", [128, 1], F32)
    nt0 = sbuf("nt0", [128, 1], F32)
    h31col = sbuf("h31col", [128, 1], F32)
    biascol = sbuf("biascol", [128, 1], F32)
    Gm = sbuf("Gm", [128, G], BF16)
    Pm = sbuf("Pm", [128, 128], BF16)
    csr = sbuf("csr", [128, 128], F32)
    KV2 = sbuf("KV2", [128, 128], BF16)
    CP2 = sbuf("CP2", [128, 128], BF16)
    gam_cols = sbuf("gam_cols", [128, 192], F32)
    gam_acc = sbuf("gam_acc", [128, 3], F32)
    gam_sc = sbuf("gam_sc", [1, 3], F32)
    thr_sc = sbuf("thr_sc", [1, 3], F32)
    ones_row = sbuf("ones_row", [1, 128], F32)
    thr = sbuf("thr", [128, 3], F32)
    thrn = sbuf("thrn", [128, 3], F32)
    qb = sbuf("qb", [128, S], BF16)
    gq = sbuf("gq", [128, S], BF16)
    kvals = sbuf("kvals", [128, 64], F32)
    Vsel = sbuf("Vsel", [128, G, 34], BF16)
    idxb0 = sbuf("idxb0", [64, G], mybir.dt.uint32)
    idxb1 = sbuf("idxb1", [64, G], mybir.dt.uint32)

    gsc = contextlib.ExitStack()
    GCW = 2048
    GR = 3
    gslab = sbuf("gslab", [128, GR, GCW], F32, gsc)
    gdum = sbuf("gdum", [128, GCW], BF16, gsc)

    gchunks = []
    nper = []
    for wd in (w1s_d, w2s_d, w3s_d):
        flat = wd.rearrange("a b c d -> (a b c d)") if len(wd.shape) == 4 \
            else wd.rearrange("a b c d e -> (a b c d e)")
        nch_ = flat.shape[0] // (128 * GCW)
        assert nch_ * 128 * GCW == flat.shape[0]
        chv = flat.rearrange("(n p q) -> n p q", p=128, q=GCW)
        for i_ in range(nch_):
            gchunks.append((chv[i_], GCW))
        nper.append(nch_)
    NGC = len(gchunks)
    assert NGC <= 192
    r1 = nper[0]
    r2 = nper[0] + nper[1]

    gam_toks = [None] * NGC
    gslab_toks = [None] * NGC

    def reg_gamma(i, eng="scalar"):
        ap, nel = gchunks[i]

        def em_dma(sy, ap=ap, nel=nel, i=i):
            return sy.dma_start(out=gslab[:, i % GR, :nel], in_=ap)
        w = [gam_toks[i - GR]] if i >= GR else []
        gslab_toks[i] = p.step("sync", em_dma, waits=w, inc="dY", inc_by=16)

        if eng == "scalar":
            def em_abs(sc, i=i, nel=nel):
                sc.activation(gdum[:, :nel], gslab[:, i % GR, :nel],
                              AF.Abs, accum_out=gam_cols[:, i:i + 1])
                sc.drain()
                return sc.nop()
            gam_toks[i] = p.step("scalar", em_abs, waits=[gslab_toks[i]],
                                 inc="sGa")
        else:
            def em_absv(v, i=i, nel=nel):
                v.tensor_reduce(gam_cols[:, i:i + 1],
                                gslab[:, i % GR, :nel], axis=AX.X,
                                op=AL.add, apply_absolute_value=True)
                v.drain()
                return v.nop()
            gam_toks[i] = p.step("vector", em_absv, waits=[gslab_toks[i]],
                                 inc="v")

    def em_iotas(gp):
        gp.iota(iota_r[:], pattern=[[1, 128]], base=0, channel_multiplier=0,
                allow_small_or_imprecise_dtypes=True)
        gp.iota(iota_g[:], pattern=[[1, G]], base=0, channel_multiplier=0,
                allow_small_or_imprecise_dtypes=True)
        gp.iota(iota_rp[:], pattern=[[1, 2], [0, 64]], base=0,
                channel_multiplier=0, allow_small_or_imprecise_dtypes=True)
        gp.iota(hcol[:], pattern=[[0, 1]], base=0, channel_multiplier=1,
                allow_small_or_imprecise_dtypes=True)
        gp.iota(hcol_i[:], pattern=[[0, 1]], base=0, channel_multiplier=1)
        gp.iota(csr[:], pattern=[[0, 2], [1, 2], [0, 32]], base=0,
                channel_multiplier=0, allow_small_or_imprecise_dtypes=True)
        gp.memset(gam_cols[:], 0.0)
        gp.memset(ones_row[:], 1.0)
        gp.drain()
        return gp.nop()
    tok_iota = p.step("gpsimd", em_iotas, inc="gp")

    def em_ident(v):
        v.tensor_scalar(ident[:], iota_r[:], hcol[:], None, op0=AL.is_equal)
        v.drain()
        return v.nop()
    tok_ident = p.step("vector", em_ident, waits=[tok_iota], inc="v")

    def em_masks(v):
        v.tensor_scalar(hbit_i[:], hcol_i[:], 1, None, op0=AL.bitwise_and)
        v.drain()
        v.tensor_copy(hm2[:], hbit_i[:])
        v.drain()
        v.tensor_tensor(h2col[:], hcol[:], hm2[:], op=AL.subtract)
        v.drain()
        v.tensor_scalar(h2col[:], h2col[:], 0.5, None, op0=AL.mult)
        v.drain()
        v.tensor_scalar(Gm[:], iota_g[:], h2col[:], None, op0=AL.is_equal)
        v.drain()
        v.tensor_scalar(Pm[:], iota_rp[:], hm2[:], None, op0=AL.is_equal)
        v.drain()
        v.tensor_scalar(csr[:], csr[:], -2.0 * BIGC, BIGC,
                        op0=AL.mult, op1=AL.add)
        v.drain()
        v.tensor_tensor(CP2[:], csr[:], Pm[:], op=AL.mult)
        v.drain()
        v.tensor_scalar(rlt64[:], hcol[:], 64.0, None, op0=AL.is_lt)
        v.drain()
        v.tensor_scalar(hbit_i[:], hcol_i[:], 31, None, op0=AL.bitwise_and)
        v.drain()
        v.tensor_copy(h31col[:], hbit_i[:])
        v.drain()
        v.tensor_scalar(hbit_i[:], hcol_i[:], 32, None, op0=AL.bitwise_and)
        v.drain()
        v.tensor_copy(nt0[:], hbit_i[:])
        v.drain()
        v.tensor_scalar(nt0[:], nt0[:], 0.0, None, op0=AL.is_equal)
        v.drain()
        v.tensor_scalar(biascol[:], nt0[:], -BIGC, None, op0=AL.mult)
        v.drain()
        v.memset(Vsel[:], 0.0)
        v.drain()
        v.tensor_scalar(Vsel[:, :, 32:33],
                        rlt64[:, None, :].broadcast_to((128, G, 1)),
                        1.0, None, op0=AL.mult)
        v.drain()
        v.tensor_scalar(Vsel[:, :, 33:34],
                        rlt64[:, None, :].broadcast_to((128, G, 1)),
                        -1.0, 1.0, op0=AL.mult, op1=AL.add)
        v.drain()
        return v.nop()
    tok_masks = p.step("vector", em_masks, inc="v")

    p1 = contextlib.ExitStack()
    XR = 2
    x_tile = sbuf("x_tile", [128, XR, D], F32, p1)
    xn_stage = sbuf("xn_stage", [128, XR, D], BF16, p1)
    stats = sbuf("stats", [128, XR, 24], F32, p1)
    mv = sbuf("mv", [128, XR, 2], F32, p1)
    veps = sbuf("veps", [128, XR, 1], F32, p1)
    rs1 = sbuf("rs1", [128, XR, 1], F32, p1)
    nt1 = sbuf("nt1", [128, XR, 1], F32, p1)
    pxn = [psum("pxn%d" % i, [128, DT, 128], BF16, p1) for i in range(2)]

    p2a = contextlib.ExitStack()
    U32 = mybir.dt.uint32
    wvs2 = sbuf("wvs2", [128, 2, DT, 512], BF16, p2a)
    vtile = sbuf("vtile", [128, 4, 512], BF16, p2a)
    keysw = sbuf("keysw", [128, 3, S], F32, p2a)
    wqTb = sbuf("wqTb", [128, DT, H], BF16, p2a)
    wkTb = sbuf("wkTb", [128, DT, H], BF16, p2a)
    idx = sbuf("idx", [128, 64], U16, p2a)
    idxcf = sbuf("idxcf", [128, 64], F32, p2a)
    kvn = sbuf("kvn", [128, 32], F32, p2a)
    identF = sbuf("identF", [128, 128], F32, p2a)
    RING = 3
    pqk = [psum("pqk%d" % i, [128, 512], F32, p2a) for i in range(RING)]
    pidx = psum("pidx", [64, 128], F32, p2a)

    vflatc = [vc.rearrange("s (hh cc) -> (s hh) cc", cc=16) for vc in v_scr]

    t_idf = p.step("vector",
                   lambda v: v.tensor_scalar(identF[:], iota_r[:], hcol[:],
                                             None, op0=AL.is_equal),
                   waits=[tok_iota], inc="v")

    gamma_ptr = [0]

    def pump_gamma(n, eng="scalar"):
        while gamma_ptr[0] < min(NGC, n):
            reg_gamma(gamma_ptr[0], eng)
            gamma_ptr[0] += 1

    xn_toks = [None] * NT
    xload_toks = [None] * NT
    xpose_toks = [None] * NT
    xev_toks = [None] * NT
    sqrt_toks = [None] * NT

    mm_evs = []
    NQK = 2 * NCH
    NVE = D // 512
    qk_ev = [None] * NQK
    vmm_toks = [[None] * NT for _ in range(NVE)]
    vsp_flat = []
    vsub_toks = [[None] * 4 for _ in range(NVE)]
    cqk = [None, None]

    def reg_vsub(ech, q, extra=()):
        def em(gp, ech=ech, q=q):
            return gp.dma_start(out=wvs2[:, ech % 2, 4 * q:4 * (q + 1), :],
                                in_=wvs_d[ech][:, 4 * q:4 * (q + 1), :])
        vsub_toks[ech][q] = p.step("gpsimd", em, waits=list(extra),
                                   inc="dG", inc_by=16)

    def reg_qk(wi, ch):
        u01 = wi * NCH + ch
        k = len(mm_evs)
        slot = k % RING

        def em_mm(t, wi=wi, ch=ch, slot=slot):
            wt = wqTb if wi == 0 else wkTb
            last = None
            for kt in range(DT):
                last = t.matmul(pqk[slot][:], wt[:, kt, :],
                                xnT[:, kt, 512 * ch:512 * (ch + 1)],
                                start=(kt == 0), stop=(kt == DT - 1))
            return last
        w = [cqk[wi], xev_toks[4 * ch + 3]]
        if k >= RING:
            w.append(mm_evs[k - RING])
        tok_mm = p.step("tensor", em_mm, waits=w, inc="t")

        if wi == 0:
            def em_ev(v, ch=ch, slot=slot):
                v.tensor_copy(qb[:, 512 * ch:512 * (ch + 1)], pqk[slot][:])
                v.drain()
                v.tensor_scalar(gq[:, 512 * ch:512 * (ch + 1)],
                                pqk[slot][:], 0.0, None, op0=AL.is_gt)
                v.drain()
                return v.nop()
        else:
            def em_ev(v, ch=ch, slot=slot):
                v.tensor_copy(keysw[:, 0, 512 * ch:512 * (ch + 1)],
                              pqk[slot][:])
                v.drain()
                return v.nop()
        qk_ev[u01] = p.step("vector", em_ev, waits=[tok_mm], inc="v")
        mm_evs.append(qk_ev[u01])

    def reg_vunit(ech, mt):
        k = len(mm_evs)
        slot = k % RING
        vu = len(vsp_flat)
        vslot = vu % 4

        def em_vmm(t, ech=ech, mt=mt, slot=slot):
            last = None
            for kt in range(DT):
                last = t.matmul(pqk[slot][:],
                                xnT[:, kt, 128 * mt:128 * (mt + 1)],
                                wvs2[:, ech % 2, kt, :],
                                start=(kt == 0), stop=(kt == DT - 1))
            return last
        w = list(vsub_toks[ech]) + [xev_toks[mt]]
        if k >= RING:
            w.append(mm_evs[k - RING])
        vmm_toks[ech][mt] = p.step("tensor", em_vmm, waits=w, inc="t")

        def em_vev(v, slot=slot, vslot=vslot):
            v.tensor_copy(vtile[:, vslot, :], pqk[slot][:])
            v.drain()
            return v.nop()
        w = [vmm_toks[ech][mt]]
        if vu >= 4:
            w.append(vsp_flat[vu - 4])
        tok_ev = p.step("vector", em_vev, waits=w, inc="v")
        mm_evs.append(tok_ev)

        def em_vsp(sc, ech=ech, mt=mt, vslot=vslot):
            return sc.dma_start(
                out=v_scr[ech][128 * mt:128 * (mt + 1), :],
                in_=vtile[:, vslot, :])
        vsp_flat.append(p.step("scalar", em_vsp, waits=[tok_ev],
                               inc="dX", inc_by=16))

    for st in range(NT):
        b = st % XR

        def em_xload(gp, st=st, b=b):
            return gp.dma_start(out=x_tile[:, b, :],
                                in_=x_tok[128 * st:128 * (st + 1), :])
        w = [xn_toks[st - XR]] if st >= XR else []
        xload_toks[st] = p.step("gpsimd", em_xload, waits=w, inc="dG",
                                inc_by=16)
        if st == 1:
            cqk[0] = p.step(
                "gpsimd",
                lambda gp: gp.dma_start(out=wqTb[:], in_=wqs_d[:]),
                inc="dG", inc_by=16)
            cqk[1] = p.step(
                "gpsimd",
                lambda gp: gp.dma_start(out=wkTb[:], in_=wks_d[:]),
                inc="dG", inc_by=16)
        elif st in (2, 3):
            reg_vsub(0, 2 * (st - 2))
            reg_vsub(0, 2 * (st - 2) + 1)
        elif 4 <= st <= 7:
            reg_vsub(1, st - 4)

        def em_stats(v, b=b):
            for c in range(4):
                v.bn_stats(stats[:, b, 6 * c:6 * c + 6],
                           x_tile[:, b, 512 * c:512 * (c + 1)])
            v.drain()
            v.bn_aggr(mv[:, b, :], stats[:, b, :])
            v.drain()
            v.tensor_scalar(veps[:, b, :], mv[:, b, 1:2], 1e-5, None,
                            op0=AL.add)
            v.drain()
            return v.nop()
        t_stats = p.step("vector", em_stats, waits=[xload_toks[st]], inc="v")

        def em_sqrt(sc, b=b):
            sc.activation(nt1[:, b, :], veps[:, b, :], AF.Ln)
            sc.drain()
            sc.activation(rs1[:, b, :], nt1[:, b, :], AF.Exp, scale=-0.5)
            sc.drain()
            return sc.nop()
        sqrt_toks[st] = p.step("scalar", em_sqrt, waits=[t_stats], inc="sc")

        def em_xn(v, b=b):
            v.tensor_scalar(xn_stage[:, b, :], x_tile[:, b, :],
                            mv[:, b, 0:1], rs1[:, b, :],
                            op0=AL.subtract, op1=AL.mult)
            v.drain()
            return v.nop()
        xn_toks[st] = p.step("vector", em_xn, waits=[sqrt_toks[st]], inc="v")

        def em_tp(t, st=st, b=b):
            last = None
            for dt in range(DT):
                last = t.transpose(pxn[st % 2][:, dt, :],
                                   xn_stage[:, b, 128 * dt:128 * (dt + 1)],
                                   ident[:])
            return last
        w = [xn_toks[st], tok_ident]
        if st >= 2:
            w.append(xev_toks[st - 2])
        xpose_toks[st] = p.step("tensor", em_tp, waits=w, inc="t")

        def em_ev(v, st=st):
            v.tensor_copy(xnT[:, :, 128 * st:128 * (st + 1)], pxn[st % 2][:])
            v.drain()
            return v.nop()
        xev_toks[st] = p.step("vector", em_ev, waits=[xpose_toks[st]],
                              inc="v")
        if st % 4 == 3:
            reg_qk(0, st // 4)
            reg_qk(1, st // 4)
        if st >= 4:
            reg_vunit(0, st - 4)
        pump_gamma(st + 1)

    for ech in (0, 1):
        for q in range(4):
            if vsub_toks[ech][q] is None:
                reg_vsub(ech, q)
    for mt in range(max(0, NT - 4), NT):
        if vmm_toks[0][mt] is None:
            reg_vunit(0, mt)

    seq_top = [0, 1, 2, 1]
    seq_bot = [1, 2, 1, 2]

    def mk_top(r):
        def em(v, r=r):
            src = keysw[:, seq_top[r], :]
            v.max(kvals[:, 8 * r:8 * r + 8], src)
            v.drain()
            v.max_index(idx[:, 8 * r:8 * r + 8], kvals[:, 8 * r:8 * r + 8],
                        src)
            if r < 3:
                v.match_replace(keysw[:, seq_top[r + 1], :],
                                kvals[:, 8 * r:8 * r + 8], src, NEG_BIG)
            v.drain()
            return v.nop()
        return em

    def em_negk(v):
        v.tensor_scalar(keysw[:, 1, :], keysw[:, 0, :], -1.0, None,
                        op0=AL.mult)
        v.drain()
        return v.nop()

    def mk_bot(r):
        def em(v, r=r):
            src = keysw[:, seq_bot[r], :]
            v.max(kvn[:, 8 * r:8 * r + 8], src)
            v.drain()
            v.max_index(idx[:, 32 + 8 * r:32 + 8 * r + 8],
                        kvn[:, 8 * r:8 * r + 8], src)
            if r < 3:
                v.match_replace(keysw[:, seq_bot[r + 1], :],
                                kvn[:, 8 * r:8 * r + 8], src, NEG_BIG)
            v.drain()
            return v.nop()
        return em

    def em_negv(v):
        v.tensor_scalar(kvals[:, 32:64], kvn[:], -1.0, None, op0=AL.mult)
        v.drain()
        return v.nop()

    sel_emits = [mk_top(r) for r in range(4)] + [em_negk] \
        + [mk_bot(r) for r in range(4)] + [em_negv]
    sel_ptr = [0]
    sel_tok = [None]

    def pump_sel(n=1):
        for _ in range(n):
            if sel_ptr[0] >= len(sel_emits):
                return
            sel_tok[0] = p.step("vector", sel_emits[sel_ptr[0]], inc="v")
            sel_ptr[0] += 1

    GTAIL = 48
    for mt in range(NT):
        reg_vunit(1, mt)
        pump_sel(1)
        pump_gamma(min(GTAIL, gamma_ptr[0] + 1))
    pump_sel(len(sel_emits))
    tok_sel = sel_tok[0]

    def em_idxc(v):
        v.tensor_copy(idxcf[:], idx[:])
        v.drain()
        v.tensor_scalar(idxcf[:], idxcf[:], 32.0, h31col[:],
                        op0=AL.mult, op1=AL.add)
        v.drain()
        return v.nop()
    t_ic = p.step("vector", em_idxc, waits=[tok_sel], inc="v")

    def em_kv2(v):
        v.tensor_tensor(KV2.ap().rearrange("p (two j) -> p two j", j=64),
                        kvals[:, None, :].broadcast_to((128, 2, 64)),
                        Pm.ap().rearrange("p (two j) -> p two j", j=64),
                        op=AL.mult)
        v.drain()
        return v.nop()
    tok_kv2 = p.step("vector", em_kv2, waits=[tok_sel, tok_masks], inc="v")

    for q2 in range(4):
        reg_vsub(2, q2, extra=[vmm_toks[0][NT - 1]])
    for mt in range(NT):
        reg_vunit(2, mt)
        pump_gamma(min(GTAIL, gamma_ptr[0] + 1))

    t_it = p.step("tensor",
                  lambda t: t.transpose(pidx[:], idxcf[:], identF[:]),
                  waits=[t_ic, t_idf], inc="t")

    def em_ib(v):
        v.tensor_copy(idxb0[:], pidx[:, 0::2])
        v.drain()
        v.tensor_copy(idxb1[:], pidx[:, 1::2])
        v.drain()
        return v.nop()
    t_ib = p.step("vector", em_ib, waits=[t_it], inc="v")

    for q2 in range(4):
        reg_vsub(3, q2, extra=[vmm_toks[1][NT - 1]])
    for mt in range(NT):
        reg_vunit(3, mt)
        pump_gamma(min(GTAIL, gamma_ptr[0] + 1))

    GPC = G // NVE
    gv_toks = [None] * (2 * G)
    for g in range(G):
        c = g // GPC
        spw = vsp_flat[c * NT + max(0, NT - p.NSUB):(c + 1) * NT]

        def em_g0(gp, g=g, c=c):
            return gp.indirect_dma_start(
                out=Vsel[0:64, g, 0:16], out_offset=None, in_=vflatc[c],
                in_offset=bass.IndirectOffsetOnAxis(ap=idxb0[:, g:g + 1],
                                                    axis=0))
        gv_toks[2 * g] = p.step("gpsimd", em_g0,
                                waits=[t_ib, tok_masks] + spw,
                                inc="dG", inc_by=16)

        def em_g1(gp, g=g, c=c):
            return gp.indirect_dma_start(
                out=Vsel[64:128, g, 16:32], out_offset=None, in_=vflatc[c],
                in_offset=bass.IndirectOffsetOnAxis(ap=idxb1[:, g:g + 1],
                                                    axis=0))
        gv_toks[2 * g + 1] = p.step("gpsimd", em_g1, waits=[gv_toks[2 * g]],
                                    inc="dG", inc_by=16)
    tok_vsel = gv_toks[2 * G - 1]

    pump_gamma(GTAIL)

    p2a_end = [tok_kv2, tok_sel, t_ic, t_it, t_ib,
               vmm_toks[NVE - 1][NT - 1], mm_evs[-1],
               qk_ev[NQK - 1], xn_toks[NT - 1], xev_toks[NT - 1],
               xpose_toks[NT - 1], sqrt_toks[NT - 1]] + p.tails("dX")

    p2a.close()
    p1.close()
    p2b = contextlib.ExitStack()
    Esl = sbuf("Esl", [128, 2, GB, S], BF16, p2b)
    Lblk = sbuf("Lblk", [128, 2, GB, 128], BF16, p2b)
    Cblk = sbuf("Cblk", [128, 2, GB, 128], BF16, p2b)
    a_sE = sbuf("a_sE", [128, NT, 32 * GB], BF16, p2b)
    aT_slab = sbuf("aT_slab", [128, 2, NT, 128], BF16, p2b)
    rden = sbuf("rden", [128, GB, 2], F32, p2b)
    SCW = min(1024, S)
    NHF = S // SCW
    psc = [psum("psc%d" % i, [128, SCW], F32, p2b) for i in range(2)]
    pav = [psum("pav%d" % i, [128, GB, 34], F32, p2b) for i in range(2)]
    pat = [psum("pat%d" % i, [128, NT // 2, 128], BF16, p2b)
           for i in range(2)]

    lb_toks = [None] * NBLK
    sc_toks = [[None] * (NHF * GB) for _ in range(NBLK)]
    ex_toks = [[None] * (NHF * GB) for _ in range(NBLK)]
    av_toks = [[None] * NT for _ in range(NBLK)]
    dv_toks = [[None] * NT for _ in range(NBLK)]
    ta_toks = [[None] * 4 for _ in range(NBLK)]
    ae_toks = [[None] * 4 for _ in range(NBLK)]
    sp_toks = [None] * NBLK

    at_view = at_scr.rearrange("(bb eb p) (st j) -> bb p eb st j",
                               bb=NBLK, p=128, j=128)

    def reg_lb(bkt):
        def em_lb(v, b=bkt):
            gsl = slice(GB * b, GB * (b + 1))
            v.tensor_tensor(Lblk[:, b % 2, :, :],
                            KV2[:, None, :].broadcast_to((128, GB, 128)),
                            Gm[:, gsl, None].broadcast_to((128, GB, 128)),
                            op=AL.mult)
            v.drain()
            v.tensor_tensor(Cblk[:, b % 2, :, :],
                            CP2[:, None, :].broadcast_to((128, GB, 128)),
                            Gm[:, gsl, None].broadcast_to((128, GB, 128)),
                            op=AL.mult)
            v.drain()
            return v.nop()
        w = [tok_kv2] + (p2a_end if bkt == 0 else [])
        if bkt >= 2:
            w.append(sc_toks[bkt - 2][NHF * GB - 1])
        lb_toks[bkt] = p.step("vector", em_lb, waits=w, inc="v")

    reg_lb(0)
    for bkt in range(NBLK):
        if bkt + 1 < NBLK:
            reg_lb(bkt + 1)
        for gi in range(GB):
            for hf in range(NHF):
                u = (bkt * GB + gi) * NHF + hf
                uu = u % 2

                def em_sc_mm(t, b=bkt, gi=gi, hf=hf, uu=uu):
                    last = None
                    for cc in range(SCW // 512):
                        n0 = SCW * hf + 512 * cc
                        t.matmul(psc[uu][:, 512 * cc:512 * (cc + 1)],
                                 Lblk[:, b % 2, gi, :], qb[:, n0:n0 + 512],
                                 start=True, stop=False)
                        last = t.matmul(psc[uu][:, 512 * cc:512 * (cc + 1)],
                                        Cblk[:, b % 2, gi, :],
                                        gq[:, n0:n0 + 512],
                                        start=False, stop=True)
                    return last
                w = [lb_toks[bkt]]
                if u >= 2:
                    w.append(ex_toks[(u - 2) // (NHF * GB)]
                             [(u - 2) % (NHF * GB)])
                if gi == 0 and hf == 0 and bkt >= 2:
                    w.append(av_toks[bkt - 2][NT - 1])
                sc_toks[bkt][NHF * gi + hf] = p.step("tensor", em_sc_mm,
                                                     waits=w, inc="t")

                def em_exp(sc, b=bkt, gi=gi, hf=hf, uu=uu):
                    return sc.activation(
                        Esl[:, b % 2, gi, SCW * hf:SCW * (hf + 1)],
                        psc[uu][:], AF.Exp, bias=biascol[:], scale=1.0)
                ex_toks[bkt][NHF * gi + hf] = p.step(
                    "scalar", em_exp, waits=[sc_toks[bkt][NHF * gi + hf]],
                    inc="sEx")
            pump_gamma(gamma_ptr[0] + 1, "vector")

        for st in range(NT):
            def em_av(t, b=bkt, st=st):
                last = None
                for gi in range(GB):
                    last = t.matmul(pav[st % 2][:, gi, :],
                                    Esl[:, b % 2, gi,
                                        128 * st:128 * (st + 1)],
                                    Vsel[:, b * GB + gi, :],
                                    start=True, stop=True)
                return last
            hi = 2 * (bkt + 1) * GB
            w = [ex_toks[bkt][NHF * GB - 1]] + gv_toks[max(0, hi - 6):hi]
            u = bkt * NT + st
            if u >= 2:
                pb, pst = divmod(u - 2, NT)
                w.append(dv_toks[pb][pst])
            av_toks[bkt][st] = p.step("tensor", em_av, waits=w, inc="t")

            def em_dv(v, st=st):
                v.reciprocal(rden[:], pav[st % 2][:, :, 32:34])
                v.drain()
                v.tensor_tensor(
                    a_sE[:, st, :].rearrange("p (g par c) -> p g par c",
                                             g=GB, c=16),
                    pav[st % 2][:, :, 0:32].rearrange(
                        "p g (par c) -> p g par c", c=16),
                    rden[:, :, :, None].broadcast_to((128, GB, 2, 16)),
                    op=AL.mult)
                v.drain()
                return v.nop()
            dv_toks[bkt][st] = p.step("vector", em_dv,
                                      waits=[av_toks[bkt][st]], inc="v")
        pump_gamma(gamma_ptr[0] + 1, "vector")

        HB = NT // 2
        for hb in range(2):
            for eb in range(2):
                u2 = hb * 2 + eb

                def em_ta(t, hb=hb, eb=eb, u2=u2, b=bkt):
                    last = None
                    for si in range(HB):
                        st = hb * HB + si
                        last = t.transpose(
                            pat[u2 % 2][:, si, :],
                            a_sE[:, st, 128 * eb:128 * (eb + 1)],
                            ident[:])
                    return last
                w = [dv_toks[bkt][hb * HB + HB - 1]]
                gu = bkt * 4 + u2
                if gu >= 2:
                    pb, pu2 = divmod(gu - 2, 4)
                    w.append(ae_toks[pb][pu2])
                ta_toks[bkt][u2] = p.step("tensor", em_ta, waits=w, inc="t")

                def em_ae(v, hb=hb, eb=eb, u2=u2):
                    v.tensor_copy(
                        aT_slab[:, eb, hb * HB:(hb + 1) * HB, :],
                        pat[u2 % 2][:])
                    v.drain()
                    return v.nop()
                w = [ta_toks[bkt][u2]]
                if bkt >= 1 and u2 == 0:
                    w.append(sp_toks[bkt - 1])
                ae_toks[bkt][u2] = p.step("vector", em_ae, waits=w, inc="v")

        def em_spill(sc, b=bkt):
            return sc.dma_start(out=at_view[b], in_=aT_slab[:])
        sp_toks[bkt] = p.step("scalar", em_spill,
                              waits=[ae_toks[bkt][3]],
                              inc="dX", inc_by=16)
        pump_gamma(gamma_ptr[0] + 2, "scalar")

    pump_gamma(NGC, "vector")
    p2b_end = [sp_toks[NBLK - 1], gam_toks[NGC - 1],
               av_toks[NBLK - 1][NT - 1], ta_toks[NBLK - 1][3],
               ae_toks[NBLK - 1][3],
               ex_toks[NBLK - 1][NHF * GB - 1]] + p.tails("dX", "dY", "dG")

    p2b.close()
    gsc.close()
    p3 = contextlib.ExitStack()
    wraw = sbuf("wraw", [128, 3, DT, 128], F32, p3)
    sa1 = sbuf("sa1", [128, DT, 128], BF16, p3)
    sb1 = sbuf("sb1", [128, DT, 128], BF16, p3)
    wq1 = sbuf("wq1", [128, 2, DT, 128], BF16, p3)
    wq2 = sbuf("wq2", [128, 2, DT, 128], BF16, p3)
    wq3 = sbuf("wq3", [128, 2, SFT, 128], BF16, p3)
    sigb = sbuf("sigb", [128, 2, S], BF16, p3)
    hid = sbuf("hid", [128, SFT, S], BF16, p3)
    o3st = sbuf("o3st", [128, 2, S], BF16, p3)
    pth_sc = contextlib.ExitStack()
    pth = psum("pth", [128, 4], F32, pth_sc)

    def em_gred(v):
        v.tensor_reduce(gam_acc[:, 0:1], gam_cols[:, 0:r1], axis=AX.X,
                        op=AL.add)
        v.drain()
        v.tensor_reduce(gam_acc[:, 1:2], gam_cols[:, r1:r2], axis=AX.X,
                        op=AL.add)
        v.drain()
        v.tensor_reduce(gam_acc[:, 2:3], gam_cols[:, r2:r2 + (NGC - r2)],
                        axis=AX.X, op=AL.add)
        v.drain()
        return v.nop()
    t_gr = p.step("vector", em_gred, waits=[gam_toks[NGC - 1]], inc="v")

    def em_gpr(gp):
        gp.tensor_reduce(gam_sc[:], gam_acc[:], axis=AX.C, op=AL.add)
        gp.drain()
        return gp.nop()
    t_gp = p.step("gpsimd", em_gpr, waits=[t_gr], inc="gp")

    def em_thr(v):
        v.tensor_scalar(thr_sc[:], gam_sc[:], 0.5 / n_w_elems, 0.5e-8,
                        op0=AL.mult, op1=AL.add)
        v.drain()
        return v.nop()
    t_th = p.step("vector", em_thr, waits=[t_gp], inc="v")

    t_tm = p.step("tensor",
                  lambda t: t.matmul(pth[:, 0:3], ones_row[:], thr_sc[:],
                                     start=True, stop=True),
                  waits=[t_th] + p2b_end, inc="t")

    def em_thev(v):
        v.tensor_copy(thr[:], pth[:, 0:3])
        v.drain()
        v.tensor_scalar(thrn[:], pth[:, 0:3], -1.0, None, op0=AL.mult)
        v.drain()
        return v.nop()
    t_te = p.step("vector", em_thev, waits=[t_tm], inc="v")
    pth_sc.close()
    pf1 = psum("pf1", [128, S], F32, p3)
    pf2 = psum("pf2", [128, S], F32, p3)


    slab_seq = []
    for sti in range(NST):
        for fi in range(SFT):
            f = sti * SFT + fi
            slab_seq.append((0, f))
            slab_seq.append((1, f))
        for d in range(DT):
            slab_seq.append((2, (sti, d)))
    NSLAB = len(slab_seq)
    sl_load = [None] * NSLAB
    sl_q = [None] * NSLAB
    sl_a = [None] * NSLAB
    slab_of = {}
    for i, ka in enumerate(slab_seq):
        slab_of[ka] = i

    f1_toks = [None] * FT
    f2_toks = [None] * FT
    sg_toks = [None] * FT
    hd_toks = [None] * FT
    f3_toks = [None] * (NST * DT)
    o3_toks = [None] * (NST * DT)
    od_toks = [None] * (NST * DT)

    def reg_slab(i):
        kind, arg = slab_seq[i]
        if kind == 0:
            src = w1s_d[arg]
            nk, bias_i = DT, 0
        elif kind == 1:
            src = w2s_d[arg]
            nk, bias_i = DT, 1
        else:
            sti, d = arg
            src = w3s_d[sti, d]
            nk, bias_i = SFT, 2

        def em_l(sy, src=src, i=i, nk=nk):
            return sy.dma_start(out=wraw[:, i % 3, :nk, :], in_=src)
        w = list(p2b_end) if i < 3 else [sl_q[i - 3]]
        sl_load[i] = p.step("sync", em_l, waits=w, inc="dY", inc_by=16)

        def em_q(sc, i=i, nk=nk, bias_i=bias_i):
            sc.activation(sa1[:, :nk, :], wraw[:, i % 3, :nk, :], AF.Sign,
                          bias=thr[:, bias_i:bias_i + 1], scale=1.0)
            sc.drain()
            sc.activation(sb1[:, :nk, :], wraw[:, i % 3, :nk, :], AF.Sign,
                          bias=thrn[:, bias_i:bias_i + 1], scale=1.0)
            sc.drain()
            return sc.nop()
        w = [sl_load[i], t_te]
        if i >= 1:
            w.append(sl_a[i - 1])
        sl_q[i] = p.step("scalar", em_q, waits=w, inc="sc")

        if kind == 0:
            dst, ridx, nk2 = wq1, arg % 2, DT
        elif kind == 1:
            dst, ridx, nk2 = wq2, arg % 2, DT
        else:
            dst, ridx, nk2 = wq3, arg[1] % 2, SFT

        def em_a(v, dst=dst, ridx=ridx, nk2=nk2):
            v.tensor_tensor(dst[:, ridx, :nk2, :], sa1[:, :nk2, :],
                            sb1[:, :nk2, :], op=AL.add)
            v.drain()
            return v.nop()
        w = [sl_q[i]]
        if kind == 0 and arg >= 2:
            w.append(f1_toks[arg - 2])
        if kind == 1 and arg >= 2:
            w.append(f2_toks[arg - 2])
        if kind == 2:
            u3 = arg[0] * DT + arg[1]
            if u3 >= 2:
                w.append(f3_toks[u3 - 2])
        sl_a[i] = p.step("vector", em_a, waits=w, inc="v")

    def reg_slab_upto(i):
        while reg_ptr[0] <= min(i, NSLAB - 1):
            reg_slab(reg_ptr[0])
            reg_ptr[0] += 1

    reg_ptr = [0]
    atf_sc = contextlib.ExitStack()
    at_toks = []
    reg_slab_upto(1)
    for sti in range(NST):
        for fi in range(SFT):
            f = sti * SFT + fi
            if fi + 1 < SFT:
                reg_slab_upto(slab_of[(1, f + 1)])
            else:
                reg_slab_upto(slab_of[(2, (sti, 0))])

            def em_f1(t, f=f):
                last = None
                for kt in range(DT):
                    for ch in range(NCH):
                        last = t.matmul(pf1[:, 512 * ch:512 * (ch + 1)],
                                        wq1[:, f % 2, kt, :],
                                        xnT[:, kt, 512 * ch:512 * (ch + 1)],
                                        start=(kt == 0), stop=(kt == DT - 1))
                return last
            w = [sl_a[slab_of[(0, f)]]]
            if f >= 1:
                w.append(sg_toks[f - 1])
            if fi == 0 and sti >= 1:
                w.append(o3_toks[sti * DT - 1])
                w.append(o3_toks[sti * DT - 2])
            f1_toks[f] = p.step("tensor", em_f1, waits=w, inc="t")

            def em_sg(sc, f=f):
                return sc.activation(sigb[:, f % 2, :], pf1[:], AF.Sigmoid,
                                     scale=0.5)
            w = [f1_toks[f]]
            if f >= 2:
                w.append(hd_toks[f - 2])
            sg_toks[f] = p.step("scalar", em_sg, waits=w, inc="sc")

            def em_f2(t, f=f):
                last = None
                for kt in range(DT):
                    for ch in range(NCH):
                        last = t.matmul(pf2[:, 512 * ch:512 * (ch + 1)],
                                        wq2[:, f % 2, kt, :],
                                        xnT[:, kt, 512 * ch:512 * (ch + 1)],
                                        start=(kt == 0), stop=(kt == DT - 1))
                return last
            w = [sl_a[slab_of[(1, f)]]]
            if f >= 1:
                w.append(hd_toks[f - 1])
            if fi == 0 and sti >= 1:
                w.append(o3_toks[sti * DT - 1])
            f2_toks[f] = p.step("tensor", em_f2, waits=w, inc="t")

            def em_hd(v, f=f, fi=fi):
                v.scalar_tensor_tensor(hid[:, fi, :], pf2[:], 0.5,
                                       sigb[:, f % 2, :],
                                       op0=AL.mult, op1=AL.mult)
                v.drain()
                return v.nop()
            hd_toks[f] = p.step("vector", em_hd,
                                waits=[f2_toks[f], sg_toks[f]], inc="v")

        if sti == NST - 1:
            xnT_sc.close()
            attnF = sbuf("attnF", [128, DT, S], BF16, atf_sc, side="right")
            atv = at_scr.rearrange("(kt p) s -> p kt s", p=128)
            for kt in range(DT):
                def em_atl(sc, kt=kt):
                    return sc.dma_start(out=attnF[:, kt, :],
                                        in_=atv[:, kt, :])
                w = [f1_toks[FT - 1], f2_toks[FT - 1],
                     sp_toks[NBLK - 1]] if kt == 0 else []
                at_toks.append(p.step("scalar", em_atl, waits=w,
                                      inc="dX", inc_by=16))

        for d in range(DT):
            u = sti * DT + d
            if d + 1 < DT:
                reg_slab_upto(slab_of[(2, (sti, d + 1))])
            elif sti + 1 < NST:
                reg_slab_upto(slab_of[(1, (sti + 1) * SFT)])

            def em_f3(t, sti=sti, d=d):
                pp = pf1 if d % 2 == 0 else pf2
                last = None
                for ki in range(SFT):
                    for ch in range(NCH):
                        last = t.matmul(pp[:, 512 * ch:512 * (ch + 1)],
                                        wq3[:, d % 2, ki, :],
                                        hid[:, ki, 512 * ch:512 * (ch + 1)],
                                        start=(ki == 0),
                                        stop=(ki == SFT - 1))
                return last
            w = [sl_a[slab_of[(2, (sti, d))]], hd_toks[sti * SFT + SFT - 1]]
            if d <= 1:
                w.append(sg_toks[sti * SFT + SFT - 1])
            if u >= 2:
                w.append(o3_toks[u - 2])
            f3_toks[u] = p.step("tensor", em_f3, waits=w, inc="t")

            def em_o3(v, d=d, u=u):
                v.tensor_copy(o3st[:, u % 2, :],
                              (pf1 if d % 2 == 0 else pf2)[:])
                v.drain()
                return v.nop()
            w = [f3_toks[u]]
            if u >= 2:
                w.append(od_toks[u - 2])
            o3_toks[u] = p.step("vector", em_o3, waits=w, inc="v")

            def em_od(gp, sti=sti, d=d, u=u):
                kw = {} if sti == 0 else {"accum_op": AL.add}
                return gp.dma_start(out=o3_scr[128 * d:128 * (d + 1), :],
                                    in_=o3st[:, u % 2, :], **kw)
            od_toks[u] = p.step("gpsimd", em_od, waits=[o3_toks[u]],
                                inc="dG", inc_by=16)

    p3_end = [od_toks[NST * DT - 1], f3_toks[NST * DT - 1],
              o3_toks[NST * DT - 1], sl_q[NSLAB - 1],
              sl_a[NSLAB - 1], f2_toks[FT - 1]] + p.tails("dX", "dY", "dG")

    p3.close()
    p4 = contextlib.ExitStack()
    woraw = sbuf("woraw", [128, 3, DT, 128], F32, p4)
    wob = sbuf("wob", [128, 3, DT, 128], BF16, p4)
    xTt = sbuf("xTt", [128, 4, S], F32, p4)
    o3ld = sbuf("o3ld", [128, 4, S], BF16, p4)
    outA = sbuf("outA", [128, 2, S], F32, p4)
    outB = sbuf("outB", [128, 2, S], F32, p4)
    pwo = [psum("pwo%d" % i, [128, S], F32, p4) for i in range(2)]

    o3v = o3_scr.rearrange("(mt p) s -> p mt s", p=128)
    xTv = xT_d.rearrange("(mt p) s -> p mt s", p=128)

    wol_toks = [None] * DT
    woc_toks = [None] * DT
    wo_toks = [None] * DT
    xl_toks = [None] * DT
    ol_toks = [None] * DT
    fA_toks = [None] * DT
    fB_toks = [None] * DT
    out_toks = [None] * DT

    for mm in range(DT + 2):
        if mm < DT:
            mt = mm

            def em_wol(gp, mt=mt):
                return gp.dma_start(out=woraw[:, mt % 3, :, :],
                                    in_=wos_d[mt])
            w = list(p3_end) if mt == 0 else []
            if mt >= 3:
                w.append(woc_toks[mt - 3])
            wol_toks[mt] = p.step("gpsimd", em_wol, waits=w, inc="dG",
                                  inc_by=16)

            def em_woc(v, mt=mt):
                v.tensor_copy(wob[:, mt % 3, :, :], woraw[:, mt % 3, :, :])
                v.drain()
                return v.nop()
            w = [wol_toks[mt]] + (list(p3_end) if mt == 0 else [])
            if mt >= 3:
                w.append(wo_toks[mt - 3])
            woc_toks[mt] = p.step("vector", em_woc, waits=w, inc="v")

            def em_xl(sy, mt=mt):
                return sy.dma_start(out=xTt[:, mt % 4, :], in_=xTv[:, mt, :])
            w = [fA_toks[mt - 4]] if mt >= 4 else list(p3_end)
            xl_toks[mt] = p.step("sync", em_xl, waits=w, inc="dY", inc_by=16)

            def em_ol(sy, mt=mt):
                return sy.dma_start(out=o3ld[:, mt % 4, :], in_=o3v[:, mt, :])
            w = [od_toks[(NST - 1) * DT + mt]]
            if mt >= 4:
                w.append(fB_toks[mt - 4])
            else:
                w += list(p3_end)
            ol_toks[mt] = p.step("sync", em_ol, waits=w, inc="dY", inc_by=16)

        if mm >= 2:
            m2 = mm - 2

            def em_fA(v, mt=m2):
                v.scalar_tensor_tensor(outA[:, mt % 2, :], pwo[mt % 2][:],
                                       1.0, xTt[:, mt % 4, :],
                                       op0=AL.mult, op1=AL.add)
                v.drain()
                return v.nop()
            w = [wo_toks[m2], xl_toks[m2]]
            if m2 >= 2:
                w.append(fB_toks[m2 - 2])
            fA_toks[m2] = p.step("vector", em_fA, waits=w, inc="v")

            def em_fB(v, mt=m2):
                v.scalar_tensor_tensor(outB[:, mt % 2, :], o3ld[:, mt % 4, :],
                                       0.5, outA[:, mt % 2, :],
                                       op0=AL.mult, op1=AL.add)
                v.drain()
                return v.nop()
            w = [fA_toks[m2], ol_toks[m2]]
            if m2 >= 2:
                w.append(out_toks[m2 - 2])
            fB_toks[m2] = p.step("vector", em_fB, waits=w, inc="v")

            def em_out(sc, mt=m2):
                return sc.dma_start(out=out_d[128 * mt:128 * (mt + 1), :],
                                    in_=outB[:, mt % 2, :])
            out_toks[m2] = p.step("scalar", em_out, waits=[fB_toks[m2]],
                                  inc="dX", inc_by=16)

        if mm < DT:
            mt = mm

            def em_wo(t, mt=mt):
                last = None
                for kt in range(DT):
                    for ch in range(NCH):
                        last = t.matmul(
                            pwo[mt % 2][:, 512 * ch:512 * (ch + 1)],
                            wob[:, mt % 3, kt, :],
                            attnF[:, kt, 512 * ch:512 * (ch + 1)],
                            start=(kt == 0), stop=(kt == DT - 1))
                return last
            w = [woc_toks[mt]] + at_toks
            if mt >= 2:
                w.append(fA_toks[mt - 2])
            wo_toks[mt] = p.step("tensor", em_wo, waits=w, inc="t")

    p.step("scalar", lambda sc: None, waits=[out_toks[DT - 1]])
    p.step("sync", lambda sy: None, waits=[sl_load[NSLAB - 1]])
    p.step("gpsimd", lambda gp: None, waits=[od_toks[NST * DT - 1]])

    with nc.Block() as blk:
        p.emit(blk)
    p4.close()

    return nc


_CACHED = {}


def _get_nc():
    if "nc" not in _CACHED:
        _CACHED["nc"] = build_program()
    return _CACHED["nc"]


def kernel(L0_attn_x, L0_attn_Wq, L0_attn_Wk, L0_attn_Wv, L0_attn_Wo,
           L0_ffn_w1, L0_ffn_w2, L0_ffn_w3):
    B = L0_attn_x.shape[0]
    nc = _get_nc()

    def C(a):
        return np.ascontiguousarray(np.asarray(a), dtype=np.float32)

    D = L0_attn_x.shape[2]
    S = L0_attn_x.shape[1]
    H = L0_attn_Wq.shape[0]
    F = L0_ffn_w1.shape[0]
    DT, FT, NVE = D // 128, F // 128, D // 512
    NST = 4 if FT % 4 == 0 else (2 if FT % 2 == 0 else 1)
    SFT = FT // NST
    wqs = C(np.asarray(L0_attn_Wq).T.reshape(DT, 128, H).transpose(1, 0, 2))
    wks = C(np.asarray(L0_attn_Wk).T.reshape(DT, 128, H).transpose(1, 0, 2))
    wvs = C(np.asarray(L0_attn_Wv).T.reshape(DT, 128, NVE, 512)
            .transpose(2, 1, 0, 3))
    wos = C(np.asarray(L0_attn_Wo).T.reshape(DT, 128, DT, 128)
            .transpose(2, 1, 0, 3))
    w1s = C(np.asarray(L0_ffn_w1).T.reshape(DT, 128, FT, 128)
            .transpose(2, 1, 0, 3))
    w2s = C(np.asarray(L0_ffn_w2).T.reshape(DT, 128, FT, 128)
            .transpose(2, 1, 0, 3))
    w3s = C(np.asarray(L0_ffn_w3).T.reshape(NST, SFT, 128, DT, 128)
            .transpose(0, 3, 2, 1, 4))

    in_maps = []
    for b in range(B):
        xb = C(L0_attn_x[b])
        in_maps.append({
            "x_tok": xb, "xT": C(xb.T),
            "wqs": wqs, "wks": wks, "wvs": wvs, "wos": wos,
            "w1s": w1s, "w2s": w2s, "w3s": w3s,
        })

    res = run_bass_kernel_spmd(nc, in_maps, core_ids=list(range(B)))
    out = np.empty((B, L0_attn_x.shape[1], L0_attn_x.shape[2]),
                   dtype=np.float32)
    for b in range(B):
        out[b] = np.asarray(res.results[b]["out"]).T
    return out


# version 23 (final)
# speedup vs baseline: 1.2508x; 1.0027x over previous
import numpy as np
import contextlib
import concourse.bass as bass
import concourse.mybir as mybir
from concourse.bass_utils import run_bass_kernel_spmd
from concourse import library_config

F32 = mybir.dt.float32
BF16 = mybir.dt.bfloat16
U16 = mybir.dt.uint16
I16 = mybir.dt.int16
AL = mybir.AluOpType
AF = mybir.ActivationFunctionType
AX = mybir.AxisListType

BIGC = 512.0
NEG_BIG = -1e30
ENGINES = ("sync", "scalar", "vector", "tensor", "gpsimd")


class Prog:

    def __init__(self, nc):
        self.nc = nc
        self.steps = {e: [] for e in ENGINES}
        self.sems = {}
        self.counts = {}
        self.last_dma = {}
        self._es = contextlib.ExitStack()

    def sem(self, name):
        if name not in self.sems:
            self.sems[name] = self._es.enter_context(self.nc.semaphore(name))
            self.counts[name] = 0
        return self.sems[name]

    NSUB = 6

    def step(self, engine, emit, waits=(), inc=None, inc_by=1):
        tok = None
        waits = [w for w in waits if w is not None]
        if inc is not None:
            if inc_by == 16:
                jname = inc + "#j"
                j = self.counts.get(jname, 0)
                self.counts[jname] = j + 1
                sub = "%s%d" % (inc, j % self.NSUB)
                self.sem(sub)
                self.counts[sub] += 16
                if self.counts[sub] > 16:
                    waits.append((sub, self.counts[sub] - 16))
                tok = (sub, self.counts[sub], 16)
                self.last_dma.setdefault(inc, {})[sub] = tok
            else:
                self.sem(inc)
                self.counts[inc] += inc_by
                tok = (inc, self.counts[inc], inc_by)
        self.steps[engine].append((tuple(waits), emit, tok))
        return tok

    def tails(self, *queues):
        out = []
        for q in queues:
            out.extend(self.last_dma.get(q, {}).values())
        return out

    def emit(self, block):
        for engine in ENGINES:
            steps = self.steps[engine]
            if not steps:
                continue

            def body(eng, engine_steps=steps):
                waited = {}
                for waits, emit_fn, tok in engine_steps:
                    for w in waits:
                        sname, val = w[0], w[1]
                        if waited.get(sname, -1) < val:
                            eng.wait_ge(self.sems[sname], val)
                            waited[sname] = val
                    last = emit_fn(eng)
                    if tok is not None:
                        if last is None:
                            last = eng.nop()
                        last.then_inc(self.sems[tok[0]], tok[2])

            getattr(block, engine)(body)


def build_program(S=2048, F=5632):
    D, H = 2048, 128
    NT = S // 128
    DT = D // 128
    FT = F // 128
    NCH = S // 512
    G = H // 2
    GB = 8
    NBLK = G // GB
    NST = 4 if FT % 4 == 0 else (2 if FT % 2 == 0 else 1)
    SFT = FT // NST
    n_w_elems = float(F * D)

    nc = bass.Bass("TRN2", target_bir_lowering=False)
    p = Prog(nc)

    NVE_ = D // 512
    NST_ = 4 if FT % 4 == 0 else (2 if FT % 2 == 0 else 1)
    SFT_ = FT // NST_
    x_tok = nc.dram_tensor("x_tok", [S, D], F32, kind="ExternalInput").ap()
    xT_d = nc.dram_tensor("xT", [D, S], F32, kind="ExternalInput").ap()
    wqs_d = nc.dram_tensor("wqs", [128, DT, H], F32, kind="ExternalInput").ap()
    wks_d = nc.dram_tensor("wks", [128, DT, H], F32, kind="ExternalInput").ap()
    wvs_d = nc.dram_tensor("wvs", [NVE_, 128, DT, 512], F32,
                           kind="ExternalInput").ap()
    wos_d = nc.dram_tensor("wos", [DT, 128, DT, 128], F32,
                           kind="ExternalInput").ap()
    w1s_d = nc.dram_tensor("w1s", [FT, 128, DT, 128], F32,
                           kind="ExternalInput").ap()
    w2s_d = nc.dram_tensor("w2s", [FT, 128, DT, 128], F32,
                           kind="ExternalInput").ap()
    w3s_d = nc.dram_tensor("w3s", [NST_, DT, 128, SFT_, 128], F32,
                           kind="ExternalInput").ap()
    out_d = nc.dram_tensor("out", [D, S], F32, kind="ExternalOutput").ap()
    at_scr = nc.dram_tensor("at_scr", [D, S], BF16, kind="Internal").ap()
    o3_scr = nc.dram_tensor("o3_scr", [D, S], BF16, kind="Internal").ap()
    v_scr = [nc.dram_tensor("v_scr%d" % c, [S, 512], BF16,
                            kind="Internal").ap() for c in range(NVE_)]

    es = contextlib.ExitStack()

    def sbuf(name, shape, dtype, stack=None, side=None):
        kw = {} if side is None else {"side": side}
        return (stack or es).enter_context(
            nc.sbuf_tensor(name, shape, dtype, **kw))

    def psum(name, shape, dtype=F32, stack=None):
        return (stack or es).enter_context(
            nc.psum_tensor(name, shape, dtype))

    xnT_sc = contextlib.ExitStack()
    xnT = sbuf("xnT", [128, DT, S], BF16, xnT_sc, side="right")
    ident = sbuf("ident", [128, 128], BF16)
    iota_r = sbuf("iota_r", [128, 128], F32)
    iota_g = sbuf("iota_g", [128, G], F32)
    iota_rp = sbuf("iota_rp", [128, 128], F32)
    hcol = sbuf("hcol", [128, 1], F32)
    hcol_i = sbuf("hcol_i", [128, 1], mybir.dt.int32)
    hbit_i = sbuf("hbit_i", [128, 1], mybir.dt.int32)
    hm2 = sbuf("hm2", [128, 1], F32)
    h2col = sbuf("h2col", [128, 1], F32)
    rlt64 = sbuf("rlt64", [128, 1], F32)
    nt0 = sbuf("nt0", [128, 1], F32)
    h31col = sbuf("h31col", [128, 1], F32)
    biascol = sbuf("biascol", [128, 1], F32)
    Gm = sbuf("Gm", [128, G], BF16)
    Pm = sbuf("Pm", [128, 128], BF16)
    csr = sbuf("csr", [128, 128], F32)
    KV2 = sbuf("KV2", [128, 128], BF16)
    CP2 = sbuf("CP2", [128, 128], BF16)
    gam_cols = sbuf("gam_cols", [128, 192], F32)
    gam_acc = sbuf("gam_acc", [128, 3], F32)
    gam_sc = sbuf("gam_sc", [1, 3], F32)
    thr_sc = sbuf("thr_sc", [1, 3], F32)
    ones_row = sbuf("ones_row", [1, 128], F32)
    thr = sbuf("thr", [128, 3], F32)
    thrn = sbuf("thrn", [128, 3], F32)
    qb = sbuf("qb", [128, S], BF16)
    gq = sbuf("gq", [128, S], BF16)
    kvals = sbuf("kvals", [128, 64], F32)
    Vsel = sbuf("Vsel", [128, G, 34], BF16)
    idxb0 = sbuf("idxb0", [64, G], mybir.dt.uint32)
    idxb1 = sbuf("idxb1", [64, G], mybir.dt.uint32)

    gsc = contextlib.ExitStack()
    GCW = 2048
    GR = 3
    gslab = sbuf("gslab", [128, GR, GCW], F32, gsc)
    gdum = sbuf("gdum", [128, GCW], BF16, gsc)

    gchunks = []
    nper = []
    for wd in (w1s_d, w2s_d, w3s_d):
        flat = wd.rearrange("a b c d -> (a b c d)") if len(wd.shape) == 4 \
            else wd.rearrange("a b c d e -> (a b c d e)")
        nch_ = flat.shape[0] // (128 * GCW)
        assert nch_ * 128 * GCW == flat.shape[0]
        chv = flat.rearrange("(n p q) -> n p q", p=128, q=GCW)
        for i_ in range(nch_):
            gchunks.append((chv[i_], GCW))
        nper.append(nch_)
    NGC = len(gchunks)
    assert NGC <= 192
    r1 = nper[0]
    r2 = nper[0] + nper[1]

    gam_toks = [None] * NGC
    gslab_toks = [None] * NGC

    def reg_gamma(i, eng="scalar"):
        ap, nel = gchunks[i]

        def em_dma(sy, ap=ap, nel=nel, i=i):
            return sy.dma_start(out=gslab[:, i % GR, :nel], in_=ap)
        w = [gam_toks[i - GR]] if i >= GR else []
        gslab_toks[i] = p.step("sync", em_dma, waits=w, inc="dY", inc_by=16)

        if eng == "scalar":
            def em_abs(sc, i=i, nel=nel):
                sc.activation(gdum[:, :nel], gslab[:, i % GR, :nel],
                              AF.Abs, accum_out=gam_cols[:, i:i + 1])
                sc.drain()
                return sc.nop()
            gam_toks[i] = p.step("scalar", em_abs, waits=[gslab_toks[i]],
                                 inc="sGa")
        else:
            def em_absv(v, i=i, nel=nel):
                v.tensor_reduce(gam_cols[:, i:i + 1],
                                gslab[:, i % GR, :nel], axis=AX.X,
                                op=AL.add, apply_absolute_value=True)
                v.drain()
                return v.nop()
            gam_toks[i] = p.step("vector", em_absv, waits=[gslab_toks[i]],
                                 inc="v")

    def em_iotas(gp):
        gp.iota(iota_r[:], pattern=[[1, 128]], base=0, channel_multiplier=0,
                allow_small_or_imprecise_dtypes=True)
        gp.iota(iota_g[:], pattern=[[1, G]], base=0, channel_multiplier=0,
                allow_small_or_imprecise_dtypes=True)
        gp.iota(iota_rp[:], pattern=[[1, 2], [0, 64]], base=0,
                channel_multiplier=0, allow_small_or_imprecise_dtypes=True)
        gp.iota(hcol[:], pattern=[[0, 1]], base=0, channel_multiplier=1,
                allow_small_or_imprecise_dtypes=True)
        gp.iota(hcol_i[:], pattern=[[0, 1]], base=0, channel_multiplier=1)
        gp.iota(csr[:], pattern=[[0, 2], [1, 2], [0, 32]], base=0,
                channel_multiplier=0, allow_small_or_imprecise_dtypes=True)
        gp.memset(gam_cols[:], 0.0)
        gp.memset(ones_row[:], 1.0)
        gp.drain()
        return gp.nop()
    tok_iota = p.step("gpsimd", em_iotas, inc="gp")

    def em_ident(v):
        v.tensor_scalar(ident[:], iota_r[:], hcol[:], None, op0=AL.is_equal)
        v.drain()
        return v.nop()
    tok_ident = p.step("vector", em_ident, waits=[tok_iota], inc="v")

    def em_masks(v):
        v.tensor_scalar(hbit_i[:], hcol_i[:], 1, None, op0=AL.bitwise_and)
        v.drain()
        v.tensor_copy(hm2[:], hbit_i[:])
        v.drain()
        v.tensor_tensor(h2col[:], hcol[:], hm2[:], op=AL.subtract)
        v.drain()
        v.tensor_scalar(h2col[:], h2col[:], 0.5, None, op0=AL.mult)
        v.drain()
        v.tensor_scalar(Gm[:], iota_g[:], h2col[:], None, op0=AL.is_equal)
        v.drain()
        v.tensor_scalar(Pm[:], iota_rp[:], hm2[:], None, op0=AL.is_equal)
        v.drain()
        v.tensor_scalar(csr[:], csr[:], -2.0 * BIGC, BIGC,
                        op0=AL.mult, op1=AL.add)
        v.drain()
        v.tensor_tensor(CP2[:], csr[:], Pm[:], op=AL.mult)
        v.drain()
        v.tensor_scalar(rlt64[:], hcol[:], 64.0, None, op0=AL.is_lt)
        v.drain()
        v.tensor_scalar(hbit_i[:], hcol_i[:], 31, None, op0=AL.bitwise_and)
        v.drain()
        v.tensor_copy(h31col[:], hbit_i[:])
        v.drain()
        v.tensor_scalar(hbit_i[:], hcol_i[:], 32, None, op0=AL.bitwise_and)
        v.drain()
        v.tensor_copy(nt0[:], hbit_i[:])
        v.drain()
        v.tensor_scalar(nt0[:], nt0[:], 0.0, None, op0=AL.is_equal)
        v.drain()
        v.tensor_scalar(biascol[:], nt0[:], -BIGC, None, op0=AL.mult)
        v.drain()
        v.memset(Vsel[:], 0.0)
        v.drain()
        v.tensor_scalar(Vsel[:, :, 32:33],
                        rlt64[:, None, :].broadcast_to((128, G, 1)),
                        1.0, None, op0=AL.mult)
        v.drain()
        v.tensor_scalar(Vsel[:, :, 33:34],
                        rlt64[:, None, :].broadcast_to((128, G, 1)),
                        -1.0, 1.0, op0=AL.mult, op1=AL.add)
        v.drain()
        return v.nop()
    tok_masks = p.step("vector", em_masks, inc="v")

    p1 = contextlib.ExitStack()
    XR = 2
    x_tile = sbuf("x_tile", [128, XR, D], F32, p1)
    xn_stage = sbuf("xn_stage", [128, XR, D], BF16, p1)
    stats = sbuf("stats", [128, XR, 24], F32, p1)
    mv = sbuf("mv", [128, XR, 2], F32, p1)
    veps = sbuf("veps", [128, XR, 1], F32, p1)
    rs1 = sbuf("rs1", [128, XR, 1], F32, p1)
    nt1 = sbuf("nt1", [128, XR, 1], F32, p1)
    pxn = [psum("pxn%d" % i, [128, DT, 128], BF16, p1) for i in range(2)]

    p2a = contextlib.ExitStack()
    U32 = mybir.dt.uint32
    wvs2 = sbuf("wvs2", [128, 2, DT, 512], BF16, p2a)
    vtile = sbuf("vtile", [128, 4, 512], BF16, p2a)
    keysw = sbuf("keysw", [128, 3, S], F32, p2a)
    wqTb = sbuf("wqTb", [128, DT, H], BF16, p2a)
    wkTb = sbuf("wkTb", [128, DT, H], BF16, p2a)
    idx = sbuf("idx", [128, 64], U16, p2a)
    idxcf = sbuf("idxcf", [128, 64], F32, p2a)
    kvn = sbuf("kvn", [128, 32], F32, p2a)
    identF = sbuf("identF", [128, 128], F32, p2a)
    RING = 3
    pqk = [psum("pqk%d" % i, [128, 512], F32, p2a) for i in range(RING)]
    pidx = psum("pidx", [64, 128], F32, p2a)

    vflatc = [vc.rearrange("s (hh cc) -> (s hh) cc", cc=16) for vc in v_scr]

    t_idf = p.step("vector",
                   lambda v: v.tensor_scalar(identF[:], iota_r[:], hcol[:],
                                             None, op0=AL.is_equal),
                   waits=[tok_iota], inc="v")

    gamma_ptr = [0]

    def pump_gamma(n, eng="scalar"):
        while gamma_ptr[0] < min(NGC, n):
            reg_gamma(gamma_ptr[0], eng)
            gamma_ptr[0] += 1

    xn_toks = [None] * NT
    xload_toks = [None] * NT
    xpose_toks = [None] * NT
    xev_toks = [None] * NT
    sqrt_toks = [None] * NT

    mm_evs = []
    NQK = 2 * NCH
    NVE = D // 512
    qk_ev = [None] * NQK
    vmm_toks = [[None] * NT for _ in range(NVE)]
    vsp_flat = []
    vsub_toks = [[None] * 4 for _ in range(NVE)]
    cqk = [None, None]

    def reg_vsub(ech, q, extra=()):
        def em(gp, ech=ech, q=q):
            return gp.dma_start(out=wvs2[:, ech % 2, 4 * q:4 * (q + 1), :],
                                in_=wvs_d[ech][:, 4 * q:4 * (q + 1), :])
        vsub_toks[ech][q] = p.step("gpsimd", em, waits=list(extra),
                                   inc="dG", inc_by=16)

    def reg_qk(wi, ch):
        u01 = wi * NCH + ch
        k = len(mm_evs)
        slot = k % RING

        def em_mm(t, wi=wi, ch=ch, slot=slot):
            wt = wqTb if wi == 0 else wkTb
            last = None
            for kt in range(DT):
                last = t.matmul(pqk[slot][:], wt[:, kt, :],
                                xnT[:, kt, 512 * ch:512 * (ch + 1)],
                                start=(kt == 0), stop=(kt == DT - 1))
            return last
        w = [cqk[wi], xev_toks[4 * ch + 3]]
        if k >= RING:
            w.append(mm_evs[k - RING])
        tok_mm = p.step("tensor", em_mm, waits=w, inc="t")

        if wi == 0:
            def em_ev(v, ch=ch, slot=slot):
                v.tensor_copy(qb[:, 512 * ch:512 * (ch + 1)], pqk[slot][:])
                v.drain()
                v.tensor_scalar(gq[:, 512 * ch:512 * (ch + 1)],
                                pqk[slot][:], 0.0, None, op0=AL.is_gt)
                v.drain()
                return v.nop()
        else:
            def em_ev(v, ch=ch, slot=slot):
                v.tensor_copy(keysw[:, 0, 512 * ch:512 * (ch + 1)],
                              pqk[slot][:])
                v.drain()
                return v.nop()
        qk_ev[u01] = p.step("vector", em_ev, waits=[tok_mm], inc="v")
        mm_evs.append(qk_ev[u01])

    def reg_vunit(ech, mt):
        k = len(mm_evs)
        slot = k % RING
        vu = len(vsp_flat)
        vslot = vu % 4

        def em_vmm(t, ech=ech, mt=mt, slot=slot):
            last = None
            for kt in range(DT):
                last = t.matmul(pqk[slot][:],
                                xnT[:, kt, 128 * mt:128 * (mt + 1)],
                                wvs2[:, ech % 2, kt, :],
                                start=(kt == 0), stop=(kt == DT - 1))
            return last
        w = list(vsub_toks[ech]) + [xev_toks[mt]]
        if k >= RING:
            w.append(mm_evs[k - RING])
        vmm_toks[ech][mt] = p.step("tensor", em_vmm, waits=w, inc="t")

        def em_vev(v, slot=slot, vslot=vslot):
            v.tensor_copy(vtile[:, vslot, :], pqk[slot][:])
            v.drain()
            return v.nop()
        w = [vmm_toks[ech][mt]]
        if vu >= 4:
            w.append(vsp_flat[vu - 4])
        tok_ev = p.step("vector", em_vev, waits=w, inc="v")
        mm_evs.append(tok_ev)

        def em_vsp(sc, ech=ech, mt=mt, vslot=vslot):
            return sc.dma_start(
                out=v_scr[ech][128 * mt:128 * (mt + 1), :],
                in_=vtile[:, vslot, :])
        vsp_flat.append(p.step("scalar", em_vsp, waits=[tok_ev],
                               inc="dX", inc_by=16))

    for st in range(NT):
        b = st % XR

        def em_xload(gp, st=st, b=b):
            return gp.dma_start(out=x_tile[:, b, :],
                                in_=x_tok[128 * st:128 * (st + 1), :])
        w = [xn_toks[st - XR]] if st >= XR else []
        xload_toks[st] = p.step("gpsimd", em_xload, waits=w, inc="dG",
                                inc_by=16)
        if st == 1:
            cqk[0] = p.step(
                "gpsimd",
                lambda gp: gp.dma_start(out=wqTb[:], in_=wqs_d[:]),
                inc="dG", inc_by=16)
            cqk[1] = p.step(
                "gpsimd",
                lambda gp: gp.dma_start(out=wkTb[:], in_=wks_d[:]),
                inc="dG", inc_by=16)
        elif st in (2, 3):
            reg_vsub(0, 2 * (st - 2))
            reg_vsub(0, 2 * (st - 2) + 1)
        elif 4 <= st <= 7:
            reg_vsub(1, st - 4)

        def em_stats(v, b=b):
            for c in range(4):
                v.bn_stats(stats[:, b, 6 * c:6 * c + 6],
                           x_tile[:, b, 512 * c:512 * (c + 1)])
            v.drain()
            v.bn_aggr(mv[:, b, :], stats[:, b, :])
            v.drain()
            v.tensor_scalar(veps[:, b, :], mv[:, b, 1:2], 1e-5, None,
                            op0=AL.add)
            v.drain()
            return v.nop()
        t_stats = p.step("vector", em_stats, waits=[xload_toks[st]], inc="v")

        def em_sqrt(sc, b=b):
            sc.activation(nt1[:, b, :], veps[:, b, :], AF.Ln)
            sc.drain()
            sc.activation(rs1[:, b, :], nt1[:, b, :], AF.Exp, scale=-0.5)
            sc.drain()
            return sc.nop()
        sqrt_toks[st] = p.step("scalar", em_sqrt, waits=[t_stats], inc="sc")

        def em_xn(v, b=b):
            v.tensor_scalar(xn_stage[:, b, :], x_tile[:, b, :],
                            mv[:, b, 0:1], rs1[:, b, :],
                            op0=AL.subtract, op1=AL.mult)
            v.drain()
            return v.nop()
        xn_toks[st] = p.step("vector", em_xn, waits=[sqrt_toks[st]], inc="v")

        def em_tp(t, st=st, b=b):
            last = None
            for dt in range(DT):
                last = t.transpose(pxn[st % 2][:, dt, :],
                                   xn_stage[:, b, 128 * dt:128 * (dt + 1)],
                                   ident[:])
            return last
        w = [xn_toks[st], tok_ident]
        if st >= 2:
            w.append(xev_toks[st - 2])
        xpose_toks[st] = p.step("tensor", em_tp, waits=w, inc="t")

        def em_ev(v, st=st):
            v.tensor_copy(xnT[:, :, 128 * st:128 * (st + 1)], pxn[st % 2][:])
            v.drain()
            return v.nop()
        xev_toks[st] = p.step("vector", em_ev, waits=[xpose_toks[st]],
                              inc="v")
        if st % 4 == 3:
            reg_qk(0, st // 4)
            reg_qk(1, st // 4)
        if st >= 4:
            reg_vunit(0, st - 4)
        pump_gamma(st + 1)

    for ech in (0, 1):
        for q in range(4):
            if vsub_toks[ech][q] is None:
                reg_vsub(ech, q)
    for mt in range(max(0, NT - 4), NT):
        if vmm_toks[0][mt] is None:
            reg_vunit(0, mt)

    seq_top = [0, 1, 2, 1]
    seq_bot = [1, 2, 1, 2]

    def mk_top(r):
        def em(v, r=r):
            src = keysw[:, seq_top[r], :]
            v.max(kvals[:, 8 * r:8 * r + 8], src)
            v.drain()
            v.max_index(idx[:, 8 * r:8 * r + 8], kvals[:, 8 * r:8 * r + 8],
                        src)
            if r < 3:
                v.match_replace(keysw[:, seq_top[r + 1], :],
                                kvals[:, 8 * r:8 * r + 8], src, NEG_BIG)
            v.drain()
            return v.nop()
        return em

    def em_negk(v):
        v.tensor_scalar(keysw[:, 1, :], keysw[:, 0, :], -1.0, None,
                        op0=AL.mult)
        v.drain()
        return v.nop()

    def mk_bot(r):
        def em(v, r=r):
            src = keysw[:, seq_bot[r], :]
            v.max(kvn[:, 8 * r:8 * r + 8], src)
            v.drain()
            v.max_index(idx[:, 32 + 8 * r:32 + 8 * r + 8],
                        kvn[:, 8 * r:8 * r + 8], src)
            if r < 3:
                v.match_replace(keysw[:, seq_bot[r + 1], :],
                                kvn[:, 8 * r:8 * r + 8], src, NEG_BIG)
            v.drain()
            return v.nop()
        return em

    def em_negv(v):
        v.tensor_scalar(kvals[:, 32:64], kvn[:], -1.0, None, op0=AL.mult)
        v.drain()
        return v.nop()

    sel_emits = [mk_top(r) for r in range(4)] + [em_negk] \
        + [mk_bot(r) for r in range(4)] + [em_negv]
    sel_ptr = [0]
    sel_tok = [None]

    def pump_sel(n=1):
        for _ in range(n):
            if sel_ptr[0] >= len(sel_emits):
                return
            sel_tok[0] = p.step("vector", sel_emits[sel_ptr[0]], inc="v")
            sel_ptr[0] += 1

    GTAIL = 48
    for mt in range(NT):
        reg_vunit(1, mt)
        pump_sel(1)
        pump_gamma(min(GTAIL, gamma_ptr[0] + 1))
    pump_sel(len(sel_emits))
    tok_sel = sel_tok[0]

    def em_idxc(v):
        v.tensor_copy(idxcf[:], idx[:])
        v.drain()
        v.tensor_scalar(idxcf[:], idxcf[:], 32.0, h31col[:],
                        op0=AL.mult, op1=AL.add)
        v.drain()
        return v.nop()
    t_ic = p.step("vector", em_idxc, waits=[tok_sel], inc="v")

    def em_kv2(v):
        v.tensor_tensor(KV2.ap().rearrange("p (two j) -> p two j", j=64),
                        kvals[:, None, :].broadcast_to((128, 2, 64)),
                        Pm.ap().rearrange("p (two j) -> p two j", j=64),
                        op=AL.mult)
        v.drain()
        return v.nop()
    tok_kv2 = p.step("vector", em_kv2, waits=[tok_sel, tok_masks], inc="v")

    for q2 in range(4):
        reg_vsub(2, q2, extra=[vmm_toks[0][NT - 1]])
    for mt in range(NT):
        reg_vunit(2, mt)
        pump_gamma(min(GTAIL, gamma_ptr[0] + 1))

    t_it = p.step("tensor",
                  lambda t: t.transpose(pidx[:], idxcf[:], identF[:]),
                  waits=[t_ic, t_idf], inc="t")

    def em_ib(v):
        v.tensor_copy(idxb0[:], pidx[:, 0::2])
        v.drain()
        v.tensor_copy(idxb1[:], pidx[:, 1::2])
        v.drain()
        return v.nop()
    t_ib = p.step("vector", em_ib, waits=[t_it], inc="v")

    for q2 in range(4):
        reg_vsub(3, q2, extra=[vmm_toks[1][NT - 1]])
    for mt in range(NT):
        reg_vunit(3, mt)
        pump_gamma(min(GTAIL, gamma_ptr[0] + 1))

    GPC = G // NVE
    gv_toks = [None] * (2 * G)
    for g in range(G):
        c = g // GPC
        spw = vsp_flat[c * NT + max(0, NT - p.NSUB):(c + 1) * NT]

        def em_g0(gp, g=g, c=c):
            return gp.indirect_dma_start(
                out=Vsel[0:64, g, 0:16], out_offset=None, in_=vflatc[c],
                in_offset=bass.IndirectOffsetOnAxis(ap=idxb0[:, g:g + 1],
                                                    axis=0))
        gv_toks[2 * g] = p.step("gpsimd", em_g0,
                                waits=[t_ib, tok_masks] + spw,
                                inc="dG", inc_by=16)

        def em_g1(gp, g=g, c=c):
            return gp.indirect_dma_start(
                out=Vsel[64:128, g, 16:32], out_offset=None, in_=vflatc[c],
                in_offset=bass.IndirectOffsetOnAxis(ap=idxb1[:, g:g + 1],
                                                    axis=0))
        gv_toks[2 * g + 1] = p.step("gpsimd", em_g1, waits=[gv_toks[2 * g]],
                                    inc="dG", inc_by=16)
    tok_vsel = gv_toks[2 * G - 1]

    pump_gamma(GTAIL)

    p2a_end = [tok_kv2, tok_sel, t_ic, t_it, t_ib,
               vmm_toks[NVE - 1][NT - 1], mm_evs[-1],
               qk_ev[NQK - 1], xn_toks[NT - 1], xev_toks[NT - 1],
               xpose_toks[NT - 1], sqrt_toks[NT - 1]] + p.tails("dX")

    p2a.close()
    p1.close()
    p2b = contextlib.ExitStack()
    Esl = sbuf("Esl", [128, 2, GB, S], BF16, p2b)
    Lblk = sbuf("Lblk", [128, 2, GB, 128], BF16, p2b)
    Cblk = sbuf("Cblk", [128, 2, GB, 128], BF16, p2b)
    a_sE = sbuf("a_sE", [128, NT, 32 * GB], BF16, p2b)
    aT_slab = sbuf("aT_slab", [128, 2, NT, 128], BF16, p2b)
    rden = sbuf("rden", [128, GB, 2], F32, p2b)
    SCW = min(1024, S)
    NHF = S // SCW
    psc = [psum("psc%d" % i, [128, SCW], F32, p2b) for i in range(2)]
    pav = [psum("pav%d" % i, [128, GB, 34], F32, p2b) for i in range(2)]
    pat = [psum("pat%d" % i, [128, NT // 2, 128], BF16, p2b)
           for i in range(2)]

    lb_toks = [None] * NBLK
    sc_toks = [[None] * (NHF * GB) for _ in range(NBLK)]
    ex_toks = [[None] * (NHF * GB) for _ in range(NBLK)]
    av_toks = [[None] * NT for _ in range(NBLK)]
    dv_toks = [[None] * NT for _ in range(NBLK)]
    ta_toks = [[None] * 4 for _ in range(NBLK)]
    ae_toks = [[None] * 4 for _ in range(NBLK)]
    sp_toks = [None] * NBLK

    at_view = at_scr.rearrange("(bb eb p) (st j) -> bb p eb st j",
                               bb=NBLK, p=128, j=128)

    def reg_lb(bkt):
        def em_lb(v, b=bkt):
            gsl = slice(GB * b, GB * (b + 1))
            v.tensor_tensor(Lblk[:, b % 2, :, :],
                            KV2[:, None, :].broadcast_to((128, GB, 128)),
                            Gm[:, gsl, None].broadcast_to((128, GB, 128)),
                            op=AL.mult)
            v.drain()
            v.tensor_tensor(Cblk[:, b % 2, :, :],
                            CP2[:, None, :].broadcast_to((128, GB, 128)),
                            Gm[:, gsl, None].broadcast_to((128, GB, 128)),
                            op=AL.mult)
            v.drain()
            return v.nop()
        w = [tok_kv2] + (p2a_end if bkt == 0 else [])
        if bkt >= 2:
            w.append(sc_toks[bkt - 2][NHF * GB - 1])
        lb_toks[bkt] = p.step("vector", em_lb, waits=w, inc="v")

    reg_lb(0)
    for bkt in range(NBLK):
        if bkt + 1 < NBLK:
            reg_lb(bkt + 1)
        for gi in range(GB):
            for hf in range(NHF):
                u = (bkt * GB + gi) * NHF + hf
                uu = u % 2

                def em_sc_mm(t, b=bkt, gi=gi, hf=hf, uu=uu):
                    last = None
                    for cc in range(SCW // 512):
                        n0 = SCW * hf + 512 * cc
                        t.matmul(psc[uu][:, 512 * cc:512 * (cc + 1)],
                                 Lblk[:, b % 2, gi, :], qb[:, n0:n0 + 512],
                                 start=True, stop=False)
                        last = t.matmul(psc[uu][:, 512 * cc:512 * (cc + 1)],
                                        Cblk[:, b % 2, gi, :],
                                        gq[:, n0:n0 + 512],
                                        start=False, stop=True)
                    return last
                w = [lb_toks[bkt]]
                if u >= 2:
                    w.append(ex_toks[(u - 2) // (NHF * GB)]
                             [(u - 2) % (NHF * GB)])
                if gi == 0 and hf == 0 and bkt >= 2:
                    w.append(av_toks[bkt - 2][NT - 1])
                sc_toks[bkt][NHF * gi + hf] = p.step("tensor", em_sc_mm,
                                                     waits=w, inc="t")

                def em_exp(sc, b=bkt, gi=gi, hf=hf, uu=uu):
                    return sc.activation(
                        Esl[:, b % 2, gi, SCW * hf:SCW * (hf + 1)],
                        psc[uu][:], AF.Exp, bias=biascol[:], scale=1.0)
                ex_toks[bkt][NHF * gi + hf] = p.step(
                    "scalar", em_exp, waits=[sc_toks[bkt][NHF * gi + hf]],
                    inc="sEx")
            pump_gamma(gamma_ptr[0] + 1, "vector")

        for st in range(NT):
            def em_av(t, b=bkt, st=st):
                last = None
                for gi in range(GB):
                    last = t.matmul(pav[st % 2][:, gi, :],
                                    Esl[:, b % 2, gi,
                                        128 * st:128 * (st + 1)],
                                    Vsel[:, b * GB + gi, :],
                                    start=True, stop=True)
                return last
            hi = 2 * (bkt + 1) * GB
            w = [ex_toks[bkt][NHF * GB - 1]] + gv_toks[max(0, hi - 6):hi]
            u = bkt * NT + st
            if u >= 2:
                pb, pst = divmod(u - 2, NT)
                w.append(dv_toks[pb][pst])
            av_toks[bkt][st] = p.step("tensor", em_av, waits=w, inc="t")

            def em_dv(v, st=st):
                v.reciprocal(rden[:], pav[st % 2][:, :, 32:34])
                v.drain()
                v.tensor_tensor(
                    a_sE[:, st, :].rearrange("p (g par c) -> p g par c",
                                             g=GB, c=16),
                    pav[st % 2][:, :, 0:32].rearrange(
                        "p g (par c) -> p g par c", c=16),
                    rden[:, :, :, None].broadcast_to((128, GB, 2, 16)),
                    op=AL.mult)
                v.drain()
                return v.nop()
            dv_toks[bkt][st] = p.step("vector", em_dv,
                                      waits=[av_toks[bkt][st]], inc="v")
        pump_gamma(gamma_ptr[0] + 1, "vector")

        HB = NT // 2
        for hb in range(2):
            for eb in range(2):
                u2 = hb * 2 + eb

                def em_ta(t, hb=hb, eb=eb, u2=u2, b=bkt):
                    last = None
                    for si in range(HB):
                        st = hb * HB + si
                        last = t.transpose(
                            pat[u2 % 2][:, si, :],
                            a_sE[:, st, 128 * eb:128 * (eb + 1)],
                            ident[:])
                    return last
                w = [dv_toks[bkt][hb * HB + HB - 1]]
                gu = bkt * 4 + u2
                if gu >= 2:
                    pb, pu2 = divmod(gu - 2, 4)
                    w.append(ae_toks[pb][pu2])
                ta_toks[bkt][u2] = p.step("tensor", em_ta, waits=w, inc="t")

                def em_ae(v, hb=hb, eb=eb, u2=u2):
                    v.tensor_copy(
                        aT_slab[:, eb, hb * HB:(hb + 1) * HB, :],
                        pat[u2 % 2][:])
                    v.drain()
                    return v.nop()
                w = [ta_toks[bkt][u2]]
                if bkt >= 1 and u2 == 0:
                    w.append(sp_toks[bkt - 1])
                ae_toks[bkt][u2] = p.step("vector", em_ae, waits=w, inc="v")

        def em_spill(sc, b=bkt):
            return sc.dma_start(out=at_view[b], in_=aT_slab[:])
        sp_toks[bkt] = p.step("scalar", em_spill,
                              waits=[ae_toks[bkt][3]],
                              inc="dX", inc_by=16)
        pump_gamma(gamma_ptr[0] + 2, "scalar")

    pump_gamma(NGC, "vector")
    p2b_end = [sp_toks[NBLK - 1], gam_toks[NGC - 1],
               av_toks[NBLK - 1][NT - 1], ta_toks[NBLK - 1][3],
               ae_toks[NBLK - 1][3],
               ex_toks[NBLK - 1][NHF * GB - 1]] + p.tails("dX", "dY", "dG")

    p2b.close()
    gsc.close()
    p3 = contextlib.ExitStack()
    wraw = sbuf("wraw", [128, 3, DT, 128], F32, p3)
    sa1 = sbuf("sa1", [128, DT, 128], BF16, p3)
    sb1 = sbuf("sb1", [128, DT, 128], BF16, p3)
    wq1 = sbuf("wq1", [128, 2, DT, 128], BF16, p3)
    wq2 = sbuf("wq2", [128, 2, DT, 128], BF16, p3)
    wq3 = sbuf("wq3", [128, 2, SFT, 128], BF16, p3)
    sigb = sbuf("sigb", [128, 2, S], BF16, p3)
    hid = sbuf("hid", [128, SFT, S], BF16, p3)
    o3st = sbuf("o3st", [128, 2, S], BF16, p3)
    pth_sc = contextlib.ExitStack()
    pth = psum("pth", [128, 4], F32, pth_sc)

    def em_gred(v):
        v.tensor_reduce(gam_acc[:, 0:1], gam_cols[:, 0:r1], axis=AX.X,
                        op=AL.add)
        v.drain()
        v.tensor_reduce(gam_acc[:, 1:2], gam_cols[:, r1:r2], axis=AX.X,
                        op=AL.add)
        v.drain()
        v.tensor_reduce(gam_acc[:, 2:3], gam_cols[:, r2:r2 + (NGC - r2)],
                        axis=AX.X, op=AL.add)
        v.drain()
        return v.nop()
    t_gr = p.step("vector", em_gred, waits=[gam_toks[NGC - 1]], inc="v")

    def em_gpr(gp):
        gp.tensor_reduce(gam_sc[:], gam_acc[:], axis=AX.C, op=AL.add)
        gp.drain()
        return gp.nop()
    t_gp = p.step("gpsimd", em_gpr, waits=[t_gr], inc="gp")

    def em_thr(v):
        v.tensor_scalar(thr_sc[:], gam_sc[:], 0.5 / n_w_elems, 0.5e-8,
                        op0=AL.mult, op1=AL.add)
        v.drain()
        return v.nop()
    t_th = p.step("vector", em_thr, waits=[t_gp], inc="v")

    t_tm = p.step("tensor",
                  lambda t: t.matmul(pth[:, 0:3], ones_row[:], thr_sc[:],
                                     start=True, stop=True),
                  waits=[t_th] + p2b_end, inc="t")

    def em_thev(v):
        v.tensor_copy(thr[:], pth[:, 0:3])
        v.drain()
        v.tensor_scalar(thrn[:], pth[:, 0:3], -1.0, None, op0=AL.mult)
        v.drain()
        return v.nop()
    t_te = p.step("vector", em_thev, waits=[t_tm], inc="v")
    pth_sc.close()
    pf1 = psum("pf1", [128, S], F32, p3)
    pf2 = psum("pf2", [128, S], F32, p3)


    slab_seq = []
    for sti in range(NST):
        for fi in range(SFT):
            f = sti * SFT + fi
            slab_seq.append((0, f))
            slab_seq.append((1, f))
        for d in range(DT):
            slab_seq.append((2, (sti, d)))
    NSLAB = len(slab_seq)
    sl_load = [None] * NSLAB
    sl_q = [None] * NSLAB
    sl_a = [None] * NSLAB
    slab_of = {}
    for i, ka in enumerate(slab_seq):
        slab_of[ka] = i

    f1_toks = [None] * FT
    f2_toks = [None] * FT
    sg_toks = [None] * FT
    hd_toks = [None] * FT
    f3_toks = [None] * (NST * DT)
    o3_toks = [None] * (NST * DT)
    od_toks = [None] * (NST * DT)

    def reg_slab(i):
        kind, arg = slab_seq[i]
        if kind == 0:
            src = w1s_d[arg]
            nk, bias_i = DT, 0
        elif kind == 1:
            src = w2s_d[arg]
            nk, bias_i = DT, 1
        else:
            sti, d = arg
            src = w3s_d[sti, d]
            nk, bias_i = SFT, 2

        def em_l(sy, src=src, i=i, nk=nk):
            return sy.dma_start(out=wraw[:, i % 3, :nk, :], in_=src)
        w = list(p2b_end) if i < 3 else [sl_q[i - 3]]
        sl_load[i] = p.step("sync", em_l, waits=w, inc="dY", inc_by=16)

        def em_q(sc, i=i, nk=nk, bias_i=bias_i):
            sc.activation(sa1[:, :nk, :], wraw[:, i % 3, :nk, :], AF.Sign,
                          bias=thr[:, bias_i:bias_i + 1], scale=1.0)
            sc.drain()
            sc.activation(sb1[:, :nk, :], wraw[:, i % 3, :nk, :], AF.Sign,
                          bias=thrn[:, bias_i:bias_i + 1], scale=1.0)
            sc.drain()
            return sc.nop()
        w = [sl_load[i], t_te]
        if i >= 1:
            w.append(sl_a[i - 1])
        sl_q[i] = p.step("scalar", em_q, waits=w, inc="sc")

        if kind == 0:
            dst, ridx, nk2 = wq1, arg % 2, DT
        elif kind == 1:
            dst, ridx, nk2 = wq2, arg % 2, DT
        else:
            dst, ridx, nk2 = wq3, arg[1] % 2, SFT

        def em_a(v, dst=dst, ridx=ridx, nk2=nk2):
            v.tensor_tensor(dst[:, ridx, :nk2, :], sa1[:, :nk2, :],
                            sb1[:, :nk2, :], op=AL.add)
            v.drain()
            return v.nop()
        w = [sl_q[i]]
        if kind == 0 and arg >= 2:
            w.append(f1_toks[arg - 2])
        if kind == 1 and arg >= 2:
            w.append(f2_toks[arg - 2])
        if kind == 2:
            u3 = arg[0] * DT + arg[1]
            if u3 >= 2:
                w.append(f3_toks[u3 - 2])
        sl_a[i] = p.step("vector", em_a, waits=w, inc="v")

    def reg_slab_upto(i):
        while reg_ptr[0] <= min(i, NSLAB - 1):
            reg_slab(reg_ptr[0])
            reg_ptr[0] += 1

    reg_ptr = [0]
    atf_sc = contextlib.ExitStack()
    at_toks = []
    reg_slab_upto(1)
    for sti in range(NST):
        for fi in range(SFT):
            f = sti * SFT + fi
            if fi + 1 < SFT:
                reg_slab_upto(slab_of[(1, f + 1)])
            else:
                reg_slab_upto(slab_of[(2, (sti, 0))])

            def em_f1(t, f=f):
                last = None
                for kt in range(DT):
                    for ch in range(NCH):
                        last = t.matmul(pf1[:, 512 * ch:512 * (ch + 1)],
                                        wq1[:, f % 2, kt, :],
                                        xnT[:, kt, 512 * ch:512 * (ch + 1)],
                                        start=(kt == 0), stop=(kt == DT - 1))
                return last
            w = [sl_a[slab_of[(0, f)]]]
            if f >= 1:
                w.append(sg_toks[f - 1])
            if fi == 0 and sti >= 1:
                w.append(o3_toks[sti * DT - 1])
                w.append(o3_toks[sti * DT - 2])
            f1_toks[f] = p.step("tensor", em_f1, waits=w, inc="t")

            def em_sg(sc, f=f):
                return sc.activation(sigb[:, f % 2, :], pf1[:], AF.Sigmoid,
                                     scale=0.5)
            w = [f1_toks[f]]
            if f >= 2:
                w.append(hd_toks[f - 2])
            sg_toks[f] = p.step("scalar", em_sg, waits=w, inc="sc")

            def em_f2(t, f=f):
                last = None
                for kt in range(DT):
                    for ch in range(NCH):
                        last = t.matmul(pf2[:, 512 * ch:512 * (ch + 1)],
                                        wq2[:, f % 2, kt, :],
                                        xnT[:, kt, 512 * ch:512 * (ch + 1)],
                                        start=(kt == 0), stop=(kt == DT - 1))
                return last
            w = [sl_a[slab_of[(1, f)]]]
            if f >= 1:
                w.append(hd_toks[f - 1])
            if fi == 0 and sti >= 1:
                w.append(o3_toks[sti * DT - 1])
            f2_toks[f] = p.step("tensor", em_f2, waits=w, inc="t")

            def em_hd(v, f=f, fi=fi):
                v.scalar_tensor_tensor(hid[:, fi, :], pf2[:], 0.5,
                                       sigb[:, f % 2, :],
                                       op0=AL.mult, op1=AL.mult)
                v.drain()
                return v.nop()
            hd_toks[f] = p.step("vector", em_hd,
                                waits=[f2_toks[f], sg_toks[f]], inc="v")

        if sti == NST - 1:
            xnT_sc.close()
            attnF = sbuf("attnF", [128, DT, S], BF16, atf_sc, side="right")
            atv = at_scr.rearrange("(kt p) s -> p kt s", p=128)
            for kt in range(DT):
                def em_atl(sc, kt=kt):
                    return sc.dma_start(out=attnF[:, kt, :],
                                        in_=atv[:, kt, :])
                w = [f1_toks[FT - 1], f2_toks[FT - 1],
                     sp_toks[NBLK - 1]] if kt == 0 else []
                at_toks.append(p.step("scalar", em_atl, waits=w,
                                      inc="dX", inc_by=16))

        for d in range(DT):
            u = sti * DT + d
            if d + 1 < DT:
                reg_slab_upto(slab_of[(2, (sti, d + 1))])
            elif sti + 1 < NST:
                reg_slab_upto(slab_of[(1, (sti + 1) * SFT)])

            def em_f3(t, sti=sti, d=d):
                pp = pf1 if d % 2 == 0 else pf2
                last = None
                for ki in range(SFT):
                    for ch in range(NCH):
                        last = t.matmul(pp[:, 512 * ch:512 * (ch + 1)],
                                        wq3[:, d % 2, ki, :],
                                        hid[:, ki, 512 * ch:512 * (ch + 1)],
                                        start=(ki == 0),
                                        stop=(ki == SFT - 1))
                return last
            w = [sl_a[slab_of[(2, (sti, d))]], hd_toks[sti * SFT + SFT - 1]]
            if d <= 1:
                w.append(sg_toks[sti * SFT + SFT - 1])
            if u >= 2:
                w.append(o3_toks[u - 2])
            f3_toks[u] = p.step("tensor", em_f3, waits=w, inc="t")

            def em_o3(v, d=d, u=u):
                v.tensor_copy(o3st[:, u % 2, :],
                              (pf1 if d % 2 == 0 else pf2)[:])
                v.drain()
                return v.nop()
            w = [f3_toks[u]]
            if u >= 2:
                w.append(od_toks[u - 2])
            o3_toks[u] = p.step("vector", em_o3, waits=w, inc="v")

            def em_od(gp, sti=sti, d=d, u=u):
                kw = {} if sti == 0 else {"accum_op": AL.add}
                return gp.dma_start(out=o3_scr[128 * d:128 * (d + 1), :],
                                    in_=o3st[:, u % 2, :], **kw)
            od_toks[u] = p.step("gpsimd", em_od, waits=[o3_toks[u]],
                                inc="dG", inc_by=16)

    p3_end = [od_toks[NST * DT - 1], f3_toks[NST * DT - 1],
              o3_toks[NST * DT - 1], sl_q[NSLAB - 1],
              sl_a[NSLAB - 1], f2_toks[FT - 1]] + p.tails("dX", "dY", "dG")

    p3.close()
    p4 = contextlib.ExitStack()
    woraw = sbuf("woraw", [128, 3, DT, 128], F32, p4)
    wob = sbuf("wob", [128, 3, DT, 128], BF16, p4)
    xTt = sbuf("xTt", [128, 4, S], F32, p4)
    o3ld = sbuf("o3ld", [128, 4, S], BF16, p4)
    outA = sbuf("outA", [128, 2, S], F32, p4)
    outB = sbuf("outB", [128, 2, S], F32, p4)
    pwo = [psum("pwo%d" % i, [128, S], F32, p4) for i in range(2)]

    o3v = o3_scr.rearrange("(mt p) s -> p mt s", p=128)
    xTv = xT_d.rearrange("(mt p) s -> p mt s", p=128)

    wol_toks = [None] * DT
    woc_toks = [None] * DT
    wo_toks = [None] * DT
    xl_toks = [None] * DT
    ol_toks = [None] * DT
    fA_toks = [None] * DT
    fB_toks = [None] * DT
    out_toks = [None] * DT

    for mm in range(DT + 2):
        if mm < DT:
            mt = mm

            def em_wol(gp, mt=mt):
                return gp.dma_start(out=woraw[:, mt % 3, :, :],
                                    in_=wos_d[mt])
            w = list(p3_end) if mt == 0 else []
            if mt >= 3:
                w.append(woc_toks[mt - 3])
            wol_toks[mt] = p.step("gpsimd", em_wol, waits=w, inc="dG",
                                  inc_by=16)

            def em_woc(v, mt=mt):
                v.tensor_copy(wob[:, mt % 3, :, :], woraw[:, mt % 3, :, :])
                v.drain()
                return v.nop()
            w = [wol_toks[mt]] + (list(p3_end) if mt == 0 else [])
            if mt >= 3:
                w.append(wo_toks[mt - 3])
            woc_toks[mt] = p.step("vector", em_woc, waits=w, inc="v")

            def em_xl(sy, mt=mt):
                return sy.dma_start(out=xTt[:, mt % 4, :], in_=xTv[:, mt, :])
            w = [fA_toks[mt - 4]] if mt >= 4 else list(p3_end)
            xl_toks[mt] = p.step("sync", em_xl, waits=w, inc="dY", inc_by=16)

            def em_ol(sy, mt=mt):
                return sy.dma_start(out=o3ld[:, mt % 4, :], in_=o3v[:, mt, :])
            w = [od_toks[(NST - 1) * DT + mt]]
            if mt >= 4:
                w.append(fB_toks[mt - 4])
            else:
                w += list(p3_end)
            ol_toks[mt] = p.step("sync", em_ol, waits=w, inc="dY", inc_by=16)

        if mm >= 2:
            m2 = mm - 2

            def em_fA(v, mt=m2):
                v.scalar_tensor_tensor(outA[:, mt % 2, :], pwo[mt % 2][:],
                                       1.0, xTt[:, mt % 4, :],
                                       op0=AL.mult, op1=AL.add)
                v.drain()
                return v.nop()
            w = [wo_toks[m2], xl_toks[m2]]
            if m2 >= 2:
                w.append(fB_toks[m2 - 2])
            fA_toks[m2] = p.step("vector", em_fA, waits=w, inc="v")

            def em_fB(v, mt=m2):
                v.scalar_tensor_tensor(outB[:, mt % 2, :], o3ld[:, mt % 4, :],
                                       0.5, outA[:, mt % 2, :],
                                       op0=AL.mult, op1=AL.add)
                v.drain()
                return v.nop()
            w = [fA_toks[m2], ol_toks[m2]]
            if m2 >= 2:
                w.append(out_toks[m2 - 2])
            fB_toks[m2] = p.step("vector", em_fB, waits=w, inc="v")

            def em_out(sc, mt=m2):
                return sc.dma_start(out=out_d[128 * mt:128 * (mt + 1), :],
                                    in_=outB[:, mt % 2, :])
            out_toks[m2] = p.step("scalar", em_out, waits=[fB_toks[m2]],
                                  inc="dX", inc_by=16)

        if mm < DT:
            mt = mm

            def em_wo(t, mt=mt):
                last = None
                for kt in range(DT):
                    for ch in range(NCH):
                        last = t.matmul(
                            pwo[mt % 2][:, 512 * ch:512 * (ch + 1)],
                            wob[:, mt % 3, kt, :],
                            attnF[:, kt, 512 * ch:512 * (ch + 1)],
                            start=(kt == 0), stop=(kt == DT - 1))
                return last
            w = [woc_toks[mt]] + at_toks
            if mt >= 2:
                w.append(fA_toks[mt - 2])
            wo_toks[mt] = p.step("tensor", em_wo, waits=w, inc="t")

    p.step("scalar", lambda sc: None, waits=[out_toks[DT - 1]])
    p.step("sync", lambda sy: None, waits=[sl_load[NSLAB - 1]])
    p.step("gpsimd", lambda gp: None, waits=[od_toks[NST * DT - 1]])

    with nc.Block() as blk:
        p.emit(blk)
    p4.close()

    return nc


_CACHED = {}


def _get_nc():
    if "nc" not in _CACHED:
        _CACHED["nc"] = build_program()
    return _CACHED["nc"]


def kernel(L0_attn_x, L0_attn_Wq, L0_attn_Wk, L0_attn_Wv, L0_attn_Wo,
           L0_ffn_w1, L0_ffn_w2, L0_ffn_w3):
    B = L0_attn_x.shape[0]
    nc = _get_nc()

    def C(a):
        return np.ascontiguousarray(np.asarray(a), dtype=np.float32)

    D = L0_attn_x.shape[2]
    S = L0_attn_x.shape[1]
    H = L0_attn_Wq.shape[0]
    F = L0_ffn_w1.shape[0]
    DT, FT, NVE = D // 128, F // 128, D // 512
    NST = 4 if FT % 4 == 0 else (2 if FT % 2 == 0 else 1)
    SFT = FT // NST
    wqs = C(np.asarray(L0_attn_Wq).T.reshape(DT, 128, H).transpose(1, 0, 2))
    wks = C(np.asarray(L0_attn_Wk).T.reshape(DT, 128, H).transpose(1, 0, 2))
    wvs = C(np.asarray(L0_attn_Wv).T.reshape(DT, 128, NVE, 512)
            .transpose(2, 1, 0, 3))
    wos = C(np.asarray(L0_attn_Wo).T.reshape(DT, 128, DT, 128)
            .transpose(2, 1, 0, 3))
    w1s = C(np.asarray(L0_ffn_w1).T.reshape(DT, 128, FT, 128)
            .transpose(2, 1, 0, 3))
    w2s = C(np.asarray(L0_ffn_w2).T.reshape(DT, 128, FT, 128)
            .transpose(2, 1, 0, 3))
    w3s = C(np.asarray(L0_ffn_w3).T.reshape(NST, SFT, 128, DT, 128)
            .transpose(0, 3, 2, 1, 4))

    in_maps = []
    for b in range(B):
        xb = C(L0_attn_x[b])
        in_maps.append({
            "x_tok": xb, "xT": C(xb.T),
            "wqs": wqs, "wks": wks, "wvs": wvs, "wos": wos,
            "w1s": w1s, "w2s": w2s, "w3s": w3s,
        })

    res = run_bass_kernel_spmd(nc, in_maps, core_ids=list(range(B)))
    out = np.empty((B, L0_attn_x.shape[1], L0_attn_x.shape[2]),
                   dtype=np.float32)
    for b in range(B):
        out[b] = np.asarray(res.results[b]["out"]).T
    return out
